# version 01
# baseline (speedup 1.0000x reference)
from contextlib import ExitStack
import numpy as np

import concourse.bass as bass
import concourse.mybir as mybir
from concourse.bass_utils import run_bass_kernel_spmd

F32 = mybir.dt.float32
BF16 = mybir.dt.bfloat16
U8 = mybir.dt.uint8
AF = mybir.ActivationFunctionType
ALU = mybir.AluOpType

B, L, D = 4, 4096, 1024
ED, NST, KCV, RDT = 1024, 16, 4, 64
HFF = 4096
NEXP, HEXP = 8, 2048
TC = 512
NTC = L // TC
OWN = L // 2
PC = 512
NPC = OWN // PC
CH = 256
EPS = 1e-5
KB = 1024

ENGS = ("sync", "tensor", "vector", "scalar", "gpsimd")

WPK_LAYOUT = [
    ("w_in", (D, 2 * ED)), ("w_xp", (ED, RDT + 2 * NST)), ("w_dt", (RDT, ED)),
    ("w_op", (ED, D)), ("w_f1", (D, HFF)), ("w_f2", (HFF, D)),
    ("w_g", (D, NEXP)), ("w_e1", (NEXP, D, HEXP)), ("w_e2", (NEXP, HEXP, D)),
]
WPK_TOT = sum(int(np.prod(s)) for _, s in WPK_LAYOUT)
PPK_COLS = 544 + D


class Sem:
    def __init__(self, h):
        self.h = h
        self.n = 0


class Ctl:
    def __init__(self):
        self.q = {e: [] for e in ENGS}

    def op(self, eng, fn, sem=None, k=1):
        if sem is None:
            self.q[eng].append(fn)
        else:
            sem.n += k
            self.q[eng].append(lambda E, fn=fn, s=sem.h, k=k: fn(E).then_inc(s, k))

    def dma(self, eng, out, in_, sem):
        sem.n += 16
        self.q[eng].append(
            lambda E, o=out, i=in_, s=sem.h: E.dma_start(out=o, in_=i).then_inc(s, 16))

    def dma_dyn(self, eng, mk, sem, tag=""):
        sem.n += 16

        def run(E, mk=mk, s=sem.h, tag=tag):
            try:
                o, i = mk(E)
                E.dma_start(out=o, in_=i).then_inc(s, 16)
            except Exception:
                import sys
                print("dma_dyn failed:", tag, file=sys.stderr)
                raise
        self.q[eng].append(run)

    def wait(self, eng, sem, val=None):
        v = sem.n if val is None else val
        if v > 0:
            self.q[eng].append(lambda E, s=sem.h, v=v: E.wait_ge(s, v))


def build(do_post=True, ntc=NTC, dump=None, post_until="full", races=True):
    nc = bass.Bass("TRN2", target_bir_lowering=False, debug=False,
                   detect_race_conditions=races)

    def din(name, shape, dt=F32):
        return nc.dram_tensor(name, shape, dt, kind="ExternalInput").ap()

    x_in = din("x", [L, D])
    x_own = din("x_own", [OWN, D])
    wpk = din("wpk", [WPK_TOT])
    ppk = din("ppk", [128, PPK_COLS])

    wofs = {}
    _o = 0
    for nm, shp in WPK_LAYOUT:
        n = int(np.prod(shp))
        wofs[nm] = (_o, shp)
        _o += n

    def wview(nm):
        o, shp = wofs[nm]
        if len(shp) == 2:
            return wpk[o:o + shp[0] * shp[1]].rearrange("(a b) -> a b", b=shp[1])
        return wpk[o:o + shp[0] * shp[1] * shp[2]].rearrange(
            "(a b c) -> a b c", b=shp[1], c=shp[2])

    w_in = wview("w_in")
    w_xp = wview("w_xp")
    w_dt = wview("w_dt")
    w_op = wview("w_op")
    w_f1 = wview("w_f1")
    w_f2 = wview("w_f2")
    w_g = wview("w_g")
    w_e1 = wview("w_e1")
    w_e2 = wview("w_e2")
    p_vec = ppk[:, 0:96]
    p_alog = ppk[:, 96:224]
    p_cw = ppk[:, 224:256]
    p_fb1 = ppk[:, 256:288]
    p_eb1 = ppk[:, 288:416]
    ident = ppk[:, 416:544]
    p_b2r = ppk[0:NEXP, 544:544 + D]

    o_out = nc.dram_tensor("o_out", [OWN, D], F32, kind="ExternalOutput").ap()
    o_dbg = None
    if dump is not None:
        o_dbg = nc.dram_tensor("o_dbg", list(dump[1]), F32, kind="ExternalOutput").ap()

    def scr(name, shape, dt):
        return nc.dram_tensor(name, shape, dt).ap()

    BF_LAYOUT = [
        ("d_wbf_in", (D, 2 * ED)), ("d_wbf_op", (ED, D)), ("d_wbf_f1", (D, HFF)),
        ("d_wbf_f2", (HFF, D)), ("d_wbf_e1", (NEXP * D, HEXP)),
        ("d_wbf_e2", (NEXP * HEXP, D)), ("d_bc", (2 * NST, L)),
        ("d_ymix", (ED, L)), ("d_g", (NEXP, OWN)),
    ]
    bf_tot = sum(s[0] * s[1] for _, s in BF_LAYOUT)
    d_bf = scr("d_bf", [bf_tot], BF16)
    bofs = {}
    _b = 0
    for nm, shp in BF_LAYOUT:
        bofs[nm] = (_b, shp)
        _b += shp[0] * shp[1]

    def bview(nm):
        o, shp = bofs[nm]
        return d_bf[o:o + shp[0] * shp[1]].rearrange("(a b) -> a b", b=shp[1])

    d_wbf_in = bview("d_wbf_in")
    d_wbf_op = bview("d_wbf_op")
    d_wbf_f1 = bview("d_wbf_f1")
    d_wbf_f2 = bview("d_wbf_f2")
    d_wbf_e1 = bview("d_wbf_e1")
    d_wbf_e2 = bview("d_wbf_e2")
    d_bc = bview("d_bc")
    d_ymix = bview("d_ymix")
    d_g = bview("d_g")

    ctl = Ctl()

    with ExitStack() as ctx:
        E = ctx.enter_context

        s_ld = Sem(E(nc.semaphore("s_ld")))
        s_st = Sem(E(nc.semaphore("s_st")))
        s_pe = Sem(E(nc.semaphore("s_pe")))
        s_ve = Sem(E(nc.semaphore("s_ve")))
        s_ac = Sem(E(nc.semaphore("s_ac")))
        s_gp = Sem(E(nc.semaphore("s_gp")))
        SEMS = {"tensor": s_pe, "vector": s_ve, "scalar": s_ac, "gpsimd": s_gp}

        def barrier():
            for e, sem in SEMS.items():
                ctl.op(e, lambda E: E.drain(), sem)
            vals = {e: SEMS[e].n for e in SEMS}
            for e in ENGS:
                for o, sem in SEMS.items():
                    if o != e:
                        ctl.wait(e, sem, vals[o])

        prm = E(nc.sbuf_tensor("prm", [128, 8 * 12], F32))
        acoef = E(nc.sbuf_tensor("acoef", [128, 8 * NST], F32))
        cw = E(nc.sbuf_tensor("cw", [128, 8 * KCV], F32))
        fb1 = E(nc.sbuf_tensor("fb1", [128, 32], F32))
        eb1 = E(nc.sbuf_tensor("eb1", [128, NEXP * 16], F32))
        b2r = E(nc.sbuf_tensor("b2r", [NEXP, D], BF16))
        idn = E(nc.sbuf_tensor("idn", [128, 128], F32))
        onesb = E(nc.sbuf_tensor("onesb", [1, 128], BF16))
        onescol = E(nc.sbuf_tensor("onescol", [128, 1], BF16))
        xpw = E(nc.sbuf_tensor("xpw", [128, 8 * 96], BF16))
        dtw = E(nc.sbuf_tensor("dtw", [RDT, ED], BF16))
        gwb = E(nc.sbuf_tensor("gwb", [128, 8 * NEXP], BF16))
        ws = E(nc.sbuf_tensor("ws", [128, 1024], F32))
        wb = E(nc.sbuf_tensor("wb", [128, 2048], BF16))
        xtm = [E(nc.sbuf_tensor(f"xtm{i}", [128, 1024], F32)) for i in range(2)]
        hstate = E(nc.sbuf_tensor("hstate", [128, 8 * NST], F32))
        halo = E(nc.sbuf_tensor("halo", [128, 8 * KCV], BF16))
        rows = E(nc.sbuf_tensor("rows", [1, 1536], F32))
        cst = E(nc.sbuf_tensor("cst", [128, 2], F32))
        rowsb = E(nc.sbuf_tensor("rowsb", [1, 1024], BF16))
        AR = E(nc.sbuf_tensor("arena", [128, 174 * KB], U8))

        def carve(off_b, dt, cols, parts=128):
            nbytes = cols * (4 if dt == F32 else 2)
            assert off_b % 64 == 0 and off_b + nbytes <= 174 * KB, (off_b, nbytes)
            return AR[0:parts, off_b: off_b + nbytes].bitcast(dt)

        winp = carve(0, BF16, 8 * 2048)
        dA = carve(32 * KB, F32, NST * TC)
        Hs = carve(64 * KB, BF16, NST * TC)
        BXs = carve(80 * KB, BF16, NST * TC)
        Bb = carve(96 * KB, BF16, NST * TC)
        Cb = carve(112 * KB, BF16, NST * TC)
        Xb = carve(128 * KB, BF16, 8 * TC)
        xc = Xb
        T0 = carve(136 * KB, BF16, 8 * TC)
        dl = T0
        T1 = carve(144 * KB, BF16, 8 * TC)
        sz = T1
        hm = carve(152 * KB, BF16, 8 * TC)
        xi = carve(160 * KB, BF16, TC + 32)
        dbcb = carve(161 * KB + 256, BF16, TC)
        bcf = carve(162 * KB + 256, BF16, TC)
        mb_ = carve(163 * KB + 256, BF16, TC)
        sb_ = carve(164 * KB + 256, BF16, TC)
        rb_ = carve(165 * KB + 256, BF16, TC)
        t1f = carve(166 * KB + 256, F32, TC)
        t2f = carve(168 * KB + 256, F32, TC)
        ysub = t1f
        yq = t2f
        ymt = carve(170 * KB + 256, BF16, TC)
        s0b = carve(32 * KB, F32, 512)
        s0c = carve(34 * KB, BF16, 1024)

        p_tr = E(nc.psum_tensor("p_tr", [128, 512], F32))
        p_st = E(nc.psum_tensor("p_st", [128, 512], F32))
        p_bc = E(nc.psum_tensor("p_bc", [128, 512], F32))
        p_m0 = E(nc.psum_tensor("p_m0", [128, 512], F32))
        p_m1 = E(nc.psum_tensor("p_m1", [128, 512], F32))
        banks = [p_m0, p_m1]

        def vecp(i):
            return prm[:, 8 * i: 8 * (i + 1)]

        def ecc(t, ec, w=TC):
            return t[:, ec * w:(ec + 1) * w]

        def bcast_row(dst, r0, width=TC):
            ctl.wait("tensor", s_ve)
            ctl.wait("tensor", s_ac)
            ctl.op("tensor", lambda E, r0=r0, w=width: E.matmul(
                p_bc[:, 0:w], onesb[:, :], rowsb[0:1, r0:r0 + w],
                start=True, stop=True), s_pe)
            ctl.wait("scalar", s_pe)
            ctl.op("scalar", lambda E, dst=dst: E.activation(
                out=dst, in_=p_bc[:, 0:width], func=AF.Identity), s_ac)

        def colsum(srcT, r0, width=TC):
            ctl.wait("tensor", s_ve)
            for kc in range(8):
                ctl.op("tensor", lambda E, kc=kc, srcT=srcT, w=width: E.matmul(
                    p_st[0:1, 0:w], onescol[:, :], ecc(srcT, kc, w),
                    start=(kc == 0), stop=(kc == 7)), s_pe if kc == 7 else None)
            ctl.wait("vector", s_pe)
            ctl.op("vector", lambda E, r0=r0, w=width: E.tensor_scalar(
                out=rows[0:1, r0:r0 + w], in0=p_st[0:1, 0:w],
                scalar1=1.0 / D, scalar2=None, op0=ALU.mult), s_ve)

        def stat_rows(width=TC):
            w = width
            ctl.op("vector", lambda E: E.tensor_tensor(
                out=rows[0:1, 2 * w:3 * w], in0=rows[0:1, 0:w],
                in1=rows[0:1, 0:w], op=ALU.mult))
            ctl.op("vector", lambda E: E.tensor_tensor(
                out=rows[0:1, w:2 * w], in0=rows[0:1, w:2 * w],
                in1=rows[0:1, 2 * w:3 * w], op=ALU.subtract), s_ve)
            ctl.wait("scalar", s_ve)
            ctl.op("scalar", lambda E: E.activation(
                out=rows[0:1, 2 * w:3 * w], in_=rows[0:1, w:2 * w],
                func=AF.Sqrt, bias=cst[0:1, 0:1]), s_ac)
            ctl.wait("vector", s_ac)
            ctl.op("vector", lambda E: E.reciprocal(
                out=rows[0:1, w:2 * w], in_=rows[0:1, 2 * w:3 * w]))
            ctl.op("vector", lambda E: E.tensor_copy(
                out=rowsb[0:1, 0:w], in_=rows[0:1, 0:w]))
            ctl.op("vector", lambda E: E.tensor_copy(
                out=rowsb[0:1, w:2 * w], in_=rows[0:1, w:2 * w]), s_ve)

        def rms_row(width=TC):
            w = width
            ctl.wait("scalar", s_ve)
            ctl.op("scalar", lambda E: E.activation(
                out=rows[0:1, w:2 * w], in_=rows[0:1, 0:w], func=AF.Sqrt,
                bias=cst[0:1, 0:1]), s_ac)
            ctl.wait("vector", s_ac)
            ctl.op("vector", lambda E: E.reciprocal(
                out=rows[0:1, 0:w], in_=rows[0:1, w:2 * w]))
            ctl.op("vector", lambda E: E.tensor_copy(
                out=rowsb[0:1, 0:w], in_=rows[0:1, 0:w]), s_ve)

        def load_transpose(dst, qi, src_mk, p):
            ctl.wait("sync", s_ac)
            ctl.dma_dyn("sync", lambda E, p=p: (xtm[p][:, :], src_mk(E)), s_ld, tag="lt")
            ctl.wait("tensor", s_ld)
            ctl.wait("tensor", s_ac)
            for g in range(2):
                for dc4 in range(4):
                    dc = g * 4 + dc4
                    ctl.op("tensor", lambda E, p=p, dc=dc, dc4=dc4: E.transpose(
                        p_tr[:, dc4 * 128:(dc4 + 1) * 128],
                        xtm[p][:, dc * 128:(dc + 1) * 128], idn[:, :]),
                        s_pe if dc4 == 3 else None)
                ctl.wait("scalar", s_pe)
                for dc4 in range(4):
                    dc = g * 4 + dc4
                    ctl.op("scalar", lambda E, qi=qi, dc=dc, dc4=dc4: E.activation(
                        out=dst(dc)[:, qi * 128:(qi + 1) * 128],
                        in_=p_tr[:, dc4 * 128:(dc4 + 1) * 128], func=AF.Identity),
                        s_ac if dc4 == 3 else None)
                if g == 0:
                    ctl.wait("tensor", s_ac)

        for dst, src in [(prm[:, :], p_vec), (acoef[:, :], p_alog), (cw[:, :], p_cw),
                         (fb1[:, :], p_fb1), (eb1[:, :], p_eb1),
                         (idn[:, :], ident)]:
            ctl.dma("sync", dst, src, s_ld)
        ctl.op("vector", lambda E: E.memset(onesb[:, :], 1.0), s_ve)
        ctl.dma("sync", ws[:, 0:768].rearrange("p (kc r) -> p kc r", r=96),
                w_xp.rearrange("(kc p) r -> p kc r", p=128), s_ld)
        ctl.dma("sync", ws[:, 960:1024].rearrange("p (kc e) -> p kc e", e=NEXP),
                w_g.rearrange("(kc p) e -> p kc e", p=128), s_ld)
        ctl.wait("scalar", s_ld)
        ctl.op("scalar", lambda E: E.activation(out=xpw[:, :], in_=ws[:, 0:768],
                                                func=AF.Identity))
        ctl.op("scalar", lambda E: E.activation(out=gwb[:, :], in_=ws[:, 960:1024],
                                                func=AF.Identity), s_ac)
        ctl.wait("sync", s_ac)
        ctl.dma("sync", ws[0:RDT, 0:ED], w_dt, s_ld)
        ctl.wait("scalar", s_ld)
        ctl.op("scalar", lambda E: E.activation(out=dtw[:, :], in_=ws[0:RDT, 0:ED],
                                                func=AF.Identity), s_ac)
        ctl.op("scalar", lambda E: E.activation(out=acoef[:, :], in_=acoef[:, :],
                                                func=AF.Exp), s_ac)
        ctl.wait("sync", s_ac)
        ctl.dma("sync", ws[0:NEXP, 0:D], p_b2r, s_ld)
        ctl.wait("scalar", s_ld)
        ctl.op("scalar", lambda E: E.activation(out=b2r[:, :], in_=ws[0:NEXP, 0:D],
                                                func=AF.Identity), s_ac)
        ctl.wait("vector", s_ac)
        ctl.op("vector", lambda E: E.tensor_scalar(out=acoef[:, :], in0=acoef[:, :],
                                                   scalar1=-1.0, scalar2=None,
                                                   op0=ALU.mult))
        ctl.op("vector", lambda E: E.memset(onescol[:, :], 1.0))
        ctl.op("vector", lambda E: E.memset(cst[:, 0:1], EPS))
        ctl.op("vector", lambda E: E.memset(cst[:, 1:2], 1.0))
        ctl.op("vector", lambda E: E.memset(hstate[:, :], 0.0))
        ctl.op("vector", lambda E: E.memset(halo[:, :], 0.0), s_ve)

        jobs = []
        for k in range(8):
            jobs.append((w_in[k * 128:(k + 1) * 128, :],
                         d_wbf_in[k * 128:(k + 1) * 128, :], 2048))
        for k in range(8):
            jobs.append((w_op[k * 128:(k + 1) * 128, :],
                         d_wbf_op[k * 128:(k + 1) * 128, :], 1024))
        for k in range(8):
            for h in range(2):
                jobs.append((w_f1[k * 128:(k + 1) * 128, h * 2048:(h + 1) * 2048],
                             d_wbf_f1[k * 128:(k + 1) * 128, h * 2048:(h + 1) * 2048],
                             2048))
        for k in range(32):
            jobs.append((w_f2[k * 128:(k + 1) * 128, :],
                         d_wbf_f2[k * 128:(k + 1) * 128, :], 1024))
        w_e1f = w_e1.rearrange("e d h -> (e d) h")
        w_e2f = w_e2.rearrange("e h d -> (e h) d")
        for k in range(NEXP * D // 128):
            jobs.append((w_e1f[k * 128:(k + 1) * 128, :],
                         d_wbf_e1[k * 128:(k + 1) * 128, :], 2048))
        for k in range(NEXP * HEXP // 128):
            jobs.append((w_e2f[k * 128:(k + 1) * 128, :],
                         d_wbf_e2[k * 128:(k + 1) * 128, :], 1024))

        segs = []
        for (src, dst, ncols) in jobs:
            for c0 in range(0, ncols, 512):
                segs.append((src[:, c0:c0 + 512], dst[:, c0:c0 + 512]))
        ctl.wait("sync", s_ac)
        lb = [ws[:, 0:512], s0b]
        cb = [wb[:, 0:512], s0c[:, 0:512]]
        stn = [0, 0]
        for j, (src, dst) in enumerate(segs):
            p = j % 2
            ctl.wait("sync", s_st, stn[p])
            ctl.dma("sync", lb[p], src, s_ld)
            ctl.wait("scalar", s_ld)
            ctl.op("scalar", lambda E, p=p: E.activation(
                out=cb[p], in_=lb[p], func=AF.Identity), s_ac)
            ctl.wait("gpsimd", s_ac)
            ctl.dma("gpsimd", dst, cb[p], s_st)
            stn[p] = s_st.n
        ctl.wait("sync", s_st)
        barrier()

        for kc in range(8):
            ctl.dma("sync", winp[:, kc * 2048:(kc + 1) * 2048],
                    d_wbf_in[kc * 128:(kc + 1) * 128, :], s_ld)
        barrier()

        bank_free = [s_ac.n, s_ac.n]
        for tc in range(ntc):
            for tb in range(TC // 128):
                load_transpose(lambda dc: ecc(Xb, dc), tb,
                               (lambda E, tc=tc, tb=tb:
                                x_in[tc * TC + tb * 128: tc * TC + (tb + 1) * 128, :]),
                               tb % 2)
            ctl.wait("vector", s_ac)
            ctl.op("vector", lambda E: E.tensor_tensor(
                out=T0[:, :], in0=Xb[:, :], in1=Xb[:, :], op=ALU.mult), s_ve)
            colsum(Xb, 0)
            colsum(T0, TC)
            stat_rows()
            bcast_row(mb_[:, :], 0)
            bcast_row(sb_[:, :], TC)
            ctl.wait("vector", s_ac)
            for ec in range(8):
                ctl.op("vector", lambda E, ec=ec: E.tensor_tensor(
                    out=ecc(T0, ec), in0=ecc(Xb, ec), in1=mb_[:, :], op=ALU.subtract))
                ctl.op("vector", lambda E, ec=ec: E.tensor_tensor(
                    out=ecc(T0, ec), in0=ecc(T0, ec), in1=sb_[:, :], op=ALU.mult))
                ctl.op("vector", lambda E, ec=ec: E.tensor_scalar(
                    out=ecc(T1, ec), in0=ecc(T0, ec),
                    scalar1=vecp(0)[:, ec:ec + 1], scalar2=vecp(1)[:, ec:ec + 1],
                    op0=ALU.mult, op1=ALU.add))
            ctl.op("vector", lambda E: E.tensor_tensor(
                out=T0[:, :], in0=T1[:, :], in1=T1[:, :], op=ALU.mult), s_ve)
            colsum(T0, 0)
            rms_row()
            bcast_row(rb_[:, :], 0)
            ctl.wait("vector", s_ac)
            for ec in range(8):
                ctl.op("vector", lambda E, ec=ec: E.scalar_tensor_tensor(
                    out=ecc(hm, ec), in0=rb_[:, :], scalar=vecp(4)[:, ec:ec + 1],
                    in1=ecc(T1, ec), op0=ALU.mult, op1=ALU.mult),
                    s_ve if ec == 7 else None)
            if dump is not None and dump[0] == "hm" and tc == 0:
                ctl.wait("vector", s_ve)
                ctl.op("vector", lambda E: E.tensor_copy(
                    out=dA[:, 0:8 * TC], in_=hm[:, :]), s_ve)
                ctl.wait("gpsimd", s_ve)
                ctl.dma("gpsimd", o_dbg, dA[:, 0:8 * TC], s_st)

            ctl.wait("tensor", s_ve)
            for ec in range(8):
                bk = banks[ec % 2]
                ctl.wait("tensor", s_ac, bank_free[ec % 2])
                for kc in range(8):
                    ctl.op("tensor", lambda E, bk=bk, ec=ec, kc=kc: E.matmul(
                        bk[:, 0:TC],
                        winp[:, kc * 2048 + ec * 128: kc * 2048 + (ec + 1) * 128],
                        ecc(hm, kc), start=(kc == 0), stop=(kc == 7)),
                        s_pe if kc == 7 else None)
                ctl.wait("scalar", s_pe)
                ctl.wait("scalar", s_ve)
                ctl.op("scalar", lambda E, bk=bk: E.activation(
                    out=xi[:, 4:4 + TC], in_=bk[:, 0:TC], func=AF.Identity), s_ac)
                bank_free[ec % 2] = s_ac.n
                ctl.wait("vector", s_ac)
                ctl.op("vector", lambda E, ec=ec: E.tensor_copy(
                    out=xi[:, 0:4], in_=halo[:, ec * 4:(ec + 1) * 4]))
                ctl.op("vector", lambda E, ec=ec: E.tensor_scalar(
                    out=t1f[:, :], in0=xi[:, 1:1 + TC],
                    scalar1=cw[:, ec * 4:ec * 4 + 1], scalar2=vecp(5)[:, ec:ec + 1],
                    op0=ALU.mult, op1=ALU.add))
                for k in range(1, 4):
                    ctl.op("vector", lambda E, ec=ec, k=k: E.scalar_tensor_tensor(
                        out=t1f[:, :], in0=xi[:, 1 + k:1 + k + TC],
                        scalar=cw[:, ec * 4 + k:ec * 4 + k + 1], in1=t1f[:, :],
                        op0=ALU.mult, op1=ALU.add))
                ctl.op("vector", lambda E, ec=ec: E.tensor_copy(
                    out=halo[:, ec * 4:(ec + 1) * 4], in_=xi[:, TC:TC + 4]), s_ve)
                ctl.wait("scalar", s_ve)
                ctl.op("scalar", lambda E, ec=ec: E.activation(
                    out=ecc(xc, ec), in_=t1f[:, :], func=AF.Silu), s_ac)
            for ec in range(8):
                bk = banks[ec % 2]
                ctl.wait("tensor", s_ac, bank_free[ec % 2])
                for kc in range(8):
                    ctl.op("tensor", lambda E, bk=bk, ec=ec, kc=kc: E.matmul(
                        bk[:, 0:TC],
                        winp[:, kc * 2048 + 1024 + ec * 128:
                             kc * 2048 + 1024 + (ec + 1) * 128],
                        ecc(hm, kc), start=(kc == 0), stop=(kc == 7)),
                        s_pe if kc == 7 else None)
                ctl.wait("scalar", s_pe)
                ctl.op("scalar", lambda E, bk=bk, ec=ec: E.activation(
                    out=ecc(sz, ec), in_=bk[:, 0:TC], func=AF.Silu), s_ac)
                bank_free[ec % 2] = s_ac.n
            if dump is not None and dump[0] == "xc" and tc == 0:
                ctl.wait("vector", s_ac)
                ctl.op("vector", lambda E: E.tensor_copy(out=dA[:, 0:8 * TC],
                                                         in_=xc[:, :]), s_ve)
                ctl.wait("gpsimd", s_ve)
                ctl.dma("gpsimd", o_dbg, dA[:, 0:8 * TC], s_st)

            ctl.wait("tensor", s_ac)
            ctl.wait("tensor", s_ve)
            for kc in range(8):
                ctl.op("tensor", lambda E, kc=kc: E.matmul(
                    p_st[0:96, 0:TC], xpw[:, kc * 96:(kc + 1) * 96], ecc(xc, kc),
                    start=(kc == 0), stop=(kc == 7)), s_pe if kc == 7 else None)
            ctl.wait("vector", s_pe)
            ctl.op("vector", lambda E: E.tensor_copy(
                out=dbcb[0:96, :], in_=p_st[0:96, 0:TC]))
            ctl.op("vector", lambda E: E.tensor_copy(
                out=bcf[0:32, :], in_=p_st[64:96, 0:TC]), s_ve)
            ctl.wait("gpsimd", s_ve)
            ctl.dma("gpsimd", d_bc[:, tc * TC:(tc + 1) * TC], bcf[0:32, :], s_st)
            ctl.wait("sync", s_st)
            for half, dst in ((0, Bb), (1, Cb)):
                ctl.dma("sync", dst[:, :].rearrange("p (n t) -> p n t", n=NST),
                        d_bc[half * NST:(half + 1) * NST, tc * TC:(tc + 1) * TC]
                        .unsqueeze(0).partition_broadcast(128), s_ld)
            if dump is not None and dump[0] == "dbc" and tc == 0:
                ctl.wait("vector", s_ve)
                ctl.op("vector", lambda E: E.tensor_copy(out=dA[:, 0:TC],
                                                         in_=dbcb[:, :]), s_ve)
                ctl.wait("gpsimd", s_ve)
                ctl.dma("gpsimd", o_dbg, dA[:, 0:TC], s_st)
            for ec in range(8):
                bk = banks[ec % 2]
                ctl.wait("tensor", s_ac, bank_free[ec % 2])
                ctl.wait("tensor", s_ve)
                ctl.op("tensor", lambda E, bk=bk, ec=ec: E.matmul(
                    bk[:, 0:TC], dtw[:, ec * 128:(ec + 1) * 128], dbcb[0:RDT, :],
                    start=True, stop=True), s_pe)
                ctl.wait("scalar", s_pe)
                ctl.wait("scalar", s_ve)
                ctl.op("scalar", lambda E, bk=bk, ec=ec: E.activation(
                    out=t1f[:, :], in_=bk[:, 0:TC], func=AF.Abs,
                    bias=vecp(6)[:, ec:ec + 1]))
                ctl.op("scalar", lambda E: E.activation(
                    out=t1f[:, :], in_=t1f[:, :], func=AF.Exp, scale=-1.0))
                ctl.op("scalar", lambda E: E.activation(
                    out=t1f[:, :], in_=t1f[:, :], func=AF.Ln, bias=cst[:, 1:2]))
                ctl.op("scalar", lambda E, bk=bk, ec=ec: E.activation(
                    out=t2f[:, :], in_=bk[:, 0:TC], func=AF.Relu,
                    bias=vecp(6)[:, ec:ec + 1]), s_ac)
                bank_free[ec % 2] = s_ac.n
                ctl.wait("vector", s_ac)
                if dump is not None and dump[0] == "dtp" and tc == 0 and ec == 3:
                    ctl.op("vector", lambda E, bk=bk: E.tensor_copy(
                        out=dA[:, 0:TC], in_=bk[:, 0:TC]))
                    ctl.op("vector", lambda E: E.tensor_copy(
                        out=dA[:, TC:2 * TC], in_=t1f[:, :]))
                    ctl.op("vector", lambda E: E.tensor_copy(
                        out=dA[:, 2 * TC:3 * TC], in_=t2f[:, :]), s_ve)
                    ctl.wait("gpsimd", s_ve)
                    ctl.dma("gpsimd", o_dbg, dA[:, 0:3 * TC], s_st)
                ctl.op("vector", lambda E, ec=ec: E.tensor_tensor(
                    out=ecc(dl, ec), in0=t1f[:, :], in1=t2f[:, :], op=ALU.add),
                    s_ve)
            if dump is not None and dump[0] == "delta" and tc == 0:
                ctl.wait("vector", s_ve)
                ctl.op("vector", lambda E: E.tensor_copy(out=dA[:, 0:8 * TC],
                                                         in_=dl[:, :]), s_ve)
                ctl.wait("gpsimd", s_ve)
                ctl.dma("gpsimd", o_dbg, dA[:, 0:8 * TC], s_st)
                ctl.wait("scalar", s_st)

            ctl.wait("scalar", s_ve)
            ctl.wait("vector", s_ld)
            ctl.wait("scalar", s_ld)
            for ec in range(8):
                ctl.wait("scalar", s_ve)
                for n in range(NST):
                    ctl.op("scalar", lambda E, ec=ec, n=n: E.activation(
                        out=dA[:, n * TC:(n + 1) * TC], in_=ecc(dl, ec),
                        func=AF.Exp, scale=acoef[:, ec * NST + n:ec * NST + n + 1]),
                        s_ac if n == NST - 1 else None)
                ctl.wait("vector", s_st)
                ctl.op("vector", lambda E, ec=ec: E.tensor_tensor(
                    out=ymt[:, :], in0=ecc(dl, ec), in1=ecc(xc, ec), op=ALU.mult))
                ctl.wait("vector", s_ac)
                ctl.op("vector", lambda E: E.tensor_tensor(
                    out=BXs[:, :].rearrange("p (n t) -> p n t", n=NST),
                    in0=ymt[:, :].unsqueeze(1).to_broadcast([128, NST, TC]),
                    in1=Bb[:, :].rearrange("p (n t) -> p n t", n=NST),
                    op=ALU.mult))
                for n in range(NST):
                    ctl.op("vector", lambda E, ec=ec, n=n: E.tensor_tensor_scan(
                        out=Hs[:, n * TC:(n + 1) * TC],
                        data0=dA[:, n * TC:(n + 1) * TC],
                        data1=BXs[:, n * TC:(n + 1) * TC],
                        initial=hstate[:, ec * NST + n:ec * NST + n + 1],
                        op0=ALU.mult, op1=ALU.add))
                ctl.op("vector", lambda E, ec=ec: E.tensor_copy(
                    out=hstate[:, ec * NST:(ec + 1) * NST],
                    in_=Hs[:, :].rearrange("p (n t) -> p n t", n=NST)[:, :, TC - 1]))
                ctl.op("vector", lambda E: E.tensor_tensor(
                    out=BXs[:, :], in0=Hs[:, :], in1=Cb[:, :], op=ALU.mult))
                ctl.op("vector", lambda E: E.tensor_reduce(
                    out=ysub[:, :],
                    in_=BXs[:, :].rearrange("p (n t) -> p t n", n=NST),
                    axis=mybir.AxisListType.X, op=ALU.add))
                ctl.op("vector", lambda E, ec=ec: E.scalar_tensor_tensor(
                    out=yq[:, :], in0=ecc(xc, ec), scalar=vecp(7)[:, ec:ec + 1],
                    in1=ysub[:, :], op0=ALU.mult, op1=ALU.add))
                ctl.op("vector", lambda E, ec=ec: E.tensor_tensor(
                    out=ymt[:, :], in0=yq[:, :], in1=ecc(sz, ec), op=ALU.mult), s_ve)
                ctl.wait("gpsimd", s_ve)
                ctl.dma("gpsimd", d_ymix[ec * 128:(ec + 1) * 128,
                                         tc * CH:(tc + 1) * CH], ymt[:, 0:CH], s_st)
                ctl.dma("gpsimd", d_ymix[ec * 128:(ec + 1) * 128,
                                         L // 2 + tc * CH: L // 2 + (tc + 1) * CH],
                        ymt[:, CH:2 * CH], s_st)
            barrier()

        if dump is not None and dump[0] == "ymix":
            for ec in range(8):
                ctl.wait("sync", s_ve)
                ctl.dma("sync", wb[:, 0:512], d_ymix[ec * 128:(ec + 1) * 128, 0:512], s_ld)
                ctl.wait("vector", s_ld)
                ctl.op("vector", lambda E: E.tensor_copy(out=ws[:, 0:512],
                                                         in_=wb[:, 0:512]), s_ve)
                ctl.wait("gpsimd", s_ve)
                ctl.dma("gpsimd", o_dbg[ec * 128:(ec + 1) * 128, :], ws[:, 0:512], s_st)

        if not do_post:
            ctl.wait("sync", s_st)
            _replay(nc, ctl)
            return nc

        f_ = carve(0, F32, 8 * OWN)
        s2 = carve(64 * KB, BF16, 8 * OWN)
        h3 = carve(96 * KB, BF16, 8 * OWN)
        u1 = h3
        Z = 128 * KB
        ymx = carve(Z, BF16, 8 * PC)
        rfull = carve(Z + 8 * KB, F32, 8 * PC)
        x2b = carve(Z + 24 * KB, BF16, 8 * PC)
        pt1 = carve(Z + 32 * KB, F32, PC)
        pt2 = carve(Z + 34 * KB, F32, PC)
        pmb = carve(Z + 36 * KB, BF16, PC)
        psb = carve(Z + 37 * KB, BF16, PC)
        h1 = carve(Z, BF16, 4 * OWN)
        w1s = carve(Z + 16 * KB, BF16, 8 * 128)
        w2s = carve(Z + 18 * KB, BF16, 4 * 128)
        hE = carve(Z, BF16, 16 * (OWN // 2))
        e1s = carve(Z + 38 * KB, BF16, 8 * 128)
        e2s = carve(Z + 40 * KB, BF16, 16 * 128)
        gb_ = carve(Z + 32 * KB, BF16, OWN)
        gful = carve(Z + 8 * KB, BF16, OWN)
        glog = carve(Z + 12 * KB, F32, PC)
        gsc = carve(Z + 14 * KB, F32, 64)
        x3full = carve(Z + 8 * KB, F32, 8 * PC)
        ptf = carve(Z + 32 * KB, F32, PC)
        otm2 = [ws[:, 0:1024], wb[:, :].bitcast(F32)]

        def voff(E):
            if not hasattr(E, "_v256"):
                r = E.alloc_register("rho256")
                E.reg_mov(r, (E.partition_id() % 2) * (L // 2))
                E._v256 = E.snap(r, min_val=0, max_val=L // 2)
            return E._v256

        bank_free = [s_ac.n, s_ac.n]
        vb_free = [s_ve.n, s_ve.n]

        def pdump(srcT, wsrc, cols=PC):
            ctl.wait("vector", s_ac)
            ctl.wait("vector", s_pe)
            for m in range(8):
                ctl.op("vector", lambda E, m=m: E.tensor_copy(
                    out=ecc(rfull, m, PC)[:, 0:cols],
                    in_=ecc(srcT, m, wsrc)[:, 0:cols]), s_ve if m == 7 else None)
            ctl.wait("gpsimd", s_ve)
            ctl.dma("gpsimd", o_dbg, rfull[:, :], s_st)
            ctl.wait("vector", s_st)
            ctl.wait("scalar", s_st)
            ctl.wait("tensor", s_st)

        NPC_RUN = 1 if post_until.startswith("s5a_") else NPC
        for c in range(NPC_RUN):
            ctl.wait("sync", s_ve)
            ctl.wait("sync", s_ac)
            ctl.dma_dyn("sync", lambda E, c=c: (
                ymx[:, :].rearrange("p (e t) -> p e t", e=8),
                d_ymix[:, bass.ds(voff(E) + c * PC, PC)]
                .rearrange("(e p) t -> p e t", p=128)), s_ld, tag="ymx")
            for q in range(4):
                load_transpose(lambda dc: ecc(rfull, dc, PC), q,
                               (lambda E, c=c, q=q:
                                x_own[c * PC + q * 128: c * PC + (q + 1) * 128, :]),
                               q % 2)
            if post_until == "s5a_ld" and c == 0:
                ctl.wait("sync", s_ld)
                ctl.wait("sync", s_ac)
                _replay(nc, ctl)
                return nc
            ctl.wait("vector", s_ac)
            for m in range(8):
                ctl.op("vector", lambda E, m=m: E.tensor_copy(
                    out=ecc(x2b, m, PC), in_=ecc(rfull, m, PC)),
                    s_ve if m == 7 else None)
            colsum(x2b, 0, PC)
            ctl.wait("vector", s_pe)
            ctl.op("vector", lambda E: E.tensor_tensor(
                out=x2b[:, :], in0=x2b[:, :], in1=x2b[:, :], op=ALU.mult), s_ve)
            colsum(x2b, PC, PC)
            stat_rows(PC)
            bcast_row(pmb[:, :], 0, PC)
            bcast_row(psb[:, :], PC, PC)
            if post_until == "s5a_stats" and c == 0:
                ctl.wait("scalar", s_ac)
                _replay(nc, ctl)
                return nc
            for m in range(8):
                bk = banks[m % 2]
                ctl.wait("tensor", s_ve, vb_free[m % 2])
                for kc in range(8):
                    ctl.wait("sync", s_pe)
                    ctl.dma("sync", wb[:, 0:1024],
                            d_wbf_op[kc * 128:(kc + 1) * 128, :], s_ld)
                    ctl.wait("tensor", s_ld)
                    ctl.op("tensor", lambda E, bk=bk, m=m, kc=kc: E.matmul(
                        bk[:, 0:PC], wb[:, m * 128:(m + 1) * 128],
                        ecc(ymx, kc, PC), start=(kc == 0), stop=(kc == 7)),
                        s_pe)
                ctl.wait("vector", s_pe)
                ctl.op("vector", lambda E, m=m: E.tensor_tensor(
                    out=pt1[:, :], in0=ecc(rfull, m, PC), in1=pmb[:, :],
                    op=ALU.subtract))
                ctl.op("vector", lambda E: E.tensor_tensor(
                    out=pt1[:, :], in0=pt1[:, :], in1=psb[:, :], op=ALU.mult))
                ctl.op("vector", lambda E, m=m: E.tensor_scalar(
                    out=pt1[:, :], in0=pt1[:, :], scalar1=vecp(0)[:, m:m + 1],
                    scalar2=vecp(1)[:, m:m + 1], op0=ALU.mult, op1=ALU.add))
                ctl.op("vector", lambda E, m=m, bk=bk: E.tensor_tensor(
                    out=pt2[:, :], in0=bk[:, 0:PC], in1=ecc(rfull, m, PC),
                    op=ALU.add), s_ve)
                vb_free[m % 2] = s_ve.n
                ctl.op("vector", lambda E, m=m: E.tensor_tensor(
                    out=ecc(x2b, m, PC), in0=pt2[:, :], in1=pt1[:, :], op=ALU.add))
                ctl.op("vector", lambda E, m=m, c=c: E.tensor_tensor(
                    out=ecc(s2, m, OWN)[:, c * PC:(c + 1) * PC],
                    in0=ecc(x2b, m, PC), in1=ecc(rfull, m, PC), op=ALU.add), s_ve)
            if post_until == "s5a_op" and c == 0:
                ctl.wait("vector", s_ve)
                _replay(nc, ctl)
                return nc
            ctl.op("vector", lambda E: E.tensor_tensor(
                out=ymx[:, :], in0=x2b[:, :], in1=x2b[:, :], op=ALU.mult), s_ve)
            colsum(x2b, 0, PC)
            colsum(ymx, PC, PC)
            stat_rows(PC)
            bcast_row(pmb[:, :], 0, PC)
            bcast_row(psb[:, :], PC, PC)
            ctl.wait("vector", s_ac)
            for m in range(8):
                ctl.op("vector", lambda E, m=m: E.tensor_tensor(
                    out=pt1[:, :].bitcast(BF16)[:, 0:PC], in0=ecc(x2b, m, PC),
                    in1=pmb[:, :], op=ALU.subtract))
                ctl.op("vector", lambda E: E.tensor_tensor(
                    out=pt1[:, :].bitcast(BF16)[:, 0:PC],
                    in0=pt1[:, :].bitcast(BF16)[:, 0:PC], in1=psb[:, :], op=ALU.mult))
                ctl.op("vector", lambda E, m=m, c=c: E.tensor_scalar(
                    out=ecc(h3, m, OWN)[:, c * PC:(c + 1) * PC],
                    in0=pt1[:, :].bitcast(BF16)[:, 0:PC],
                    scalar1=vecp(2)[:, m:m + 1], scalar2=vecp(3)[:, m:m + 1],
                    op0=ALU.mult, op1=ALU.add), s_ve)
            if dump is not None and c == 0 and dump[0] in ("x2", "h3", "s2"):
                dsrc = {"x2": (x2b, PC), "h3": (h3, OWN), "s2": (s2, OWN)}[dump[0]]
                pdump(dsrc[0], dsrc[1])

        barrier()
        if post_until == "s5a":
            ctl.wait("sync", s_st)
            _replay(nc, ctl)
            return nc
        bank_free = [s_ac.n, s_ac.n]
        vb_free = [s_ve.n, s_ve.n]

        for j in range(8):
            for jm in range(4):
                hidx = j * 512 + jm * 128
                ctl.wait("sync", s_pe)
                ctl.dma("sync", w1s[:, :].rearrange("p (kc h) -> p kc h", h=128),
                        d_wbf_f1.rearrange("(kc p) h -> p kc h", p=128)
                        [:, :, hidx:hidx + 128], s_ld)
                ctl.wait("tensor", s_ld)
                for tt in range(4):
                    bk = banks[tt % 2]
                    ctl.wait("tensor", s_ac, bank_free[tt % 2])
                    for kc in range(8):
                        ctl.op("tensor", lambda E, bk=bk, kc=kc, tt=tt: E.matmul(
                            bk[:, 0:PC], w1s[:, kc * 128:(kc + 1) * 128],
                            ecc(h3, kc, OWN)[:, tt * PC:(tt + 1) * PC],
                            start=(kc == 0), stop=(kc == 7)),
                            s_pe if kc == 7 else None)
                    ctl.wait("scalar", s_pe)
                    ctl.op("scalar", lambda E, bk=bk, j=j, jm=jm, tt=tt: E.activation(
                        out=ecc(h1, jm, OWN)[:, tt * PC:(tt + 1) * PC],
                        in_=bk[:, 0:PC], func=AF.Relu,
                        bias=fb1[:, j * 4 + jm:j * 4 + jm + 1]), s_ac)
                    bank_free[tt % 2] = s_ac.n
            for m in range(8):
                ctl.wait("sync", s_pe)
                ctl.dma("sync", w2s[:, :].rearrange("p (k d) -> p k d", d=128),
                        d_wbf_f2[j * 512:(j + 1) * 512, :]
                        .rearrange("(k p) d -> p k d", p=128)
                        [:, :, m * 128:(m + 1) * 128], s_ld)
                ctl.wait("tensor", s_ld)
                ctl.wait("tensor", s_ac)
                for tt in range(4):
                    bk = banks[tt % 2]
                    ctl.wait("tensor", s_ve, vb_free[tt % 2])
                    for k in range(4):
                        ctl.op("tensor", lambda E, bk=bk, k=k, tt=tt: E.matmul(
                            bk[:, 0:PC], w2s[:, k * 128:(k + 1) * 128],
                            ecc(h1, k, OWN)[:, tt * PC:(tt + 1) * PC],
                            start=(k == 0), stop=(k == 3)), s_pe if k == 3 else None)
                    ctl.wait("vector", s_pe)
                    if j == 0:
                        ctl.op("vector", lambda E, bk=bk, m=m, tt=tt: E.tensor_scalar(
                            out=ecc(f_, m, OWN)[:, tt * PC:(tt + 1) * PC],
                            in0=bk[:, 0:PC], scalar1=1.0, scalar2=None,
                            op0=ALU.mult), s_ve)
                    else:
                        ctl.op("vector", lambda E, bk=bk, m=m, tt=tt:
                               E.scalar_tensor_tensor(
                            out=ecc(f_, m, OWN)[:, tt * PC:(tt + 1) * PC],
                            in0=bk[:, 0:PC], scalar=1.0,
                            in1=ecc(f_, m, OWN)[:, tt * PC:(tt + 1) * PC],
                            op0=ALU.mult, op1=ALU.add), s_ve)
                    vb_free[tt % 2] = s_ve.n
        if dump is not None and dump[0] == "fffn":
            pdump(f_, OWN)
        barrier()
        if post_until == "ffn":
            ctl.wait("sync", s_st)
            _replay(nc, ctl)
            return nc

        for m in range(8):
            ctl.op("vector", lambda E, m=m: E.scalar_tensor_tensor(
                out=ecc(f_, m, OWN), in0=ecc(f_, m, OWN),
                scalar=vecp(8)[:, m:m + 1], in1=ecc(s2, m, OWN),
                op0=ALU.add, op1=ALU.add), s_ve)
        for tt in range(4):
            for m in range(8):
                ctl.op("vector", lambda E, m=m, tt=tt: E.tensor_copy(
                    out=ecc(x2b, m, PC), in_=ecc(f_, m, OWN)[:, tt * PC:(tt + 1) * PC]))
            ctl.op("vector", lambda E: E.tensor_tensor(
                out=ymx[:, :], in0=x2b[:, :], in1=x2b[:, :], op=ALU.mult), s_ve)
            colsum(x2b, 0, PC)
            colsum(ymx, PC, PC)
            stat_rows(PC)
            bcast_row(pmb[:, :], 0, PC)
            bcast_row(psb[:, :], PC, PC)
            ctl.wait("vector", s_ac)
            for m in range(8):
                ctl.op("vector", lambda E, m=m, tt=tt: E.tensor_tensor(
                    out=pt1[:, :].bitcast(BF16)[:, 0:PC], in0=ecc(x2b, m, PC),
                    in1=pmb[:, :], op=ALU.subtract))
                ctl.op("vector", lambda E, m=m, tt=tt: E.tensor_tensor(
                    out=ecc(u1, m, OWN)[:, tt * PC:(tt + 1) * PC],
                    in0=pt1[:, :].bitcast(BF16)[:, 0:PC], in1=psb[:, :],
                    op=ALU.mult), s_ve)
        if dump is not None and dump[0] == "u1":
            pdump(u1, OWN)
        barrier()
        if post_until == "u1":
            ctl.wait("sync", s_st)
            _replay(nc, ctl)
            return nc

        gw128 = carve(Z + 32 * KB, F32, 128)
        gb_free = [s_ac.n, s_ac.n]
        for tt in range(4):
            for sub in range(4):
                bk = banks[sub % 2]
                ctl.wait("tensor", s_ac, gb_free[sub % 2])
                ctl.wait("tensor", s_ve)
                for kc in range(8):
                    ctl.op("tensor", lambda E, bk=bk, kc=kc, tt=tt, sub=sub: E.matmul(
                        bk[:, 0:NEXP],
                        ecc(u1, kc, OWN)[:, tt * PC + sub * 128: tt * PC + (sub + 1) * 128],
                        gwb[:, kc * NEXP:(kc + 1) * NEXP],
                        start=(kc == 0), stop=(kc == 7)), s_pe if kc == 7 else None)
                ctl.wait("scalar", s_pe)
                ctl.op("scalar", lambda E, bk=bk: E.activation(
                    out=gsc[:, 0:8], in_=bk[:, 0:NEXP], func=AF.Identity), s_ac)
                gb_free[sub % 2] = s_ac.n
                ctl.wait("vector", s_ac)
                if dump is not None and dump[0] == "glx" and tt == 0 and sub == 1:
                    ctl.op("vector", lambda E: E.tensor_copy(
                        out=gw128[:, 16:24], in_=gsc[:, 0:8]), s_ve)
                    ctl.wait("gpsimd", s_ve)
                    ctl.dma("gpsimd", o_dbg, gw128[:, 16:24], s_st)
                    ctl.wait("vector", s_st)
                    ctl.wait("scalar", s_st)
                    ctl.wait("tensor", s_st)

                def vc(fn):
                    ctl.op("vector", fn, s_ve)
                    ctl.wait("vector", s_ve)
                vc(lambda E: E.max(out=gsc[:, 8:16], in_=gsc[:, 0:8]))
                if dump is not None and dump[0] == "gm8" and tt == 0 and sub == 1:
                    ctl.op("vector", lambda E: E.tensor_copy(
                        out=gw128[:, 16:24], in_=gsc[:, 0:8]))
                    ctl.op("vector", lambda E: E.tensor_copy(
                        out=gw128[:, 24:32], in_=gsc[:, 8:16]), s_ve)
                    ctl.wait("gpsimd", s_ve)
                    ctl.dma("gpsimd", o_dbg, gw128[:, 16:32], s_st)
                    ctl.wait("vector", s_st)
                    ctl.wait("scalar", s_st)
                    ctl.wait("tensor", s_st)
                vc(lambda E: E.tensor_scalar(
                    out=gsc[:, 16:17], in0=gsc[:, 8:9], scalar1=-1.0, scalar2=None,
                    op0=ALU.mult))
                ctl.wait("scalar", s_ve)
                ctl.op("scalar", lambda E: E.activation(
                    out=gsc[:, 17:25], in_=gsc[:, 0:8], func=AF.Exp,
                    bias=gsc[:, 16:17]), s_ac)
                ctl.wait("vector", s_ac)
                vc(lambda E: E.tensor_scalar(
                    out=gsc[:, 25:33], in0=gsc[:, 0:8], scalar1=gsc[:, 9:10],
                    scalar2=None, op0=ALU.is_ge))
                vc(lambda E: E.tensor_tensor(
                    out=gsc[:, 17:25], in0=gsc[:, 17:25], in1=gsc[:, 25:33],
                    op=ALU.mult))
                vc(lambda E: E.tensor_reduce(
                    out=gsc[:, 33:34], in_=gsc[:, 17:25],
                    axis=mybir.AxisListType.X, op=ALU.add))
                vc(lambda E: E.reciprocal(
                    out=gsc[:, 34:35], in_=gsc[:, 33:34]))
                vc(lambda E: E.tensor_scalar(
                    out=gw128[:, 0:8], in0=gsc[:, 17:25], scalar1=gsc[:, 34:35],
                    scalar2=None, op0=ALU.mult))
                ctl.wait("tensor", s_ve)
                ctl.wait("tensor", s_ac)
                ctl.op("tensor", lambda E: E.transpose(
                    p_tr[:, 0:128], gw128[:, :], idn[:, :]), s_pe)
                ctl.wait("scalar", s_pe)
                ctl.op("scalar", lambda E, tt=tt, sub=sub: E.activation(
                    out=gful[0:NEXP, tt * PC + sub * 128: tt * PC + (sub + 1) * 128],
                    in_=p_tr[0:NEXP, 0:128], func=AF.Identity), s_ac)
                barrier()
                ctl.wait("gpsimd", s_ac)
        ctl.dma("gpsimd", d_g[:, :], gful[0:NEXP, 0:OWN], s_st)
        if dump is not None and dump[0] == "gw":
            ctl.wait("vector", s_ac)
            ctl.op("vector", lambda E: E.tensor_copy(
                out=x2b[:, :].bitcast(F32)[0:NEXP, 0:OWN], in_=gful[0:NEXP, 0:OWN]), s_ve)
            ctl.wait("gpsimd", s_ve)
            ctl.dma("gpsimd", o_dbg, x2b[:, :].bitcast(F32)[0:NEXP, 0:OWN], s_st)
            ctl.wait("vector", s_st)
        barrier()
        if post_until == "gate":
            ctl.wait("sync", s_st)
            _replay(nc, ctl)
            return nc
        bank_free = [s_ac.n, s_ac.n]
        vb_free = [s_ve.n, s_ve.n]

        for e in range(NEXP):
            ctl.wait("sync", s_ve)
            ctl.dma("sync", gb_[:, :],
                    d_g[e:e + 1, :].partition_broadcast(128), s_ld)
            for hh in range(2):
                t0 = hh * (OWN // 2)
                for jm in range(16):
                    ctl.wait("sync", s_pe)
                    ctl.dma("sync", e1s[:, :].rearrange("p (kc h) -> p kc h", h=128),
                            d_wbf_e1[e * D:(e + 1) * D, :]
                            .rearrange("(kc p) h -> p kc h", p=128)
                            [:, :, jm * 128:(jm + 1) * 128], s_ld)
                    ctl.wait("tensor", s_ld)
                    for tt in range(2):
                        bk = banks[tt % 2]
                        ctl.wait("tensor", s_ac, bank_free[tt % 2])
                        for kc in range(8):
                            ctl.op("tensor", lambda E, bk=bk, kc=kc, tt=tt, t0=t0:
                                   E.matmul(
                                bk[:, 0:PC], e1s[:, kc * 128:(kc + 1) * 128],
                                ecc(u1, kc, OWN)[:, t0 + tt * PC: t0 + (tt + 1) * PC],
                                start=(kc == 0), stop=(kc == 7)),
                                s_pe if kc == 7 else None)
                        ctl.wait("scalar", s_pe)
                        ctl.op("scalar", lambda E, bk=bk, e=e, jm=jm, tt=tt:
                               E.activation(
                            out=ecc(hE, jm, OWN // 2)[:, tt * PC:(tt + 1) * PC],
                            in_=bk[:, 0:PC], func=AF.Relu,
                            bias=eb1[:, e * 16 + jm:e * 16 + jm + 1]), s_ac)
                        bank_free[tt % 2] = s_ac.n
                ctl.wait("vector", s_ac)
                ctl.wait("vector", s_ld)
                for jm in range(16):
                    ctl.op("vector", lambda E, jm=jm, t0=t0: E.tensor_tensor(
                        out=ecc(hE, jm, OWN // 2), in0=ecc(hE, jm, OWN // 2),
                        in1=gb_[:, t0:t0 + OWN // 2], op=ALU.mult),
                        s_ve if jm == 15 else None)
                for m in range(8):
                    ctl.wait("sync", s_pe)
                    ctl.dma("sync", e2s[:, :].rearrange("p (k d) -> p k d", d=128),
                            d_wbf_e2[e * HEXP:(e + 1) * HEXP, :]
                            .rearrange("(k p) d -> p k d", p=128)
                            [:, :, m * 128:(m + 1) * 128], s_ld)
                    ctl.wait("tensor", s_ld)
                    ctl.wait("tensor", s_ve)
                    for tt in range(2):
                        bk = banks[tt % 2]
                        ctl.wait("tensor", s_ve, vb_free[tt % 2])
                        if e == 0:
                            ctl.op("tensor", lambda E, bk=bk, m=m, tt=tt, t0=t0:
                                   E.matmul(
                                bk[:, 0:PC], b2r[:, m * 128:(m + 1) * 128],
                                gful[0:NEXP, t0 + tt * PC: t0 + (tt + 1) * PC],
                                start=True, stop=False))
                        for k in range(16):
                            ctl.op("tensor", lambda E, bk=bk, k=k, tt=tt, e=e:
                                   E.matmul(
                                bk[:, 0:PC], e2s[:, k * 128:(k + 1) * 128],
                                ecc(hE, k, OWN // 2)[:, tt * PC:(tt + 1) * PC],
                                start=(k == 0 and e != 0), stop=(k == 15)),
                                s_pe if k == 15 else None)
                        ctl.wait("vector", s_pe)
                        if e == 0:
                            ctl.op("vector", lambda E, bk=bk, m=m, tt=tt, t0=t0:
                                   E.tensor_scalar(
                                out=ecc(f_, m, OWN)[:, t0 + tt * PC: t0 + (tt + 1) * PC],
                                in0=bk[:, 0:PC], scalar1=1.0, scalar2=None,
                                op0=ALU.mult), s_ve)
                        else:
                            ctl.op("vector", lambda E, bk=bk, m=m, tt=tt, t0=t0:
                                   E.scalar_tensor_tensor(
                                out=ecc(f_, m, OWN)[:, t0 + tt * PC: t0 + (tt + 1) * PC],
                                in0=bk[:, 0:PC], scalar=1.0,
                                in1=ecc(f_, m, OWN)[:, t0 + tt * PC: t0 + (tt + 1) * PC],
                                op0=ALU.mult, op1=ALU.add), s_ve)
                        vb_free[tt % 2] = s_ve.n
        if dump is not None and dump[0] == "fmoe":
            pdump(f_, OWN)
        barrier()
        if post_until == "moe":
            ctl.wait("sync", s_st)
            _replay(nc, ctl)
            return nc

        st_done = [s_st.n, s_st.n]
        for tt in range(4):
            ctl.wait("vector", s_pe)
            for m in range(8):
                ctl.op("vector", lambda E, m=m, tt=tt: E.tensor_tensor(
                    out=ecc(x3full, m, PC),
                    in0=ecc(u1, m, OWN)[:, tt * PC:(tt + 1) * PC],
                    in1=ecc(f_, m, OWN)[:, tt * PC:(tt + 1) * PC], op=ALU.add))
            for m in range(8):
                ctl.op("vector", lambda E, m=m: E.tensor_copy(
                    out=ecc(ymx, m, PC), in_=ecc(x3full, m, PC)))
            ctl.op("vector", lambda E: E.tensor_tensor(
                out=x2b[:, :], in0=ymx[:, :], in1=ymx[:, :], op=ALU.mult), s_ve)
            colsum(ymx, 0, PC)
            colsum(x2b, PC, PC)
            stat_rows(PC)
            bcast_row(pmb[:, :], 0, PC)
            bcast_row(psb[:, :], PC, PC)
            ctl.wait("vector", s_ac)
            for m in range(8):
                ctl.op("vector", lambda E, m=m: E.tensor_tensor(
                    out=ptf[:, :], in0=ecc(x3full, m, PC), in1=pmb[:, :],
                    op=ALU.subtract))
                ctl.op("vector", lambda E, m=m: E.tensor_tensor(
                    out=ecc(x3full, m, PC), in0=ptf[:, :], in1=psb[:, :],
                    op=ALU.mult), s_ve)
            if post_until == "s5f_ln" and tt == 0:
                ctl.wait("vector", s_ve)
                _replay(nc, ctl)
                return nc
            for tb in range(4):
                if post_until.startswith("s5f_p") and tt == 0 and tb == 1:
                    nparts = int(post_until[5:])
                    ctl.wait("tensor", s_ve)
                    for g in range(2):
                        for dc4 in range(4):
                            dc = g * 4 + dc4
                            ctl.op("tensor", lambda E, dc=dc, dc4=dc4: E.transpose(
                                p_tr[:, dc4 * 128:(dc4 + 1) * 128],
                                ecc(x3full, dc, PC)[:, 128:256],
                                idn[:, :]), s_pe if dc4 == 3 else None)
                        if nparts >= 2:
                            ctl.wait("scalar", s_pe)
                            for dc4 in range(4):
                                dc = g * 4 + dc4
                                ctl.op("scalar", lambda E, dc=dc, dc4=dc4: E.activation(
                                    out=otm2[1][:, dc * 128:(dc + 1) * 128],
                                    in_=p_tr[:, dc4 * 128:(dc4 + 1) * 128],
                                    func=AF.Identity), s_ac if dc4 == 3 else None)
                            if g == 0:
                                ctl.wait("tensor", s_ac)
                    if nparts >= 3:
                        ctl.wait("gpsimd", s_ac)
                        ctl.dma("gpsimd", o_out[128:256, :], otm2[1][:, :], s_st)
                    ctl.wait("sync", s_st)
                    ctl.wait("sync", s_pe)
                    _replay(nc, ctl)
                    return nc
                p = tb % 2
                ctl.wait("tensor", s_ve)
                ctl.wait("tensor", s_ac)
                ctl.wait("scalar", s_st, st_done[p])
                for g in range(2):
                    for dc4 in range(4):
                        dc = g * 4 + dc4
                        ctl.op("tensor", lambda E, tb=tb, dc=dc, dc4=dc4: E.transpose(
                            p_tr[:, dc4 * 128:(dc4 + 1) * 128],
                            ecc(x3full, dc, PC)[:, tb * 128:(tb + 1) * 128],
                            idn[:, :]), s_pe if dc4 == 3 else None)
                    ctl.wait("scalar", s_pe)
                    for dc4 in range(4):
                        dc = g * 4 + dc4
                        ctl.op("scalar", lambda E, p=p, dc=dc, dc4=dc4: E.activation(
                            out=otm2[p][:, dc * 128:(dc + 1) * 128],
                            in_=p_tr[:, dc4 * 128:(dc4 + 1) * 128],
                            func=AF.Identity), s_ac if dc4 == 3 else None)
                    if g == 0:
                        ctl.wait("tensor", s_ac)
                ctl.wait("gpsimd", s_ac)
                ctl.dma("gpsimd",
                        o_out[tt * PC + tb * 128: tt * PC + (tb + 1) * 128, :],
                        otm2[p][:, :], s_st)
                st_done[p] = s_st.n
                if post_until == f"s5f_tb{tb}" and tt == 0:
                    ctl.wait("sync", s_st)
                    _replay(nc, ctl)
                    return nc
                if post_until == f"s5f_nw{tb}" and tt == 0:
                    _replay(nc, ctl)
                    return nc
                if post_until == "s5f_t1" and tt == 0 and tb == 0:
                    ctl.wait("sync", s_st)
                    _replay(nc, ctl)
                    return nc
                if post_until == f"s5f_tt{tt}" and tb == 3:
                    ctl.wait("sync", s_st)
                    _replay(nc, ctl)
                    return nc

        ctl.wait("sync", s_st)
        _replay(nc, ctl)
        return nc


def _replay(nc, ctl):
    with nc.Block() as block:
        def mk(eng):
            def run(E, q=ctl.q[eng]):
                for fn in q:
                    fn(E)
            return run
        block.sync(mk("sync"))
        block.tensor(mk("tensor"))
        block.vector(mk("vector"))
        block.scalar(mk("scalar"))
        block.gpsimd(mk("gpsimd"))


def _vec_block(v):
    return np.ascontiguousarray(v.reshape(8, 128).T.astype(np.float32))


def _prepare_inputs(inputs):
    f32 = np.float32
    x = np.asarray(inputs["L0_attn_x"], f32)
    pv = np.zeros((128, 96), f32)
    names = ["L0_attn_ln1_g", "L0_attn_ln1_b", "L0_attn_ln2_g", "L0_attn_ln2_b",
             "L0_attn_rms_w", "L0_attn_conv_b", "L0_attn_dt_proj_b",
             "L0_attn_D_param", "L0_attn_ffn_b2"]
    for i, nm in enumerate(names):
        pv[:, 8 * i:8 * (i + 1)] = _vec_block(np.asarray(inputs[nm], f32))
    alog = np.asarray(inputs["L0_attn_A_log"], f32).reshape(8, 128, NST) \
        .transpose(1, 0, 2).reshape(128, 8 * NST)
    cwm = np.asarray(inputs["L0_attn_conv_w"], f32).reshape(8, 128, KCV) \
        .transpose(1, 0, 2).reshape(128, 8 * KCV)
    fb1 = np.asarray(inputs["L0_attn_ffn_b1"], f32).reshape(32, 128).T
    eb1 = np.asarray(inputs["L0_ffn_b1"], f32).reshape(NEXP, 16, 128) \
        .transpose(2, 0, 1).reshape(128, NEXP * 16)
    wsrc = dict(
        w_in=inputs["L0_attn_in_proj_w"], w_xp=inputs["L0_attn_x_proj_w"],
        w_dt=inputs["L0_attn_dt_proj_w"], w_op=inputs["L0_attn_out_proj_w"],
        w_f1=inputs["L0_attn_ffn_w1"], w_f2=inputs["L0_attn_ffn_w2"],
        w_g=inputs["L0_ffn_gate"], w_e1=inputs["L0_ffn_W1"],
        w_e2=inputs["L0_ffn_W2"],
    )
    wpk = np.concatenate([np.asarray(wsrc[nm], f32).ravel()
                          for nm, _ in WPK_LAYOUT])
    ppk = np.zeros((128, PPK_COLS), f32)
    ppk[:, 0:96] = pv
    ppk[:, 96:224] = alog
    ppk[:, 224:256] = cwm
    ppk[:, 256:288] = fb1
    ppk[:, 288:416] = eb1
    ppk[:, 416:544] = np.eye(128, dtype=f32)
    ppk[0:NEXP, 544:544 + D] = np.asarray(inputs["L0_ffn_b2"], f32)
    common = dict(wpk=np.ascontiguousarray(wpk), ppk=np.ascontiguousarray(ppk))
    in_maps = []
    for c in range(8):
        b = c // 2
        rho = c % 2
        m = dict(common)
        m["x"] = np.ascontiguousarray(x[b])
        xo = x[b].reshape(L // CH, CH, D)[rho::2].reshape(OWN, D)
        m["x_own"] = np.ascontiguousarray(xo)
        in_maps.append(m)
    return in_maps


def kernel(**inputs):
    in_maps = _prepare_inputs(inputs)
    nc = build()
    res = run_bass_kernel_spmd(nc, in_maps, core_ids=list(range(8)))
    out = np.empty((B, L, D), np.float32)
    for c in range(8):
        rho = c % 2
        b = c // 2
        oo = np.asarray(res.results[c]["o_out"])
        for jc in range(8):
            g = 2 * jc + rho
            out[b, g * CH:(g + 1) * CH, :] = oo[jc * CH:(jc + 1) * CH, :]
    return out


if __name__ == "__main__":
    build()


# version 06
# speedup vs baseline: 1.3847x; 1.3847x over previous
from contextlib import ExitStack
import numpy as np

import concourse.bass as bass
import concourse.mybir as mybir
from concourse.bass_utils import run_bass_kernel_spmd

F32 = mybir.dt.float32
BF16 = mybir.dt.bfloat16
U8 = mybir.dt.uint8
AF = mybir.ActivationFunctionType
ALU = mybir.AluOpType

B, L, D = 4, 4096, 1024
ED, NST, KCV, RDT = 1024, 16, 4, 64
HFF = 4096
NEXP, HEXP = 8, 2048
TC = 512
NTC = L // TC
OWN = L // 2
PC = 512
NPC = OWN // PC
CH = 256
EPS = 1e-5
KB = 1024

ENGS = ("sync", "tensor", "vector", "scalar", "gpsimd")

WPK_LAYOUT = [
    ("w_in", (D, 2 * ED)), ("w_xp", (ED, RDT + 2 * NST)), ("w_dt", (RDT, ED)),
    ("w_op", (ED, D)), ("w_f1", (D, HFF)), ("w_f2", (HFF, D)),
    ("w_g", (D, NEXP)), ("w_e1", (NEXP, D, HEXP)), ("w_e2", (NEXP, HEXP, D)),
]
WPK_TOT = sum(int(np.prod(s)) for _, s in WPK_LAYOUT)
PPK_COLS = 544 + D


class Sem:
    def __init__(self, h):
        self.h = h
        self.n = 0


class Ctl:
    def __init__(self):
        self.q = {e: [] for e in ENGS}

    def op(self, eng, fn, sem=None, k=1):
        if sem is None:
            self.q[eng].append(fn)
        else:
            sem.n += k
            self.q[eng].append(lambda E, fn=fn, s=sem.h, k=k: fn(E).then_inc(s, k))

    def dma(self, eng, out, in_, sem):
        sem.n += 16
        self.q[eng].append(
            lambda E, o=out, i=in_, s=sem.h: E.dma_start(out=o, in_=i).then_inc(s, 16))

    def dma_dyn(self, eng, mk, sem, tag=""):
        sem.n += 16

        def run(E, mk=mk, s=sem.h, tag=tag):
            try:
                o, i = mk(E)
                E.dma_start(out=o, in_=i).then_inc(s, 16)
            except Exception:
                import sys
                print("dma_dyn failed:", tag, file=sys.stderr)
                raise
        self.q[eng].append(run)

    def wait(self, eng, sem, val=None):
        v = sem.n if val is None else val
        if v > 0:
            self.q[eng].append(lambda E, s=sem.h, v=v: E.wait_ge(s, v))


def build(do_post=True, ntc=NTC, dump=None, post_until="full", races=True):
    nc = bass.Bass("TRN2", target_bir_lowering=False, debug=False,
                   detect_race_conditions=races)

    def din(name, shape, dt=F32):
        return nc.dram_tensor(name, shape, dt, kind="ExternalInput").ap()

    x_in = din("x", [L, D])
    x_own = din("x_own", [OWN, D])
    wpk = din("wpk", [WPK_TOT])
    ppk = din("ppk", [128, PPK_COLS])

    wofs = {}
    _o = 0
    for nm, shp in WPK_LAYOUT:
        n = int(np.prod(shp))
        wofs[nm] = (_o, shp)
        _o += n

    def wview(nm):
        o, shp = wofs[nm]
        if len(shp) == 2:
            return wpk[o:o + shp[0] * shp[1]].rearrange("(a b) -> a b", b=shp[1])
        return wpk[o:o + shp[0] * shp[1] * shp[2]].rearrange(
            "(a b c) -> a b c", b=shp[1], c=shp[2])

    w_in = wview("w_in")
    w_xp = wview("w_xp")
    w_dt = wview("w_dt")
    w_op = wview("w_op")
    w_f1 = wview("w_f1")
    w_f2 = wview("w_f2")
    w_g = wview("w_g")
    w_e1 = wview("w_e1")
    w_e2 = wview("w_e2")
    p_vec = ppk[:, 0:96]
    p_alog = ppk[:, 96:224]
    p_cw = ppk[:, 224:256]
    p_fb1 = ppk[:, 256:288]
    p_eb1 = ppk[:, 288:416]
    ident = ppk[:, 416:544]
    p_b2r = ppk[0:NEXP, 544:544 + D]

    o_out = nc.dram_tensor("o_out", [OWN, D], F32, kind="ExternalOutput").ap()
    o_dbg = None
    if dump is not None:
        o_dbg = nc.dram_tensor("o_dbg", list(dump[1]), F32, kind="ExternalOutput").ap()

    def scr(name, shape, dt):
        return nc.dram_tensor(name, shape, dt).ap()

    BF_LAYOUT = [
        ("d_wbf_in", (D, 2 * ED)), ("d_wbf_op", (ED, D)), ("d_wbf_f1", (D, HFF)),
        ("d_wbf_f2", (HFF, D)), ("d_wbf_e1", (NEXP * D, HEXP)),
        ("d_wbf_e2", (NEXP * HEXP, D)), ("d_bc", (2 * NST, L)),
        ("d_ymix", (ED, L)), ("d_g", (NEXP, OWN)),
    ]
    bf_tot = sum(s[0] * s[1] for _, s in BF_LAYOUT)
    d_bf = scr("d_bf", [bf_tot], BF16)
    bofs = {}
    _b = 0
    for nm, shp in BF_LAYOUT:
        bofs[nm] = (_b, shp)
        _b += shp[0] * shp[1]

    def bview(nm):
        o, shp = bofs[nm]
        return d_bf[o:o + shp[0] * shp[1]].rearrange("(a b) -> a b", b=shp[1])

    d_wbf_in = bview("d_wbf_in")
    d_wbf_op = bview("d_wbf_op")
    d_wbf_f1 = bview("d_wbf_f1")
    d_wbf_f2 = bview("d_wbf_f2")
    d_wbf_e1 = bview("d_wbf_e1")
    d_wbf_e2 = bview("d_wbf_e2")
    d_bc = bview("d_bc")
    d_ymix = bview("d_ymix")
    d_g = bview("d_g")

    ctl = Ctl()

    with ExitStack() as ctx:
        E = ctx.enter_context

        s_ld = Sem(E(nc.semaphore("s_ld")))
        s_st = Sem(E(nc.semaphore("s_st")))
        s_pe = Sem(E(nc.semaphore("s_pe")))
        s_ve = Sem(E(nc.semaphore("s_ve")))
        s_ac = Sem(E(nc.semaphore("s_ac")))
        s_gp = Sem(E(nc.semaphore("s_gp")))
        SEMS = {"tensor": s_pe, "vector": s_ve, "scalar": s_ac, "gpsimd": s_gp}

        def barrier():
            for e, sem in SEMS.items():
                ctl.op(e, lambda E: E.drain(), sem)
            vals = {e: SEMS[e].n for e in SEMS}
            for e in ENGS:
                for o, sem in SEMS.items():
                    if o != e:
                        ctl.wait(e, sem, vals[o])

        prm = E(nc.sbuf_tensor("prm", [128, 8 * 12], F32))
        acoef = E(nc.sbuf_tensor("acoef", [128, 8 * NST], F32))
        cw = E(nc.sbuf_tensor("cw", [128, 8 * KCV], F32))
        fb1 = E(nc.sbuf_tensor("fb1", [128, 32], F32))
        eb1 = E(nc.sbuf_tensor("eb1", [128, NEXP * 16], F32))
        b2r = E(nc.sbuf_tensor("b2r", [NEXP, D], BF16))
        idn = E(nc.sbuf_tensor("idn", [128, 128], F32))
        onesb = E(nc.sbuf_tensor("onesb", [1, 128], BF16))
        onescol = E(nc.sbuf_tensor("onescol", [128, 1], BF16))
        xpw = E(nc.sbuf_tensor("xpw", [128, 8 * 96], BF16))
        dtw = E(nc.sbuf_tensor("dtw", [RDT, ED], BF16))
        gwb = E(nc.sbuf_tensor("gwb", [128, 8 * NEXP], BF16))
        ws = E(nc.sbuf_tensor("ws", [128, 1024], F32))
        wb = E(nc.sbuf_tensor("wb", [128, 2048], BF16))
        xtm = [E(nc.sbuf_tensor(f"xtm{i}", [128, 1024], F32)) for i in range(2)]
        hstate = E(nc.sbuf_tensor("hstate", [128, 8 * NST], F32))
        halo = E(nc.sbuf_tensor("halo", [128, 8 * KCV], BF16))
        rows = E(nc.sbuf_tensor("rows", [1, 1536], F32))
        cst = E(nc.sbuf_tensor("cst", [128, 2], F32))
        rowsb = E(nc.sbuf_tensor("rowsb", [1, 1024], BF16))
        AR = E(nc.sbuf_tensor("arena", [128, 174 * KB], U8))

        def carve(off_b, dt, cols, parts=128):
            nbytes = cols * (4 if dt == F32 else 2)
            assert off_b % 64 == 0 and off_b + nbytes <= 174 * KB, (off_b, nbytes)
            return AR[0:parts, off_b: off_b + nbytes].bitcast(dt)

        winp = carve(0, BF16, 8 * 2048)
        dA = carve(32 * KB, F32, NST * TC)
        Hs = carve(64 * KB, BF16, NST * TC)
        BXs = carve(80 * KB, BF16, NST * TC)
        Bb = carve(96 * KB, BF16, NST * TC)
        Cb = carve(112 * KB, BF16, NST * TC)
        Xb = carve(128 * KB, BF16, 8 * TC)
        xc = Xb
        T0 = carve(136 * KB, BF16, 8 * TC)
        dl = T0
        T1 = carve(144 * KB, BF16, 8 * TC)
        sz = T1
        hm = carve(152 * KB, BF16, 8 * TC)
        xi = carve(160 * KB, BF16, TC + 32)
        dbcb = carve(161 * KB + 256, BF16, TC)
        bcf = carve(162 * KB + 256, BF16, TC)
        mb_ = carve(163 * KB + 256, BF16, TC)
        sb_ = carve(164 * KB + 256, BF16, TC)
        rb_ = carve(165 * KB + 256, BF16, TC)
        t1f = carve(166 * KB + 256, F32, TC)
        t2f = carve(168 * KB + 256, F32, TC)
        ysub = t1f
        yq = t2f
        ymt = carve(170 * KB + 256, BF16, TC)
        s0b = carve(32 * KB, F32, 512)
        s0c = carve(34 * KB, BF16, 1024)

        p_tr = E(nc.psum_tensor("p_tr", [128, 512], F32))
        p_st = E(nc.psum_tensor("p_st", [128, 512], F32))
        p_bc = E(nc.psum_tensor("p_bc", [128, 512], F32))
        p_m0 = E(nc.psum_tensor("p_m0", [128, 512], F32))
        p_m1 = E(nc.psum_tensor("p_m1", [128, 512], F32))
        banks = [p_m0, p_m1]

        def vecp(i):
            return prm[:, 8 * i: 8 * (i + 1)]

        def ecc(t, ec, w=TC):
            return t[:, ec * w:(ec + 1) * w]

        def bcast_row(dst, r0, width=TC):
            ctl.wait("tensor", s_ve)
            ctl.wait("tensor", s_ac)
            ctl.op("tensor", lambda E, r0=r0, w=width: E.matmul(
                p_bc[:, 0:w], onesb[:, :], rowsb[0:1, r0:r0 + w],
                start=True, stop=True), s_pe)
            ctl.wait("scalar", s_pe)
            ctl.op("scalar", lambda E, dst=dst: E.activation(
                out=dst, in_=p_bc[:, 0:width], func=AF.Identity), s_ac)

        def colsum(srcT, r0, width=TC):
            ctl.wait("tensor", s_ve)
            for kc in range(8):
                ctl.op("tensor", lambda E, kc=kc, srcT=srcT, w=width: E.matmul(
                    p_st[0:1, 0:w], onescol[:, :], ecc(srcT, kc, w),
                    start=(kc == 0), stop=(kc == 7)), s_pe if kc == 7 else None)
            ctl.wait("vector", s_pe)
            ctl.op("vector", lambda E, r0=r0, w=width: E.tensor_scalar(
                out=rows[0:1, r0:r0 + w], in0=p_st[0:1, 0:w],
                scalar1=1.0 / D, scalar2=None, op0=ALU.mult), s_ve)

        def stat_rows(width=TC):
            w = width
            ctl.op("vector", lambda E: E.tensor_tensor(
                out=rows[0:1, 2 * w:3 * w], in0=rows[0:1, 0:w],
                in1=rows[0:1, 0:w], op=ALU.mult))
            ctl.op("vector", lambda E: E.tensor_tensor(
                out=rows[0:1, w:2 * w], in0=rows[0:1, w:2 * w],
                in1=rows[0:1, 2 * w:3 * w], op=ALU.subtract), s_ve)
            ctl.wait("scalar", s_ve)
            ctl.op("scalar", lambda E: E.activation(
                out=rows[0:1, 2 * w:3 * w], in_=rows[0:1, w:2 * w],
                func=AF.Sqrt, bias=cst[0:1, 0:1]), s_ac)
            ctl.wait("vector", s_ac)
            ctl.op("vector", lambda E: E.reciprocal(
                out=rows[0:1, w:2 * w], in_=rows[0:1, 2 * w:3 * w]))
            ctl.op("vector", lambda E: E.tensor_copy(
                out=rowsb[0:1, 0:w], in_=rows[0:1, 0:w]))
            ctl.op("vector", lambda E: E.tensor_copy(
                out=rowsb[0:1, w:2 * w], in_=rows[0:1, w:2 * w]), s_ve)

        def rms_row(width=TC):
            w = width
            ctl.wait("scalar", s_ve)
            ctl.op("scalar", lambda E: E.activation(
                out=rows[0:1, w:2 * w], in_=rows[0:1, 0:w], func=AF.Sqrt,
                bias=cst[0:1, 0:1]), s_ac)
            ctl.wait("vector", s_ac)
            ctl.op("vector", lambda E: E.reciprocal(
                out=rows[0:1, 0:w], in_=rows[0:1, w:2 * w]))
            ctl.op("vector", lambda E: E.tensor_copy(
                out=rowsb[0:1, 0:w], in_=rows[0:1, 0:w]), s_ve)

        def load_transpose(dst, qi, src_mk, p):
            ctl.wait("sync", s_ac)
            ctl.dma_dyn("sync", lambda E, p=p: (xtm[p][:, :], src_mk(E)), s_ld, tag="lt")
            ctl.wait("tensor", s_ld)
            ctl.wait("tensor", s_ac)
            for g in range(2):
                for dc4 in range(4):
                    dc = g * 4 + dc4
                    ctl.op("tensor", lambda E, p=p, dc=dc, dc4=dc4: E.transpose(
                        p_tr[:, dc4 * 128:(dc4 + 1) * 128],
                        xtm[p][:, dc * 128:(dc + 1) * 128], idn[:, :]),
                        s_pe if dc4 == 3 else None)
                ctl.wait("scalar", s_pe)
                for dc4 in range(4):
                    dc = g * 4 + dc4
                    ctl.op("scalar", lambda E, qi=qi, dc=dc, dc4=dc4: E.activation(
                        out=dst(dc)[:, qi * 128:(qi + 1) * 128],
                        in_=p_tr[:, dc4 * 128:(dc4 + 1) * 128], func=AF.Identity),
                        s_ac if dc4 == 3 else None)
                if g == 0:
                    ctl.wait("tensor", s_ac)

        for dst, src in [(prm[:, :], p_vec), (acoef[:, :], p_alog), (cw[:, :], p_cw),
                         (fb1[:, :], p_fb1), (eb1[:, :], p_eb1),
                         (idn[:, :], ident)]:
            ctl.dma("sync", dst, src, s_ld)
        ctl.op("vector", lambda E: E.memset(onesb[:, :], 1.0), s_ve)
        ctl.dma("sync", ws[:, 0:768].rearrange("p (kc r) -> p kc r", r=96),
                w_xp.rearrange("(kc p) r -> p kc r", p=128), s_ld)
        ctl.dma("sync", ws[:, 960:1024].rearrange("p (kc e) -> p kc e", e=NEXP),
                w_g.rearrange("(kc p) e -> p kc e", p=128), s_ld)
        ctl.wait("scalar", s_ld)
        ctl.op("scalar", lambda E: E.activation(out=xpw[:, :], in_=ws[:, 0:768],
                                                func=AF.Identity))
        ctl.op("scalar", lambda E: E.activation(out=gwb[:, :], in_=ws[:, 960:1024],
                                                func=AF.Identity), s_ac)
        ctl.wait("sync", s_ac)
        ctl.dma("sync", ws[0:RDT, 0:ED], w_dt, s_ld)
        ctl.wait("scalar", s_ld)
        ctl.op("scalar", lambda E: E.activation(out=dtw[:, :], in_=ws[0:RDT, 0:ED],
                                                func=AF.Identity), s_ac)
        ctl.op("scalar", lambda E: E.activation(out=acoef[:, :], in_=acoef[:, :],
                                                func=AF.Exp), s_ac)
        ctl.wait("sync", s_ac)
        ctl.dma("sync", ws[0:NEXP, 0:D], p_b2r, s_ld)
        ctl.wait("scalar", s_ld)
        ctl.op("scalar", lambda E: E.activation(out=b2r[:, :], in_=ws[0:NEXP, 0:D],
                                                func=AF.Identity), s_ac)
        ctl.wait("vector", s_ac)
        ctl.op("vector", lambda E: E.tensor_scalar(out=acoef[:, :], in0=acoef[:, :],
                                                   scalar1=-1.0, scalar2=None,
                                                   op0=ALU.mult))
        ctl.op("vector", lambda E: E.memset(onescol[:, :], 1.0))
        ctl.op("vector", lambda E: E.memset(cst[:, 0:1], EPS))
        ctl.op("vector", lambda E: E.memset(cst[:, 1:2], 1.0))
        ctl.op("vector", lambda E: E.memset(hstate[:, :], 0.0))
        ctl.op("vector", lambda E: E.memset(halo[:, :], 0.0), s_ve)

        jobs = []
        for k in range(8):
            jobs.append((w_in[k * 128:(k + 1) * 128, :],
                         d_wbf_in[k * 128:(k + 1) * 128, :], 2048))
        for k in range(4):
            jobs.append((w_op.rearrange("(a p) d -> p a d", p=128)
                         [:, 2 * k:2 * k + 2, :],
                         d_wbf_op.rearrange("(a p) d -> p a d", p=128)
                         [:, 2 * k:2 * k + 2, :], 2048))
        for k in range(8):
            for h in range(2):
                jobs.append((w_f1[k * 128:(k + 1) * 128, h * 2048:(h + 1) * 2048],
                             d_wbf_f1[k * 128:(k + 1) * 128, h * 2048:(h + 1) * 2048],
                             2048))
        for k in range(16):
            jobs.append((w_f2.rearrange("(a p) d -> p a d", p=128)
                         [:, 2 * k:2 * k + 2, :],
                         d_wbf_f2.rearrange("(a p) d -> p a d", p=128)
                         [:, 2 * k:2 * k + 2, :], 2048))
        w_e1f = w_e1.rearrange("e d h -> (e d) h")
        w_e2f = w_e2.rearrange("e h d -> (e h) d")
        for k in range(NEXP * D // 128):
            jobs.append((w_e1f[k * 128:(k + 1) * 128, :],
                         d_wbf_e1[k * 128:(k + 1) * 128, :], 2048))
        for k in range(NEXP * HEXP // 256):
            jobs.append((w_e2f.rearrange("(a p) d -> p a d", p=128)
                         [:, 2 * k:2 * k + 2, :],
                         d_wbf_e2.rearrange("(a p) d -> p a d", p=128)
                         [:, 2 * k:2 * k + 2, :], 2048))

        s0f = [ws[:, 0:1024].bitcast(U8).bitcast(F32) if False else ws[:, :],
               carve(32 * KB, F32, 2048), carve(40 * KB, F32, 2048)]
        s0f[0] = carve(48 * KB, F32, 2048)
        s0cb = [wb[:, :], carve(56 * KB, BF16, 2048), carve(60 * KB, BF16, 2048)]
        ND = 3
        ld_aft = [0] * len(jobs)
        cast_aft = [0] * len(jobs)
        st_aft = [0] * len(jobs)
        cast_sem = [None] * len(jobs)
        for j, (jsrc, jdst, ncols) in enumerate(jobs):
            p = j % ND
            if j >= ND:
                ctl.wait("sync", s_st, st_aft[j - ND])
            if len(jsrc.shape) == 3:
                ctl.dma("sync", s0f[p].rearrange("p (a d) -> p a d", a=2), jsrc, s_ld)
            else:
                ctl.dma("sync", s0f[p][:, 0:ncols], jsrc, s_ld)
            ld_aft[j] = s_ld.n
            ceng, csem = ("scalar", s_ac) if j % 2 == 0 else ("vector", s_ve)
            cast_sem[j] = csem
            ctl.wait(ceng, s_ld, ld_aft[j])
            if j >= ND:
                ctl.wait(ceng, s_st, st_aft[j - ND])
            if ceng == "scalar":
                ctl.op("scalar", lambda E, p=p, n=ncols: E.activation(
                    out=s0cb[p][:, 0:n], in_=s0f[p][:, 0:n], func=AF.Identity), s_ac)
            else:
                ctl.op("vector", lambda E, p=p, n=ncols: E.tensor_copy(
                    out=s0cb[p][:, 0:n], in_=s0f[p][:, 0:n]), s_ve)
            cast_aft[j] = csem.n
            ctl.wait("gpsimd", csem, cast_aft[j])
            if len(jdst.shape) == 3:
                ctl.dma("gpsimd", jdst, s0cb[p].rearrange("p (a d) -> p a d", a=2), s_st)
            else:
                ctl.dma("gpsimd", jdst[:, 0:ncols], s0cb[p][:, 0:ncols], s_st)
            st_aft[j] = s_st.n
        ctl.wait("sync", s_st)
        barrier()

        for kc in range(8):
            ctl.dma("sync", winp[:, kc * 2048:(kc + 1) * 2048],
                    d_wbf_in[kc * 128:(kc + 1) * 128, :], s_ld)
        barrier()

        bank_free = [s_ac.n, s_ac.n]
        for tc in range(ntc):
            for tb in range(TC // 128):
                load_transpose(lambda dc: ecc(Xb, dc), tb,
                               (lambda E, tc=tc, tb=tb:
                                x_in[tc * TC + tb * 128: tc * TC + (tb + 1) * 128, :]),
                               tb % 2)
            ctl.wait("vector", s_ac)
            ctl.op("vector", lambda E: E.tensor_tensor(
                out=T0[:, :], in0=Xb[:, :], in1=Xb[:, :], op=ALU.mult), s_ve)
            colsum(Xb, 0)
            colsum(T0, TC)
            stat_rows()
            bcast_row(mb_[:, :], 0)
            bcast_row(sb_[:, :], TC)
            ctl.wait("vector", s_ac)
            for ec in range(8):
                ctl.op("vector", lambda E, ec=ec: E.tensor_tensor(
                    out=ecc(T0, ec), in0=ecc(Xb, ec), in1=mb_[:, :], op=ALU.subtract))
                ctl.op("vector", lambda E, ec=ec: E.tensor_tensor(
                    out=ecc(T0, ec), in0=ecc(T0, ec), in1=sb_[:, :], op=ALU.mult))
                ctl.op("vector", lambda E, ec=ec: E.tensor_scalar(
                    out=ecc(T1, ec), in0=ecc(T0, ec),
                    scalar1=vecp(0)[:, ec:ec + 1], scalar2=vecp(1)[:, ec:ec + 1],
                    op0=ALU.mult, op1=ALU.add))
            ctl.op("vector", lambda E: E.tensor_tensor(
                out=T0[:, :], in0=T1[:, :], in1=T1[:, :], op=ALU.mult), s_ve)
            colsum(T0, 0)
            rms_row()
            bcast_row(rb_[:, :], 0)
            ctl.wait("vector", s_ac)
            for ec in range(8):
                ctl.op("vector", lambda E, ec=ec: E.scalar_tensor_tensor(
                    out=ecc(hm, ec), in0=rb_[:, :], scalar=vecp(4)[:, ec:ec + 1],
                    in1=ecc(T1, ec), op0=ALU.mult, op1=ALU.mult),
                    s_ve if ec == 7 else None)
            if dump is not None and dump[0] == "hm" and tc == 0:
                ctl.wait("vector", s_ve)
                ctl.op("vector", lambda E: E.tensor_copy(
                    out=dA[:, 0:8 * TC], in_=hm[:, :]), s_ve)
                ctl.wait("gpsimd", s_ve)
                ctl.dma("gpsimd", o_dbg, dA[:, 0:8 * TC], s_st)

            ctl.wait("tensor", s_ve)
            for ec in range(8):
                bk = banks[ec % 2]
                ctl.wait("tensor", s_ac, bank_free[ec % 2])
                for kc in range(8):
                    ctl.op("tensor", lambda E, bk=bk, ec=ec, kc=kc: E.matmul(
                        bk[:, 0:TC],
                        winp[:, kc * 2048 + ec * 128: kc * 2048 + (ec + 1) * 128],
                        ecc(hm, kc), start=(kc == 0), stop=(kc == 7)),
                        s_pe if kc == 7 else None)
                ctl.wait("scalar", s_pe)
                ctl.wait("scalar", s_ve)
                ctl.op("scalar", lambda E, bk=bk: E.activation(
                    out=xi[:, 4:4 + TC], in_=bk[:, 0:TC], func=AF.Identity), s_ac)
                bank_free[ec % 2] = s_ac.n
                ctl.wait("vector", s_ac)
                ctl.op("vector", lambda E, ec=ec: E.tensor_copy(
                    out=xi[:, 0:4], in_=halo[:, ec * 4:(ec + 1) * 4]))
                ctl.op("vector", lambda E, ec=ec: E.tensor_scalar(
                    out=t1f[:, :], in0=xi[:, 1:1 + TC],
                    scalar1=cw[:, ec * 4:ec * 4 + 1], scalar2=vecp(5)[:, ec:ec + 1],
                    op0=ALU.mult, op1=ALU.add))
                for k in range(1, 4):
                    ctl.op("vector", lambda E, ec=ec, k=k: E.scalar_tensor_tensor(
                        out=t1f[:, :], in0=xi[:, 1 + k:1 + k + TC],
                        scalar=cw[:, ec * 4 + k:ec * 4 + k + 1], in1=t1f[:, :],
                        op0=ALU.mult, op1=ALU.add))
                ctl.op("vector", lambda E, ec=ec: E.tensor_copy(
                    out=halo[:, ec * 4:(ec + 1) * 4], in_=xi[:, TC:TC + 4]), s_ve)
                ctl.wait("scalar", s_ve)
                ctl.op("scalar", lambda E, ec=ec: E.activation(
                    out=ecc(xc, ec), in_=t1f[:, :], func=AF.Silu), s_ac)
            for ec in range(8):
                bk = banks[ec % 2]
                ctl.wait("tensor", s_ac, bank_free[ec % 2])
                for kc in range(8):
                    ctl.op("tensor", lambda E, bk=bk, ec=ec, kc=kc: E.matmul(
                        bk[:, 0:TC],
                        winp[:, kc * 2048 + 1024 + ec * 128:
                             kc * 2048 + 1024 + (ec + 1) * 128],
                        ecc(hm, kc), start=(kc == 0), stop=(kc == 7)),
                        s_pe if kc == 7 else None)
                ctl.wait("scalar", s_pe)
                ctl.op("scalar", lambda E, bk=bk, ec=ec: E.activation(
                    out=ecc(sz, ec), in_=bk[:, 0:TC], func=AF.Silu), s_ac)
                bank_free[ec % 2] = s_ac.n
            if dump is not None and dump[0] == "xc" and tc == 0:
                ctl.wait("vector", s_ac)
                ctl.op("vector", lambda E: E.tensor_copy(out=dA[:, 0:8 * TC],
                                                         in_=xc[:, :]), s_ve)
                ctl.wait("gpsimd", s_ve)
                ctl.dma("gpsimd", o_dbg, dA[:, 0:8 * TC], s_st)

            ctl.wait("tensor", s_ac)
            ctl.wait("tensor", s_ve)
            for kc in range(8):
                ctl.op("tensor", lambda E, kc=kc: E.matmul(
                    p_st[0:96, 0:TC], xpw[:, kc * 96:(kc + 1) * 96], ecc(xc, kc),
                    start=(kc == 0), stop=(kc == 7)), s_pe if kc == 7 else None)
            ctl.wait("vector", s_pe)
            ctl.op("vector", lambda E: E.tensor_copy(
                out=dbcb[0:96, :], in_=p_st[0:96, 0:TC]))
            ctl.op("vector", lambda E: E.tensor_copy(
                out=bcf[0:32, :], in_=p_st[64:96, 0:TC]), s_ve)
            ctl.wait("gpsimd", s_ve)
            ctl.dma("gpsimd", d_bc[:, tc * TC:(tc + 1) * TC], bcf[0:32, :], s_st)
            ctl.wait("sync", s_st)
            for half, dst in ((0, Bb), (1, Cb)):
                ctl.dma("sync", dst[:, :].rearrange("p (n t) -> p n t", n=NST),
                        d_bc[half * NST:(half + 1) * NST, tc * TC:(tc + 1) * TC]
                        .unsqueeze(0).partition_broadcast(128), s_ld)
            if dump is not None and dump[0] == "dbc" and tc == 0:
                ctl.wait("vector", s_ve)
                ctl.op("vector", lambda E: E.tensor_copy(out=dA[:, 0:TC],
                                                         in_=dbcb[:, :]), s_ve)
                ctl.wait("gpsimd", s_ve)
                ctl.dma("gpsimd", o_dbg, dA[:, 0:TC], s_st)
            for ec in range(8):
                bk = banks[ec % 2]
                ctl.wait("tensor", s_ac, bank_free[ec % 2])
                ctl.wait("tensor", s_ve)
                ctl.op("tensor", lambda E, bk=bk, ec=ec: E.matmul(
                    bk[:, 0:TC], dtw[:, ec * 128:(ec + 1) * 128], dbcb[0:RDT, :],
                    start=True, stop=True), s_pe)
                ctl.wait("scalar", s_pe)
                ctl.wait("scalar", s_ve)
                ctl.op("scalar", lambda E, bk=bk, ec=ec: E.activation(
                    out=t1f[:, :], in_=bk[:, 0:TC], func=AF.Abs,
                    bias=vecp(6)[:, ec:ec + 1]))
                ctl.op("scalar", lambda E: E.activation(
                    out=t1f[:, :], in_=t1f[:, :], func=AF.Exp, scale=-1.0))
                ctl.op("scalar", lambda E: E.activation(
                    out=t1f[:, :], in_=t1f[:, :], func=AF.Ln, bias=cst[:, 1:2]))
                ctl.op("scalar", lambda E, bk=bk, ec=ec: E.activation(
                    out=t2f[:, :], in_=bk[:, 0:TC], func=AF.Relu,
                    bias=vecp(6)[:, ec:ec + 1]), s_ac)
                bank_free[ec % 2] = s_ac.n
                ctl.wait("vector", s_ac)
                if dump is not None and dump[0] == "dtp" and tc == 0 and ec == 3:
                    ctl.op("vector", lambda E, bk=bk: E.tensor_copy(
                        out=dA[:, 0:TC], in_=bk[:, 0:TC]))
                    ctl.op("vector", lambda E: E.tensor_copy(
                        out=dA[:, TC:2 * TC], in_=t1f[:, :]))
                    ctl.op("vector", lambda E: E.tensor_copy(
                        out=dA[:, 2 * TC:3 * TC], in_=t2f[:, :]), s_ve)
                    ctl.wait("gpsimd", s_ve)
                    ctl.dma("gpsimd", o_dbg, dA[:, 0:3 * TC], s_st)
                ctl.op("vector", lambda E, ec=ec: E.tensor_tensor(
                    out=ecc(dl, ec), in0=t1f[:, :], in1=t2f[:, :], op=ALU.add),
                    s_ve)
            if dump is not None and dump[0] == "delta" and tc == 0:
                ctl.wait("vector", s_ve)
                ctl.op("vector", lambda E: E.tensor_copy(out=dA[:, 0:8 * TC],
                                                         in_=dl[:, :]), s_ve)
                ctl.wait("gpsimd", s_ve)
                ctl.dma("gpsimd", o_dbg, dA[:, 0:8 * TC], s_st)
                ctl.wait("scalar", s_st)

            ctl.wait("scalar", s_ve)
            ctl.wait("vector", s_ld)
            ctl.wait("scalar", s_ld)
            for ec in range(8):
                ctl.wait("scalar", s_ve)
                for n in range(NST):
                    ctl.op("scalar", lambda E, ec=ec, n=n: E.activation(
                        out=dA[:, n * TC:(n + 1) * TC], in_=ecc(dl, ec),
                        func=AF.Exp, scale=acoef[:, ec * NST + n:ec * NST + n + 1]),
                        s_ac if n == NST - 1 else None)
                ctl.wait("vector", s_st)
                ctl.op("vector", lambda E, ec=ec: E.tensor_tensor(
                    out=ymt[:, :], in0=ecc(dl, ec), in1=ecc(xc, ec), op=ALU.mult))
                ctl.wait("vector", s_ac)
                ctl.op("vector", lambda E: E.tensor_tensor(
                    out=BXs[:, :].rearrange("p (n t) -> p n t", n=NST),
                    in0=ymt[:, :].unsqueeze(1).to_broadcast([128, NST, TC]),
                    in1=Bb[:, :].rearrange("p (n t) -> p n t", n=NST),
                    op=ALU.mult))
                for n in range(NST):
                    ctl.op("vector", lambda E, ec=ec, n=n: E.tensor_tensor_scan(
                        out=Hs[:, n * TC:(n + 1) * TC],
                        data0=dA[:, n * TC:(n + 1) * TC],
                        data1=BXs[:, n * TC:(n + 1) * TC],
                        initial=hstate[:, ec * NST + n:ec * NST + n + 1],
                        op0=ALU.mult, op1=ALU.add))
                ctl.op("vector", lambda E, ec=ec: E.tensor_copy(
                    out=hstate[:, ec * NST:(ec + 1) * NST],
                    in_=Hs[:, :].rearrange("p (n t) -> p n t", n=NST)[:, :, TC - 1]))
                ctl.op("vector", lambda E: E.tensor_tensor(
                    out=BXs[:, :], in0=Hs[:, :], in1=Cb[:, :], op=ALU.mult))
                ctl.op("vector", lambda E: E.tensor_reduce(
                    out=ysub[:, :],
                    in_=BXs[:, :].rearrange("p (n t) -> p t n", n=NST),
                    axis=mybir.AxisListType.X, op=ALU.add))
                ctl.op("vector", lambda E, ec=ec: E.scalar_tensor_tensor(
                    out=yq[:, :], in0=ecc(xc, ec), scalar=vecp(7)[:, ec:ec + 1],
                    in1=ysub[:, :], op0=ALU.mult, op1=ALU.add))
                ctl.op("vector", lambda E, ec=ec: E.tensor_tensor(
                    out=ymt[:, :], in0=yq[:, :], in1=ecc(sz, ec), op=ALU.mult), s_ve)
                ctl.wait("gpsimd", s_ve)
                ctl.dma("gpsimd", d_ymix[ec * 128:(ec + 1) * 128,
                                         tc * CH:(tc + 1) * CH], ymt[:, 0:CH], s_st)
                ctl.dma("gpsimd", d_ymix[ec * 128:(ec + 1) * 128,
                                         L // 2 + tc * CH: L // 2 + (tc + 1) * CH],
                        ymt[:, CH:2 * CH], s_st)
            barrier()

        if dump is not None and dump[0] == "ymix":
            for ec in range(8):
                ctl.wait("sync", s_ve)
                ctl.dma("sync", wb[:, 0:512], d_ymix[ec * 128:(ec + 1) * 128, 0:512], s_ld)
                ctl.wait("vector", s_ld)
                ctl.op("vector", lambda E: E.tensor_copy(out=ws[:, 0:512],
                                                         in_=wb[:, 0:512]), s_ve)
                ctl.wait("gpsimd", s_ve)
                ctl.dma("gpsimd", o_dbg[ec * 128:(ec + 1) * 128, :], ws[:, 0:512], s_st)

        if not do_post:
            ctl.wait("sync", s_st)
            _replay(nc, ctl)
            return nc

        f_ = carve(0, F32, 8 * OWN)
        s2 = carve(64 * KB, BF16, 8 * OWN)
        h3 = carve(96 * KB, BF16, 8 * OWN)
        u1 = h3
        Z = 128 * KB
        ymx = carve(Z, BF16, 8 * PC)
        rfull = carve(Z + 8 * KB, F32, 8 * PC)
        x2b = carve(Z + 24 * KB, BF16, 8 * PC)
        pt1 = carve(Z + 32 * KB, F32, PC)
        pt2 = carve(Z + 34 * KB, F32, PC)
        pmb = carve(Z + 36 * KB, BF16, PC)
        psb = carve(Z + 37 * KB, BF16, PC)
        h1 = carve(Z, BF16, 4 * OWN)
        w1s = carve(Z + 16 * KB, BF16, 8 * 128)
        w2s = carve(Z + 18 * KB, BF16, 4 * 128)
        hE = carve(Z, BF16, 16 * (OWN // 2))
        e1s = carve(Z + 38 * KB, BF16, 8 * 128)
        e2s = carve(Z + 40 * KB, BF16, 16 * 128)
        gb_ = carve(Z + 32 * KB, BF16, OWN)
        gful = carve(Z + 8 * KB, BF16, OWN)
        glog = carve(Z + 12 * KB, F32, PC)
        gsc = carve(Z + 14 * KB, F32, 64)
        x3full = carve(Z + 8 * KB, F32, 8 * PC)
        ptf = carve(Z + 32 * KB, F32, PC)
        otm2 = [ws[:, 0:1024], wb[:, :].bitcast(F32)]

        def voff(E):
            if not hasattr(E, "_v256"):
                r = E.alloc_register("rho256")
                E.reg_mov(r, (E.partition_id() % 2) * (L // 2))
                E._v256 = E.snap(r, min_val=0, max_val=L // 2)
            return E._v256

        bank_free = [s_ac.n, s_ac.n]
        vb_free = [s_ve.n, s_ve.n]

        def pdump(srcT, wsrc, cols=PC):
            ctl.wait("vector", s_ac)
            ctl.wait("vector", s_pe)
            for m in range(8):
                ctl.op("vector", lambda E, m=m: E.tensor_copy(
                    out=ecc(rfull, m, PC)[:, 0:cols],
                    in_=ecc(srcT, m, wsrc)[:, 0:cols]), s_ve if m == 7 else None)
            ctl.wait("gpsimd", s_ve)
            ctl.dma("gpsimd", o_dbg, rfull[:, :], s_st)
            ctl.wait("vector", s_st)
            ctl.wait("scalar", s_st)
            ctl.wait("tensor", s_st)

        NPC_RUN = 1 if post_until.startswith("s5a_") else NPC
        for c in range(NPC_RUN):
            ctl.wait("sync", s_ve)
            ctl.wait("sync", s_ac)
            ctl.dma_dyn("sync", lambda E, c=c: (
                ymx[:, :].rearrange("p (e t) -> p e t", e=8),
                d_ymix[:, bass.ds(voff(E) + c * PC, PC)]
                .rearrange("(e p) t -> p e t", p=128)), s_ld, tag="ymx")
            for q in range(4):
                load_transpose(lambda dc: ecc(rfull, dc, PC), q,
                               (lambda E, c=c, q=q:
                                x_own[c * PC + q * 128: c * PC + (q + 1) * 128, :]),
                               q % 2)
            if post_until == "s5a_ld" and c == 0:
                ctl.wait("sync", s_ld)
                ctl.wait("sync", s_ac)
                _replay(nc, ctl)
                return nc
            ctl.wait("vector", s_ac)
            for m in range(8):
                ctl.op("vector", lambda E, m=m: E.tensor_copy(
                    out=ecc(x2b, m, PC), in_=ecc(rfull, m, PC)),
                    s_ve if m == 7 else None)
            colsum(x2b, 0, PC)
            ctl.wait("vector", s_pe)
            ctl.op("vector", lambda E: E.tensor_tensor(
                out=x2b[:, :], in0=x2b[:, :], in1=x2b[:, :], op=ALU.mult), s_ve)
            colsum(x2b, PC, PC)
            stat_rows(PC)
            bcast_row(pmb[:, :], 0, PC)
            bcast_row(psb[:, :], PC, PC)
            if post_until == "s5a_stats" and c == 0:
                ctl.wait("scalar", s_ac)
                _replay(nc, ctl)
                return nc
            opbufs = [wb[:, 0:1024], wb[:, 1024:2048]]
            op_ld = {}
            op_pe = {}

            def op_load(i):
                if i >= 2:
                    ctl.wait("sync", s_pe, op_pe.get(i - 2, 0))
                else:
                    ctl.wait("sync", s_pe)
                ctl.dma("sync", opbufs[i % 2],
                        d_wbf_op[(i % 8) * 128:(i % 8 + 1) * 128, :], s_ld)
                op_ld[i] = s_ld.n

            op_load(0)
            op_load(1)
            for m in range(8):
                bk = banks[m % 2]
                ctl.wait("tensor", s_ve, vb_free[m % 2])
                for kc in range(8):
                    i = m * 8 + kc
                    ctl.wait("tensor", s_ld, op_ld[i])
                    ctl.op("tensor", lambda E, bk=bk, m=m, kc=kc, w=opbufs[i % 2]:
                           E.matmul(
                        bk[:, 0:PC], w[:, m * 128:(m + 1) * 128],
                        ecc(ymx, kc, PC), start=(kc == 0), stop=(kc == 7)),
                        s_pe)
                    op_pe[i] = s_pe.n
                    if i + 2 < 64:
                        op_load(i + 2)
                ctl.wait("vector", s_pe)
                ctl.op("vector", lambda E, m=m: E.tensor_tensor(
                    out=pt1[:, :], in0=ecc(rfull, m, PC), in1=pmb[:, :],
                    op=ALU.subtract))
                ctl.op("vector", lambda E: E.tensor_tensor(
                    out=pt1[:, :], in0=pt1[:, :], in1=psb[:, :], op=ALU.mult))
                ctl.op("vector", lambda E, m=m: E.tensor_scalar(
                    out=pt1[:, :], in0=pt1[:, :], scalar1=vecp(0)[:, m:m + 1],
                    scalar2=vecp(1)[:, m:m + 1], op0=ALU.mult, op1=ALU.add))
                ctl.op("vector", lambda E, m=m, bk=bk: E.tensor_tensor(
                    out=pt2[:, :], in0=bk[:, 0:PC], in1=ecc(rfull, m, PC),
                    op=ALU.add), s_ve)
                vb_free[m % 2] = s_ve.n
                ctl.op("vector", lambda E, m=m: E.tensor_tensor(
                    out=ecc(x2b, m, PC), in0=pt2[:, :], in1=pt1[:, :], op=ALU.add))
                ctl.op("vector", lambda E, m=m, c=c: E.tensor_tensor(
                    out=ecc(s2, m, OWN)[:, c * PC:(c + 1) * PC],
                    in0=ecc(x2b, m, PC), in1=ecc(rfull, m, PC), op=ALU.add), s_ve)
            if post_until == "s5a_op" and c == 0:
                ctl.wait("vector", s_ve)
                _replay(nc, ctl)
                return nc
            ctl.op("vector", lambda E: E.tensor_tensor(
                out=ymx[:, :], in0=x2b[:, :], in1=x2b[:, :], op=ALU.mult), s_ve)
            colsum(x2b, 0, PC)
            colsum(ymx, PC, PC)
            stat_rows(PC)
            bcast_row(pmb[:, :], 0, PC)
            bcast_row(psb[:, :], PC, PC)
            ctl.wait("vector", s_ac)
            for m in range(8):
                ctl.op("vector", lambda E, m=m: E.tensor_tensor(
                    out=pt1[:, :].bitcast(BF16)[:, 0:PC], in0=ecc(x2b, m, PC),
                    in1=pmb[:, :], op=ALU.subtract))
                ctl.op("vector", lambda E: E.tensor_tensor(
                    out=pt1[:, :].bitcast(BF16)[:, 0:PC],
                    in0=pt1[:, :].bitcast(BF16)[:, 0:PC], in1=psb[:, :], op=ALU.mult))
                ctl.op("vector", lambda E, m=m, c=c: E.tensor_scalar(
                    out=ecc(h3, m, OWN)[:, c * PC:(c + 1) * PC],
                    in0=pt1[:, :].bitcast(BF16)[:, 0:PC],
                    scalar1=vecp(2)[:, m:m + 1], scalar2=vecp(3)[:, m:m + 1],
                    op0=ALU.mult, op1=ALU.add), s_ve)
            if dump is not None and c == 0 and dump[0] in ("x2", "h3", "s2"):
                dsrc = {"x2": (x2b, PC), "h3": (h3, OWN), "s2": (s2, OWN)}[dump[0]]
                pdump(dsrc[0], dsrc[1])

        barrier()
        if post_until == "s5a":
            ctl.wait("sync", s_st)
            _replay(nc, ctl)
            return nc
        bank_free = [s_ac.n, s_ac.n]
        vb_free = [s_ve.n, s_ve.n]

        for j in range(8):
            for jm in range(4):
                hidx = j * 512 + jm * 128
                ctl.wait("sync", s_pe)
                ctl.dma("sync", w1s[:, :].rearrange("p (kc h) -> p kc h", h=128),
                        d_wbf_f1.rearrange("(kc p) h -> p kc h", p=128)
                        [:, :, hidx:hidx + 128], s_ld)
                ctl.wait("tensor", s_ld)
                for tt in range(4):
                    bk = banks[tt % 2]
                    ctl.wait("tensor", s_ac, bank_free[tt % 2])
                    for kc in range(8):
                        ctl.op("tensor", lambda E, bk=bk, kc=kc, tt=tt: E.matmul(
                            bk[:, 0:PC], w1s[:, kc * 128:(kc + 1) * 128],
                            ecc(h3, kc, OWN)[:, tt * PC:(tt + 1) * PC],
                            start=(kc == 0), stop=(kc == 7)),
                            s_pe if kc == 7 else None)
                    ctl.wait("scalar", s_pe)
                    ctl.op("scalar", lambda E, bk=bk, j=j, jm=jm, tt=tt: E.activation(
                        out=ecc(h1, jm, OWN)[:, tt * PC:(tt + 1) * PC],
                        in_=bk[:, 0:PC], func=AF.Relu,
                        bias=fb1[:, j * 4 + jm:j * 4 + jm + 1]), s_ac)
                    bank_free[tt % 2] = s_ac.n
            for m in range(8):
                ctl.wait("sync", s_pe)
                ctl.dma("sync", w2s[:, :].rearrange("p (k d) -> p k d", d=128),
                        d_wbf_f2[j * 512:(j + 1) * 512, :]
                        .rearrange("(k p) d -> p k d", p=128)
                        [:, :, m * 128:(m + 1) * 128], s_ld)
                ctl.wait("tensor", s_ld)
                ctl.wait("tensor", s_ac)
                for tt in range(4):
                    bk = banks[tt % 2]
                    ctl.wait("tensor", s_ve, vb_free[tt % 2])
                    for k in range(4):
                        ctl.op("tensor", lambda E, bk=bk, k=k, tt=tt: E.matmul(
                            bk[:, 0:PC], w2s[:, k * 128:(k + 1) * 128],
                            ecc(h1, k, OWN)[:, tt * PC:(tt + 1) * PC],
                            start=(k == 0), stop=(k == 3)), s_pe if k == 3 else None)
                    ctl.wait("vector", s_pe)
                    if j == 0:
                        ctl.op("vector", lambda E, bk=bk, m=m, tt=tt: E.tensor_scalar(
                            out=ecc(f_, m, OWN)[:, tt * PC:(tt + 1) * PC],
                            in0=bk[:, 0:PC], scalar1=1.0, scalar2=None,
                            op0=ALU.mult), s_ve)
                    else:
                        ctl.op("vector", lambda E, bk=bk, m=m, tt=tt:
                               E.scalar_tensor_tensor(
                            out=ecc(f_, m, OWN)[:, tt * PC:(tt + 1) * PC],
                            in0=bk[:, 0:PC], scalar=1.0,
                            in1=ecc(f_, m, OWN)[:, tt * PC:(tt + 1) * PC],
                            op0=ALU.mult, op1=ALU.add), s_ve)
                    vb_free[tt % 2] = s_ve.n
        if dump is not None and dump[0] == "fffn":
            pdump(f_, OWN)
        barrier()
        if post_until == "ffn":
            ctl.wait("sync", s_st)
            _replay(nc, ctl)
            return nc

        for m in range(8):
            ctl.op("vector", lambda E, m=m: E.scalar_tensor_tensor(
                out=ecc(f_, m, OWN), in0=ecc(f_, m, OWN),
                scalar=vecp(8)[:, m:m + 1], in1=ecc(s2, m, OWN),
                op0=ALU.add, op1=ALU.add), s_ve)
        for tt in range(4):
            for m in range(8):
                ctl.op("vector", lambda E, m=m, tt=tt: E.tensor_copy(
                    out=ecc(x2b, m, PC), in_=ecc(f_, m, OWN)[:, tt * PC:(tt + 1) * PC]))
            ctl.op("vector", lambda E: E.tensor_tensor(
                out=ymx[:, :], in0=x2b[:, :], in1=x2b[:, :], op=ALU.mult), s_ve)
            colsum(x2b, 0, PC)
            colsum(ymx, PC, PC)
            stat_rows(PC)
            bcast_row(pmb[:, :], 0, PC)
            bcast_row(psb[:, :], PC, PC)
            ctl.wait("vector", s_ac)
            for m in range(8):
                ctl.op("vector", lambda E, m=m, tt=tt: E.tensor_tensor(
                    out=pt1[:, :].bitcast(BF16)[:, 0:PC], in0=ecc(x2b, m, PC),
                    in1=pmb[:, :], op=ALU.subtract))
                ctl.op("vector", lambda E, m=m, tt=tt: E.tensor_tensor(
                    out=ecc(u1, m, OWN)[:, tt * PC:(tt + 1) * PC],
                    in0=pt1[:, :].bitcast(BF16)[:, 0:PC], in1=psb[:, :],
                    op=ALU.mult), s_ve)
        if dump is not None and dump[0] == "u1":
            pdump(u1, OWN)
        barrier()
        if post_until == "u1":
            ctl.wait("sync", s_st)
            _replay(nc, ctl)
            return nc

        gw128 = carve(Z + 32 * KB, F32, 128)
        gb_free = [s_ac.n, s_ac.n]
        for tt in range(4):
            for sub in range(4):
                bk = banks[sub % 2]
                ctl.wait("tensor", s_ac, gb_free[sub % 2])
                ctl.wait("tensor", s_ve)
                for kc in range(8):
                    ctl.op("tensor", lambda E, bk=bk, kc=kc, tt=tt, sub=sub: E.matmul(
                        bk[:, 0:NEXP],
                        ecc(u1, kc, OWN)[:, tt * PC + sub * 128: tt * PC + (sub + 1) * 128],
                        gwb[:, kc * NEXP:(kc + 1) * NEXP],
                        start=(kc == 0), stop=(kc == 7)), s_pe if kc == 7 else None)
                ctl.wait("scalar", s_pe)
                ctl.op("scalar", lambda E, bk=bk: E.activation(
                    out=gsc[:, 0:8], in_=bk[:, 0:NEXP], func=AF.Identity), s_ac)
                gb_free[sub % 2] = s_ac.n
                ctl.wait("vector", s_ac)
                if dump is not None and dump[0] == "glx" and tt == 0 and sub == 1:
                    ctl.op("vector", lambda E: E.tensor_copy(
                        out=gw128[:, 16:24], in_=gsc[:, 0:8]), s_ve)
                    ctl.wait("gpsimd", s_ve)
                    ctl.dma("gpsimd", o_dbg, gw128[:, 16:24], s_st)
                    ctl.wait("vector", s_st)
                    ctl.wait("scalar", s_st)
                    ctl.wait("tensor", s_st)

                def vc(fn):
                    ctl.op("vector", fn, s_ve)
                    ctl.wait("vector", s_ve)
                vc(lambda E: E.max(out=gsc[:, 8:16], in_=gsc[:, 0:8]))
                if dump is not None and dump[0] == "gm8" and tt == 0 and sub == 1:
                    ctl.op("vector", lambda E: E.tensor_copy(
                        out=gw128[:, 16:24], in_=gsc[:, 0:8]))
                    ctl.op("vector", lambda E: E.tensor_copy(
                        out=gw128[:, 24:32], in_=gsc[:, 8:16]), s_ve)
                    ctl.wait("gpsimd", s_ve)
                    ctl.dma("gpsimd", o_dbg, gw128[:, 16:32], s_st)
                    ctl.wait("vector", s_st)
                    ctl.wait("scalar", s_st)
                    ctl.wait("tensor", s_st)
                vc(lambda E: E.tensor_scalar(
                    out=gsc[:, 16:17], in0=gsc[:, 8:9], scalar1=-1.0, scalar2=None,
                    op0=ALU.mult))
                ctl.wait("scalar", s_ve)
                ctl.op("scalar", lambda E: E.activation(
                    out=gsc[:, 17:25], in_=gsc[:, 0:8], func=AF.Exp,
                    bias=gsc[:, 16:17]), s_ac)
                ctl.wait("vector", s_ac)
                vc(lambda E: E.tensor_scalar(
                    out=gsc[:, 25:33], in0=gsc[:, 0:8], scalar1=gsc[:, 9:10],
                    scalar2=None, op0=ALU.is_ge))
                vc(lambda E: E.tensor_tensor(
                    out=gsc[:, 17:25], in0=gsc[:, 17:25], in1=gsc[:, 25:33],
                    op=ALU.mult))
                vc(lambda E: E.tensor_reduce(
                    out=gsc[:, 33:34], in_=gsc[:, 17:25],
                    axis=mybir.AxisListType.X, op=ALU.add))
                vc(lambda E: E.reciprocal(
                    out=gsc[:, 34:35], in_=gsc[:, 33:34]))
                vc(lambda E: E.tensor_scalar(
                    out=gw128[:, 0:8], in0=gsc[:, 17:25], scalar1=gsc[:, 34:35],
                    scalar2=None, op0=ALU.mult))
                ctl.wait("tensor", s_ve)
                ctl.wait("tensor", s_ac)
                ctl.op("tensor", lambda E: E.transpose(
                    p_tr[:, 0:128], gw128[:, :], idn[:, :]), s_pe)
                ctl.wait("scalar", s_pe)
                ctl.op("scalar", lambda E, tt=tt, sub=sub: E.activation(
                    out=gful[0:NEXP, tt * PC + sub * 128: tt * PC + (sub + 1) * 128],
                    in_=p_tr[0:NEXP, 0:128], func=AF.Identity), s_ac)
                barrier()
                ctl.wait("gpsimd", s_ac)
        ctl.dma("gpsimd", d_g[:, :], gful[0:NEXP, 0:OWN], s_st)
        if dump is not None and dump[0] == "gw":
            ctl.wait("vector", s_ac)
            ctl.op("vector", lambda E: E.tensor_copy(
                out=x2b[:, :].bitcast(F32)[0:NEXP, 0:OWN], in_=gful[0:NEXP, 0:OWN]), s_ve)
            ctl.wait("gpsimd", s_ve)
            ctl.dma("gpsimd", o_dbg, x2b[:, :].bitcast(F32)[0:NEXP, 0:OWN], s_st)
            ctl.wait("vector", s_st)
        barrier()
        if post_until == "gate":
            ctl.wait("sync", s_st)
            _replay(nc, ctl)
            return nc
        bank_free = [s_ac.n, s_ac.n]
        vb_free = [s_ve.n, s_ve.n]

        e1sB = carve(Z + 36 * KB, BF16, 8 * 128)
        e1sC = wb[:, 0:1024]
        e2sB = ws[:, :].bitcast(BF16)
        e1bufs = [e1s, e1sB, e1sC]
        e2bufs = [e2s, e2sB]
        e1_ld = {}
        e1_pe = {}
        e2_ld = {}
        e2_pe = {}

        def e1_load(i):
            q, r = divmod(i, 16)
            ee, hh2 = divmod(q, 2)
            if i >= 3:
                ctl.wait("sync", s_pe, e1_pe.get(i - 3, 0))
            ctl.dma("sync", e1bufs[i % 3][:, :].rearrange("p (kc h) -> p kc h", h=128),
                    d_wbf_e1[ee * D:(ee + 1) * D, :]
                    .rearrange("(kc p) h -> p kc h", p=128)
                    [:, :, r * 128:(r + 1) * 128], s_ld)
            e1_ld[i] = s_ld.n

        def e2_load(i):
            q, m2 = divmod(i, 8)
            ee, hh2 = divmod(q, 2)
            if i >= 2:
                ctl.wait("sync", s_pe, e2_pe.get(i - 2, 0))
            ctl.dma("sync", e2bufs[i % 2][:, :].rearrange("p (k d) -> p k d", d=128),
                    d_wbf_e2[ee * HEXP:(ee + 1) * HEXP, :]
                    .rearrange("(k p) d -> p k d", p=128)
                    [:, :, m2 * 128:(m2 + 1) * 128], s_ld)
            e2_ld[i] = s_ld.n

        for _i in range(3):
            e1_load(_i)
        e2_load(0)
        e2_load(1)
        for e in range(NEXP):
            ctl.wait("sync", s_ve)
            ctl.dma("sync", gb_[:, :],
                    d_g[e:e + 1, :].partition_broadcast(128), s_ld)
            for hh in range(2):
                t0 = hh * (OWN // 2)
                for jm in range(16):
                    i1 = (e * 2 + hh) * 16 + jm
                    w1c = e1bufs[i1 % 3]
                    ctl.wait("tensor", s_ld, e1_ld[i1])
                    for tt in range(2):
                        bk = banks[tt % 2]
                        ctl.wait("tensor", s_ac, bank_free[tt % 2])
                        for kc in range(8):
                            ctl.op("tensor", lambda E, bk=bk, kc=kc, tt=tt, t0=t0,
                                   w=w1c: E.matmul(
                                bk[:, 0:PC], w[:, kc * 128:(kc + 1) * 128],
                                ecc(u1, kc, OWN)[:, t0 + tt * PC: t0 + (tt + 1) * PC],
                                start=(kc == 0), stop=(kc == 7)),
                                s_pe if kc == 7 else None)
                        ctl.wait("scalar", s_pe)
                        ctl.op("scalar", lambda E, bk=bk, e=e, jm=jm, tt=tt:
                               E.activation(
                            out=ecc(hE, jm, OWN // 2)[:, tt * PC:(tt + 1) * PC],
                            in_=bk[:, 0:PC], func=AF.Relu,
                            bias=eb1[:, e * 16 + jm:e * 16 + jm + 1]), s_ac)
                        bank_free[tt % 2] = s_ac.n
                    e1_pe[i1] = s_pe.n
                    if i1 + 3 < 256:
                        e1_load(i1 + 3)
                ctl.wait("vector", s_ac)
                ctl.wait("vector", s_ld)
                for jm in range(16):
                    ctl.op("vector", lambda E, jm=jm, t0=t0: E.tensor_tensor(
                        out=ecc(hE, jm, OWN // 2), in0=ecc(hE, jm, OWN // 2),
                        in1=gb_[:, t0:t0 + OWN // 2], op=ALU.mult),
                        s_ve if jm == 15 else None)
                for m in range(8):
                    i2 = (e * 2 + hh) * 8 + m
                    w2c = e2bufs[i2 % 2]
                    ctl.wait("tensor", s_ld, e2_ld[i2])
                    ctl.wait("tensor", s_ve)
                    for tt in range(2):
                        bk = banks[tt % 2]
                        ctl.wait("tensor", s_ve, vb_free[tt % 2])
                        if e == 0:
                            ctl.op("tensor", lambda E, bk=bk, m=m, tt=tt, t0=t0:
                                   E.matmul(
                                bk[:, 0:PC], b2r[:, m * 128:(m + 1) * 128],
                                gful[0:NEXP, t0 + tt * PC: t0 + (tt + 1) * PC],
                                start=True, stop=False))
                        for k in range(16):
                            ctl.op("tensor", lambda E, bk=bk, k=k, tt=tt, e=e,
                                   w=w2c: E.matmul(
                                bk[:, 0:PC], w[:, k * 128:(k + 1) * 128],
                                ecc(hE, k, OWN // 2)[:, tt * PC:(tt + 1) * PC],
                                start=(k == 0 and e != 0), stop=(k == 15)),
                                s_pe if k == 15 else None)
                        ctl.wait("vector", s_pe)
                        if e == 0:
                            ctl.op("vector", lambda E, bk=bk, m=m, tt=tt, t0=t0:
                                   E.tensor_scalar(
                                out=ecc(f_, m, OWN)[:, t0 + tt * PC: t0 + (tt + 1) * PC],
                                in0=bk[:, 0:PC], scalar1=1.0, scalar2=None,
                                op0=ALU.mult), s_ve)
                        else:
                            ctl.op("vector", lambda E, bk=bk, m=m, tt=tt, t0=t0:
                                   E.scalar_tensor_tensor(
                                out=ecc(f_, m, OWN)[:, t0 + tt * PC: t0 + (tt + 1) * PC],
                                in0=bk[:, 0:PC], scalar=1.0,
                                in1=ecc(f_, m, OWN)[:, t0 + tt * PC: t0 + (tt + 1) * PC],
                                op0=ALU.mult, op1=ALU.add), s_ve)
                        vb_free[tt % 2] = s_ve.n
                    e2_pe[i2] = s_pe.n
                    if i2 + 2 < 128:
                        e2_load(i2 + 2)
        barrier()
        if post_until == "moe":
            ctl.wait("sync", s_st)
            _replay(nc, ctl)
            return nc

        st_done = [s_st.n, s_st.n]
        for tt in range(4):
            ctl.wait("vector", s_pe)
            for m in range(8):
                ctl.op("vector", lambda E, m=m, tt=tt: E.tensor_tensor(
                    out=ecc(x3full, m, PC),
                    in0=ecc(u1, m, OWN)[:, tt * PC:(tt + 1) * PC],
                    in1=ecc(f_, m, OWN)[:, tt * PC:(tt + 1) * PC], op=ALU.add))
            for m in range(8):
                ctl.op("vector", lambda E, m=m: E.tensor_copy(
                    out=ecc(ymx, m, PC), in_=ecc(x3full, m, PC)))
            ctl.op("vector", lambda E: E.tensor_tensor(
                out=x2b[:, :], in0=ymx[:, :], in1=ymx[:, :], op=ALU.mult), s_ve)
            colsum(ymx, 0, PC)
            colsum(x2b, PC, PC)
            stat_rows(PC)
            bcast_row(pmb[:, :], 0, PC)
            bcast_row(psb[:, :], PC, PC)
            ctl.wait("vector", s_ac)
            for m in range(8):
                ctl.op("vector", lambda E, m=m: E.tensor_tensor(
                    out=ptf[:, :], in0=ecc(x3full, m, PC), in1=pmb[:, :],
                    op=ALU.subtract))
                ctl.op("vector", lambda E, m=m: E.tensor_tensor(
                    out=ecc(x3full, m, PC), in0=ptf[:, :], in1=psb[:, :],
                    op=ALU.mult), s_ve)
            if post_until == "s5f_ln" and tt == 0:
                ctl.wait("vector", s_ve)
                _replay(nc, ctl)
                return nc
            for tb in range(4):
                if post_until.startswith("s5f_p") and tt == 0 and tb == 1:
                    nparts = int(post_until[5:])
                    ctl.wait("tensor", s_ve)
                    for g in range(2):
                        for dc4 in range(4):
                            dc = g * 4 + dc4
                            ctl.op("tensor", lambda E, dc=dc, dc4=dc4: E.transpose(
                                p_tr[:, dc4 * 128:(dc4 + 1) * 128],
                                ecc(x3full, dc, PC)[:, 128:256],
                                idn[:, :]), s_pe if dc4 == 3 else None)
                        if nparts >= 2:
                            ctl.wait("scalar", s_pe)
                            for dc4 in range(4):
                                dc = g * 4 + dc4
                                ctl.op("scalar", lambda E, dc=dc, dc4=dc4: E.activation(
                                    out=otm2[1][:, dc * 128:(dc + 1) * 128],
                                    in_=p_tr[:, dc4 * 128:(dc4 + 1) * 128],
                                    func=AF.Identity), s_ac if dc4 == 3 else None)
                            if g == 0:
                                ctl.wait("tensor", s_ac)
                    if nparts >= 3:
                        ctl.wait("gpsimd", s_ac)
                        ctl.dma("gpsimd", o_out[128:256, :], otm2[1][:, :], s_st)
                    ctl.wait("sync", s_st)
                    ctl.wait("sync", s_pe)
                    _replay(nc, ctl)
                    return nc
                p = tb % 2
                ctl.wait("tensor", s_ve)
                ctl.wait("tensor", s_ac)
                ctl.wait("scalar", s_st, st_done[p])
                for g in range(2):
                    for dc4 in range(4):
                        dc = g * 4 + dc4
                        ctl.op("tensor", lambda E, tb=tb, dc=dc, dc4=dc4: E.transpose(
                            p_tr[:, dc4 * 128:(dc4 + 1) * 128],
                            ecc(x3full, dc, PC)[:, tb * 128:(tb + 1) * 128],
                            idn[:, :]), s_pe if dc4 == 3 else None)
                    ctl.wait("scalar", s_pe)
                    for dc4 in range(4):
                        dc = g * 4 + dc4
                        ctl.op("scalar", lambda E, p=p, dc=dc, dc4=dc4: E.activation(
                            out=otm2[p][:, dc * 128:(dc + 1) * 128],
                            in_=p_tr[:, dc4 * 128:(dc4 + 1) * 128],
                            func=AF.Identity), s_ac if dc4 == 3 else None)
                    if g == 0:
                        ctl.wait("tensor", s_ac)
                ctl.wait("gpsimd", s_ac)
                ctl.dma("gpsimd",
                        o_out[tt * PC + tb * 128: tt * PC + (tb + 1) * 128, :],
                        otm2[p][:, :], s_st)
                st_done[p] = s_st.n
                if post_until == f"s5f_tb{tb}" and tt == 0:
                    ctl.wait("sync", s_st)
                    _replay(nc, ctl)
                    return nc
                if post_until == f"s5f_nw{tb}" and tt == 0:
                    _replay(nc, ctl)
                    return nc
                if post_until == "s5f_t1" and tt == 0 and tb == 0:
                    ctl.wait("sync", s_st)
                    _replay(nc, ctl)
                    return nc
                if post_until == f"s5f_tt{tt}" and tb == 3:
                    ctl.wait("sync", s_st)
                    _replay(nc, ctl)
                    return nc

        ctl.wait("sync", s_st)
        _replay(nc, ctl)
        return nc


def _replay(nc, ctl):
    with nc.Block() as block:
        def mk(eng):
            def run(E, q=ctl.q[eng]):
                for fn in q:
                    fn(E)
            return run
        block.sync(mk("sync"))
        block.tensor(mk("tensor"))
        block.vector(mk("vector"))
        block.scalar(mk("scalar"))
        block.gpsimd(mk("gpsimd"))


def _vec_block(v):
    return np.ascontiguousarray(v.reshape(8, 128).T.astype(np.float32))


def _prepare_inputs(inputs):
    f32 = np.float32
    x = np.asarray(inputs["L0_attn_x"], f32)
    pv = np.zeros((128, 96), f32)
    names = ["L0_attn_ln1_g", "L0_attn_ln1_b", "L0_attn_ln2_g", "L0_attn_ln2_b",
             "L0_attn_rms_w", "L0_attn_conv_b", "L0_attn_dt_proj_b",
             "L0_attn_D_param", "L0_attn_ffn_b2"]
    for i, nm in enumerate(names):
        pv[:, 8 * i:8 * (i + 1)] = _vec_block(np.asarray(inputs[nm], f32))
    alog = np.asarray(inputs["L0_attn_A_log"], f32).reshape(8, 128, NST) \
        .transpose(1, 0, 2).reshape(128, 8 * NST)
    cwm = np.asarray(inputs["L0_attn_conv_w"], f32).reshape(8, 128, KCV) \
        .transpose(1, 0, 2).reshape(128, 8 * KCV)
    fb1 = np.asarray(inputs["L0_attn_ffn_b1"], f32).reshape(32, 128).T
    eb1 = np.asarray(inputs["L0_ffn_b1"], f32).reshape(NEXP, 16, 128) \
        .transpose(2, 0, 1).reshape(128, NEXP * 16)
    wsrc = dict(
        w_in=inputs["L0_attn_in_proj_w"], w_xp=inputs["L0_attn_x_proj_w"],
        w_dt=inputs["L0_attn_dt_proj_w"], w_op=inputs["L0_attn_out_proj_w"],
        w_f1=inputs["L0_attn_ffn_w1"], w_f2=inputs["L0_attn_ffn_w2"],
        w_g=inputs["L0_ffn_gate"], w_e1=inputs["L0_ffn_W1"],
        w_e2=inputs["L0_ffn_W2"],
    )
    wpk = np.concatenate([np.asarray(wsrc[nm], f32).ravel()
                          for nm, _ in WPK_LAYOUT])
    ppk = np.zeros((128, PPK_COLS), f32)
    ppk[:, 0:96] = pv
    ppk[:, 96:224] = alog
    ppk[:, 224:256] = cwm
    ppk[:, 256:288] = fb1
    ppk[:, 288:416] = eb1
    ppk[:, 416:544] = np.eye(128, dtype=f32)
    ppk[0:NEXP, 544:544 + D] = np.asarray(inputs["L0_ffn_b2"], f32)
    common = dict(wpk=np.ascontiguousarray(wpk), ppk=np.ascontiguousarray(ppk))
    in_maps = []
    for c in range(8):
        b = c // 2
        rho = c % 2
        m = dict(common)
        m["x"] = np.ascontiguousarray(x[b])
        xo = x[b].reshape(L // CH, CH, D)[rho::2].reshape(OWN, D)
        m["x_own"] = np.ascontiguousarray(xo)
        in_maps.append(m)
    return in_maps


def kernel(**inputs):
    in_maps = _prepare_inputs(inputs)
    nc = build()
    res = run_bass_kernel_spmd(nc, in_maps, core_ids=list(range(8)))
    out = np.empty((B, L, D), np.float32)
    for c in range(8):
        rho = c % 2
        b = c // 2
        oo = np.asarray(res.results[c]["o_out"])
        for jc in range(8):
            g = 2 * jc + rho
            out[b, g * CH:(g + 1) * CH, :] = oo[jc * CH:(jc + 1) * CH, :]
    return out


if __name__ == "__main__":
    build()


# version 07 (final)
# speedup vs baseline: 1.4915x; 1.0771x over previous
from contextlib import ExitStack
import numpy as np

import concourse.bass as bass
import concourse.mybir as mybir
from concourse.bass_utils import run_bass_kernel_spmd

F32 = mybir.dt.float32
BF16 = mybir.dt.bfloat16
U8 = mybir.dt.uint8
AF = mybir.ActivationFunctionType
ALU = mybir.AluOpType

B, L, D = 4, 4096, 1024
ED, NST, KCV, RDT = 1024, 16, 4, 64
HFF = 4096
NEXP, HEXP = 8, 2048
TC = 512
NTC = L // TC
OWN = L // 2
PC = 512
NPC = OWN // PC
CH = 256
EPS = 1e-5
KB = 1024

ENGS = ("sync", "tensor", "vector", "scalar", "gpsimd")

WPK_LAYOUT = [
    ("w_in", (D, 2 * ED)), ("w_xp", (ED, RDT + 2 * NST)), ("w_dt", (RDT, ED)),
    ("w_op", (ED, D)), ("w_f1", (D, HFF)), ("w_f2", (HFF, D)),
    ("w_g", (D, NEXP)), ("w_e1", (NEXP, D, HEXP)), ("w_e2", (NEXP, HEXP, D)),
]
WPK_TOT = sum(int(np.prod(s)) for _, s in WPK_LAYOUT)
PPK_COLS = 544 + D


class Sem:
    def __init__(self, h):
        self.h = h
        self.n = 0


class Ctl:
    def __init__(self):
        self.q = {e: [] for e in ENGS}

    def op(self, eng, fn, sem=None, k=1):
        if sem is None:
            self.q[eng].append(fn)
        else:
            sem.n += k
            self.q[eng].append(lambda E, fn=fn, s=sem.h, k=k: fn(E).then_inc(s, k))

    def dma(self, eng, out, in_, sem):
        sem.n += 16
        self.q[eng].append(
            lambda E, o=out, i=in_, s=sem.h: E.dma_start(out=o, in_=i).then_inc(s, 16))

    def dma_dyn(self, eng, mk, sem, tag=""):
        sem.n += 16

        def run(E, mk=mk, s=sem.h, tag=tag):
            try:
                o, i = mk(E)
                E.dma_start(out=o, in_=i).then_inc(s, 16)
            except Exception:
                import sys
                print("dma_dyn failed:", tag, file=sys.stderr)
                raise
        self.q[eng].append(run)

    def wait(self, eng, sem, val=None):
        v = sem.n if val is None else val
        if v > 0:
            self.q[eng].append(lambda E, s=sem.h, v=v: E.wait_ge(s, v))


def build(do_post=True, ntc=NTC, dump=None, post_until="full", races=True):
    nc = bass.Bass("TRN2", target_bir_lowering=False, debug=False,
                   detect_race_conditions=races)

    def din(name, shape, dt=F32):
        return nc.dram_tensor(name, shape, dt, kind="ExternalInput").ap()

    x_in = din("x", [L, D])
    x_own = din("x_own", [OWN, D])
    wpk = din("wpk", [WPK_TOT])
    ppk = din("ppk", [128, PPK_COLS])

    wofs = {}
    _o = 0
    for nm, shp in WPK_LAYOUT:
        n = int(np.prod(shp))
        wofs[nm] = (_o, shp)
        _o += n

    def wview(nm):
        o, shp = wofs[nm]
        if len(shp) == 2:
            return wpk[o:o + shp[0] * shp[1]].rearrange("(a b) -> a b", b=shp[1])
        return wpk[o:o + shp[0] * shp[1] * shp[2]].rearrange(
            "(a b c) -> a b c", b=shp[1], c=shp[2])

    w_in = wview("w_in")
    w_xp = wview("w_xp")
    w_dt = wview("w_dt")
    w_op = wview("w_op")
    w_f1 = wview("w_f1")
    w_f2 = wview("w_f2")
    w_g = wview("w_g")
    w_e1 = wview("w_e1")
    w_e2 = wview("w_e2")
    p_vec = ppk[:, 0:96]
    p_alog = ppk[:, 96:224]
    p_cw = ppk[:, 224:256]
    p_fb1 = ppk[:, 256:288]
    p_eb1 = ppk[:, 288:416]
    ident = ppk[:, 416:544]
    p_b2r = ppk[0:NEXP, 544:544 + D]

    o_out = nc.dram_tensor("o_out", [OWN, D], F32, kind="ExternalOutput").ap()
    o_dbg = None
    if dump is not None:
        o_dbg = nc.dram_tensor("o_dbg", list(dump[1]), F32, kind="ExternalOutput").ap()

    def scr(name, shape, dt):
        return nc.dram_tensor(name, shape, dt).ap()

    BF_LAYOUT = [
        ("d_wbf_in", (D, 2 * ED)), ("d_wbf_op", (ED, D)), ("d_wbf_f1", (D, HFF)),
        ("d_wbf_f2", (HFF, D)), ("d_wbf_e1", (NEXP * D, HEXP)),
        ("d_wbf_e2", (NEXP * HEXP, D)), ("d_bc", (2 * NST, L)),
        ("d_ymix", (ED, L)), ("d_g", (NEXP, OWN)),
    ]
    bf_tot = sum(s[0] * s[1] for _, s in BF_LAYOUT)
    d_bf = scr("d_bf", [bf_tot], BF16)
    bofs = {}
    _b = 0
    for nm, shp in BF_LAYOUT:
        bofs[nm] = (_b, shp)
        _b += shp[0] * shp[1]

    def bview(nm):
        o, shp = bofs[nm]
        return d_bf[o:o + shp[0] * shp[1]].rearrange("(a b) -> a b", b=shp[1])

    d_wbf_in = bview("d_wbf_in")
    d_wbf_op = bview("d_wbf_op")
    d_wbf_f1 = bview("d_wbf_f1")
    d_wbf_f2 = bview("d_wbf_f2")
    d_wbf_e1 = bview("d_wbf_e1")
    d_wbf_e2 = bview("d_wbf_e2")
    d_bc = bview("d_bc")
    d_ymix = bview("d_ymix")
    d_g = bview("d_g")

    ctl = Ctl()

    with ExitStack() as ctx:
        E = ctx.enter_context

        s_ld = Sem(E(nc.semaphore("s_ld")))
        s_st = Sem(E(nc.semaphore("s_st")))
        s_pe = Sem(E(nc.semaphore("s_pe")))
        s_ve = Sem(E(nc.semaphore("s_ve")))
        s_ac = Sem(E(nc.semaphore("s_ac")))
        s_gp = Sem(E(nc.semaphore("s_gp")))
        SEMS = {"tensor": s_pe, "vector": s_ve, "scalar": s_ac, "gpsimd": s_gp}

        def barrier():
            for e, sem in SEMS.items():
                ctl.op(e, lambda E: E.drain(), sem)
            vals = {e: SEMS[e].n for e in SEMS}
            for e in ENGS:
                for o, sem in SEMS.items():
                    if o != e:
                        ctl.wait(e, sem, vals[o])

        prm = E(nc.sbuf_tensor("prm", [128, 8 * 12], F32))
        acoef = E(nc.sbuf_tensor("acoef", [128, 8 * NST], F32))
        cw = E(nc.sbuf_tensor("cw", [128, 8 * KCV], F32))
        fb1 = E(nc.sbuf_tensor("fb1", [128, 32], F32))
        eb1 = E(nc.sbuf_tensor("eb1", [128, NEXP * 16], F32))
        b2r = E(nc.sbuf_tensor("b2r", [NEXP, D], BF16))
        idn = E(nc.sbuf_tensor("idn", [128, 128], F32))
        onesb = E(nc.sbuf_tensor("onesb", [1, 128], BF16))
        onescol = E(nc.sbuf_tensor("onescol", [128, 1], BF16))
        xpw = E(nc.sbuf_tensor("xpw", [128, 8 * 96], BF16))
        dtw = E(nc.sbuf_tensor("dtw", [RDT, ED], BF16))
        gwb = E(nc.sbuf_tensor("gwb", [128, 8 * NEXP], BF16))
        ws = E(nc.sbuf_tensor("ws", [128, 1024], F32))
        wb = E(nc.sbuf_tensor("wb", [128, 2048], BF16))
        xtm = [E(nc.sbuf_tensor(f"xtm{i}", [128, 1024], F32)) for i in range(2)]
        hstate = E(nc.sbuf_tensor("hstate", [128, 8 * NST], F32))
        halo = E(nc.sbuf_tensor("halo", [128, 8 * KCV], BF16))
        rows = E(nc.sbuf_tensor("rows", [1, 1536], F32))
        cst = E(nc.sbuf_tensor("cst", [128, 2], F32))
        rowsb = E(nc.sbuf_tensor("rowsb", [1, 1024], BF16))
        AR = E(nc.sbuf_tensor("arena", [128, 174 * KB], U8))

        def carve(off_b, dt, cols, parts=128):
            nbytes = cols * (4 if dt == F32 else 2)
            assert off_b % 64 == 0 and off_b + nbytes <= 174 * KB, (off_b, nbytes)
            return AR[0:parts, off_b: off_b + nbytes].bitcast(dt)

        winp = carve(0, BF16, 8 * 2048)
        dA = carve(32 * KB, F32, NST * TC)
        Hs = carve(64 * KB, BF16, NST * TC)
        BXs = carve(80 * KB, BF16, NST * TC)
        Bb = carve(96 * KB, BF16, NST * TC)
        Cb = carve(112 * KB, BF16, NST * TC)
        Xb = carve(128 * KB, BF16, 8 * TC)
        xc = Xb
        T0 = carve(136 * KB, BF16, 8 * TC)
        dl = T0
        T1 = carve(144 * KB, BF16, 8 * TC)
        sz = T1
        hm = carve(152 * KB, BF16, 8 * TC)
        xi = carve(160 * KB, BF16, TC + 32)
        dbcb = carve(161 * KB + 256, BF16, TC)
        bcf = carve(162 * KB + 256, BF16, TC)
        mb_ = carve(163 * KB + 256, BF16, TC)
        sb_ = carve(164 * KB + 256, BF16, TC)
        rb_ = carve(165 * KB + 256, BF16, TC)
        t1f = carve(166 * KB + 256, F32, TC)
        t2f = carve(168 * KB + 256, F32, TC)
        ysub = t1f
        yq = t2f
        ymt = carve(170 * KB + 256, BF16, TC)
        s0b = carve(32 * KB, F32, 512)
        s0c = carve(34 * KB, BF16, 1024)

        p_tr = E(nc.psum_tensor("p_tr", [128, 512], F32))
        p_st = E(nc.psum_tensor("p_st", [128, 512], F32))
        p_bc = E(nc.psum_tensor("p_bc", [128, 512], F32))
        p_m0 = E(nc.psum_tensor("p_m0", [128, 512], F32))
        p_m1 = E(nc.psum_tensor("p_m1", [128, 512], F32))
        banks = [p_m0, p_m1]

        def vecp(i):
            return prm[:, 8 * i: 8 * (i + 1)]

        def ecc(t, ec, w=TC):
            return t[:, ec * w:(ec + 1) * w]

        def bcast_row(dst, r0, width=TC):
            ctl.wait("tensor", s_ve)
            ctl.wait("tensor", s_ac)
            ctl.op("tensor", lambda E, r0=r0, w=width: E.matmul(
                p_bc[:, 0:w], onesb[:, :], rowsb[0:1, r0:r0 + w],
                start=True, stop=True), s_pe)
            ctl.wait("scalar", s_pe)
            ctl.op("scalar", lambda E, dst=dst: E.activation(
                out=dst, in_=p_bc[:, 0:width], func=AF.Identity), s_ac)

        def colsum(srcT, r0, width=TC):
            ctl.wait("tensor", s_ve)
            for kc in range(8):
                ctl.op("tensor", lambda E, kc=kc, srcT=srcT, w=width: E.matmul(
                    p_st[0:1, 0:w], onescol[:, :], ecc(srcT, kc, w),
                    start=(kc == 0), stop=(kc == 7)), s_pe if kc == 7 else None)
            ctl.wait("vector", s_pe)
            ctl.op("vector", lambda E, r0=r0, w=width: E.tensor_scalar(
                out=rows[0:1, r0:r0 + w], in0=p_st[0:1, 0:w],
                scalar1=1.0 / D, scalar2=None, op0=ALU.mult), s_ve)

        def stat_rows(width=TC):
            w = width
            ctl.op("vector", lambda E: E.tensor_tensor(
                out=rows[0:1, 2 * w:3 * w], in0=rows[0:1, 0:w],
                in1=rows[0:1, 0:w], op=ALU.mult))
            ctl.op("vector", lambda E: E.tensor_tensor(
                out=rows[0:1, w:2 * w], in0=rows[0:1, w:2 * w],
                in1=rows[0:1, 2 * w:3 * w], op=ALU.subtract), s_ve)
            ctl.wait("scalar", s_ve)
            ctl.op("scalar", lambda E: E.activation(
                out=rows[0:1, 2 * w:3 * w], in_=rows[0:1, w:2 * w],
                func=AF.Sqrt, bias=cst[0:1, 0:1]), s_ac)
            ctl.wait("vector", s_ac)
            ctl.op("vector", lambda E: E.reciprocal(
                out=rows[0:1, w:2 * w], in_=rows[0:1, 2 * w:3 * w]))
            ctl.op("vector", lambda E: E.tensor_copy(
                out=rowsb[0:1, 0:w], in_=rows[0:1, 0:w]))
            ctl.op("vector", lambda E: E.tensor_copy(
                out=rowsb[0:1, w:2 * w], in_=rows[0:1, w:2 * w]), s_ve)

        def rms_row(width=TC):
            w = width
            ctl.wait("scalar", s_ve)
            ctl.op("scalar", lambda E: E.activation(
                out=rows[0:1, w:2 * w], in_=rows[0:1, 0:w], func=AF.Sqrt,
                bias=cst[0:1, 0:1]), s_ac)
            ctl.wait("vector", s_ac)
            ctl.op("vector", lambda E: E.reciprocal(
                out=rows[0:1, 0:w], in_=rows[0:1, w:2 * w]))
            ctl.op("vector", lambda E: E.tensor_copy(
                out=rowsb[0:1, 0:w], in_=rows[0:1, 0:w]), s_ve)

        def load_transpose(dst, qi, src_mk, p):
            ctl.wait("sync", s_ac)
            ctl.dma_dyn("sync", lambda E, p=p: (xtm[p][:, :], src_mk(E)), s_ld, tag="lt")
            ctl.wait("tensor", s_ld)
            ctl.wait("tensor", s_ac)
            for g in range(2):
                for dc4 in range(4):
                    dc = g * 4 + dc4
                    ctl.op("tensor", lambda E, p=p, dc=dc, dc4=dc4: E.transpose(
                        p_tr[:, dc4 * 128:(dc4 + 1) * 128],
                        xtm[p][:, dc * 128:(dc + 1) * 128], idn[:, :]),
                        s_pe if dc4 == 3 else None)
                ctl.wait("scalar", s_pe)
                for dc4 in range(4):
                    dc = g * 4 + dc4
                    ctl.op("scalar", lambda E, qi=qi, dc=dc, dc4=dc4: E.activation(
                        out=dst(dc)[:, qi * 128:(qi + 1) * 128],
                        in_=p_tr[:, dc4 * 128:(dc4 + 1) * 128], func=AF.Identity),
                        s_ac if dc4 == 3 else None)
                if g == 0:
                    ctl.wait("tensor", s_ac)

        for dst, src in [(prm[:, :], p_vec), (acoef[:, :], p_alog), (cw[:, :], p_cw),
                         (fb1[:, :], p_fb1), (eb1[:, :], p_eb1),
                         (idn[:, :], ident)]:
            ctl.dma("sync", dst, src, s_ld)
        ctl.op("vector", lambda E: E.memset(onesb[:, :], 1.0), s_ve)
        ctl.dma("sync", ws[:, 0:768].rearrange("p (kc r) -> p kc r", r=96),
                w_xp.rearrange("(kc p) r -> p kc r", p=128), s_ld)
        ctl.dma("sync", ws[:, 960:1024].rearrange("p (kc e) -> p kc e", e=NEXP),
                w_g.rearrange("(kc p) e -> p kc e", p=128), s_ld)
        ctl.wait("scalar", s_ld)
        ctl.op("scalar", lambda E: E.activation(out=xpw[:, :], in_=ws[:, 0:768],
                                                func=AF.Identity))
        ctl.op("scalar", lambda E: E.activation(out=gwb[:, :], in_=ws[:, 960:1024],
                                                func=AF.Identity), s_ac)
        ctl.wait("sync", s_ac)
        ctl.dma("sync", ws[0:RDT, 0:ED], w_dt, s_ld)
        ctl.wait("scalar", s_ld)
        ctl.op("scalar", lambda E: E.activation(out=dtw[:, :], in_=ws[0:RDT, 0:ED],
                                                func=AF.Identity), s_ac)
        ctl.op("scalar", lambda E: E.activation(out=acoef[:, :], in_=acoef[:, :],
                                                func=AF.Exp), s_ac)
        ctl.wait("sync", s_ac)
        ctl.dma("sync", ws[0:NEXP, 0:D], p_b2r, s_ld)
        ctl.wait("scalar", s_ld)
        ctl.op("scalar", lambda E: E.activation(out=b2r[:, :], in_=ws[0:NEXP, 0:D],
                                                func=AF.Identity), s_ac)
        ctl.wait("vector", s_ac)
        ctl.op("vector", lambda E: E.tensor_scalar(out=acoef[:, :], in0=acoef[:, :],
                                                   scalar1=-1.0, scalar2=None,
                                                   op0=ALU.mult))
        ctl.op("vector", lambda E: E.memset(onescol[:, :], 1.0))
        ctl.op("vector", lambda E: E.memset(cst[:, 0:1], EPS))
        ctl.op("vector", lambda E: E.memset(cst[:, 1:2], 1.0))
        ctl.op("vector", lambda E: E.memset(hstate[:, :], 0.0))
        ctl.op("vector", lambda E: E.memset(halo[:, :], 0.0), s_ve)

        jobs = []
        for k in range(8):
            jobs.append((w_in[k * 128:(k + 1) * 128, :],
                         d_wbf_in[k * 128:(k + 1) * 128, :], 2048))
        for k in range(4):
            jobs.append((w_op.rearrange("(a p) d -> p a d", p=128)
                         [:, 2 * k:2 * k + 2, :],
                         d_wbf_op.rearrange("(a p) d -> p a d", p=128)
                         [:, 2 * k:2 * k + 2, :], 2048))
        for k in range(8):
            for h in range(2):
                jobs.append((w_f1[k * 128:(k + 1) * 128, h * 2048:(h + 1) * 2048],
                             d_wbf_f1[k * 128:(k + 1) * 128, h * 2048:(h + 1) * 2048],
                             2048))
        for k in range(16):
            jobs.append((w_f2.rearrange("(a p) d -> p a d", p=128)
                         [:, 2 * k:2 * k + 2, :],
                         d_wbf_f2.rearrange("(a p) d -> p a d", p=128)
                         [:, 2 * k:2 * k + 2, :], 2048))
        w_e1f = w_e1.rearrange("e d h -> (e d) h")
        w_e2f = w_e2.rearrange("e h d -> (e h) d")
        for k in range(NEXP * D // 128):
            jobs.append((w_e1f[k * 128:(k + 1) * 128, :],
                         d_wbf_e1[k * 128:(k + 1) * 128, :], 2048))
        for k in range(NEXP * HEXP // 256):
            jobs.append((w_e2f.rearrange("(a p) d -> p a d", p=128)
                         [:, 2 * k:2 * k + 2, :],
                         d_wbf_e2.rearrange("(a p) d -> p a d", p=128)
                         [:, 2 * k:2 * k + 2, :], 2048))

        s0f = [ws[:, 0:1024].bitcast(U8).bitcast(F32) if False else ws[:, :],
               carve(32 * KB, F32, 2048), carve(40 * KB, F32, 2048)]
        s0f[0] = carve(48 * KB, F32, 2048)
        s0cb = [wb[:, :], carve(56 * KB, BF16, 2048), carve(60 * KB, BF16, 2048)]
        ND = 3
        ld_aft = [0] * len(jobs)
        cast_aft = [0] * len(jobs)
        st_aft = [0] * len(jobs)
        cast_sem = [None] * len(jobs)
        for j, (jsrc, jdst, ncols) in enumerate(jobs):
            p = j % ND
            if j >= ND:
                ctl.wait("sync", s_st, st_aft[j - ND])
            if len(jsrc.shape) == 3:
                ctl.dma("sync", s0f[p].rearrange("p (a d) -> p a d", a=2), jsrc, s_ld)
            else:
                ctl.dma("sync", s0f[p][:, 0:ncols], jsrc, s_ld)
            ld_aft[j] = s_ld.n
            ceng, csem = ("scalar", s_ac) if j % 2 == 0 else ("vector", s_ve)
            cast_sem[j] = csem
            ctl.wait(ceng, s_ld, ld_aft[j])
            if j >= ND:
                ctl.wait(ceng, s_st, st_aft[j - ND])
            if ceng == "scalar":
                ctl.op("scalar", lambda E, p=p, n=ncols: E.activation(
                    out=s0cb[p][:, 0:n], in_=s0f[p][:, 0:n], func=AF.Identity), s_ac)
            else:
                ctl.op("vector", lambda E, p=p, n=ncols: E.tensor_copy(
                    out=s0cb[p][:, 0:n], in_=s0f[p][:, 0:n]), s_ve)
            cast_aft[j] = csem.n
            ctl.wait("gpsimd", csem, cast_aft[j])
            if len(jdst.shape) == 3:
                ctl.dma("gpsimd", jdst, s0cb[p].rearrange("p (a d) -> p a d", a=2), s_st)
            else:
                ctl.dma("gpsimd", jdst[:, 0:ncols], s0cb[p][:, 0:ncols], s_st)
            st_aft[j] = s_st.n
        ctl.wait("sync", s_st)
        barrier()

        for kc in range(8):
            ctl.dma("sync", winp[:, kc * 2048:(kc + 1) * 2048],
                    d_wbf_in[kc * 128:(kc + 1) * 128, :], s_ld)
        barrier()

        bank_free = [s_ac.n, s_ac.n]
        for tc in range(ntc):
            for tb in range(TC // 128):
                load_transpose(lambda dc: ecc(Xb, dc), tb,
                               (lambda E, tc=tc, tb=tb:
                                x_in[tc * TC + tb * 128: tc * TC + (tb + 1) * 128, :]),
                               tb % 2)
            ctl.wait("vector", s_ac)
            ctl.op("vector", lambda E: E.tensor_tensor(
                out=T0[:, :], in0=Xb[:, :], in1=Xb[:, :], op=ALU.mult), s_ve)
            colsum(Xb, 0)
            colsum(T0, TC)
            stat_rows()
            bcast_row(mb_[:, :], 0)
            bcast_row(sb_[:, :], TC)
            ctl.wait("vector", s_ac)
            for ec in range(8):
                ctl.op("vector", lambda E, ec=ec: E.tensor_tensor(
                    out=ecc(T0, ec), in0=ecc(Xb, ec), in1=mb_[:, :], op=ALU.subtract))
                ctl.op("vector", lambda E, ec=ec: E.tensor_tensor(
                    out=ecc(T0, ec), in0=ecc(T0, ec), in1=sb_[:, :], op=ALU.mult))
                ctl.op("vector", lambda E, ec=ec: E.tensor_scalar(
                    out=ecc(T1, ec), in0=ecc(T0, ec),
                    scalar1=vecp(0)[:, ec:ec + 1], scalar2=vecp(1)[:, ec:ec + 1],
                    op0=ALU.mult, op1=ALU.add))
            ctl.op("vector", lambda E: E.tensor_tensor(
                out=T0[:, :], in0=T1[:, :], in1=T1[:, :], op=ALU.mult), s_ve)
            colsum(T0, 0)
            rms_row()
            bcast_row(rb_[:, :], 0)
            ctl.wait("vector", s_ac)
            for ec in range(8):
                ctl.op("vector", lambda E, ec=ec: E.scalar_tensor_tensor(
                    out=ecc(hm, ec), in0=rb_[:, :], scalar=vecp(4)[:, ec:ec + 1],
                    in1=ecc(T1, ec), op0=ALU.mult, op1=ALU.mult),
                    s_ve if ec == 7 else None)
            if dump is not None and dump[0] == "hm" and tc == 0:
                ctl.wait("vector", s_ve)
                ctl.op("vector", lambda E: E.tensor_copy(
                    out=dA[:, 0:8 * TC], in_=hm[:, :]), s_ve)
                ctl.wait("gpsimd", s_ve)
                ctl.dma("gpsimd", o_dbg, dA[:, 0:8 * TC], s_st)

            ctl.wait("tensor", s_ve)
            for ec in range(8):
                bk = banks[ec % 2]
                ctl.wait("tensor", s_ac, bank_free[ec % 2])
                for kc in range(8):
                    ctl.op("tensor", lambda E, bk=bk, ec=ec, kc=kc: E.matmul(
                        bk[:, 0:TC],
                        winp[:, kc * 2048 + ec * 128: kc * 2048 + (ec + 1) * 128],
                        ecc(hm, kc), start=(kc == 0), stop=(kc == 7)),
                        s_pe if kc == 7 else None)
                ctl.wait("scalar", s_pe)
                ctl.wait("scalar", s_ve)
                ctl.op("scalar", lambda E, bk=bk: E.activation(
                    out=xi[:, 4:4 + TC], in_=bk[:, 0:TC], func=AF.Identity), s_ac)
                bank_free[ec % 2] = s_ac.n
                ctl.wait("vector", s_ac)
                ctl.op("vector", lambda E, ec=ec: E.tensor_copy(
                    out=xi[:, 0:4], in_=halo[:, ec * 4:(ec + 1) * 4]))
                ctl.op("vector", lambda E, ec=ec: E.tensor_scalar(
                    out=t1f[:, :], in0=xi[:, 1:1 + TC],
                    scalar1=cw[:, ec * 4:ec * 4 + 1], scalar2=vecp(5)[:, ec:ec + 1],
                    op0=ALU.mult, op1=ALU.add))
                for k in range(1, 4):
                    ctl.op("vector", lambda E, ec=ec, k=k: E.scalar_tensor_tensor(
                        out=t1f[:, :], in0=xi[:, 1 + k:1 + k + TC],
                        scalar=cw[:, ec * 4 + k:ec * 4 + k + 1], in1=t1f[:, :],
                        op0=ALU.mult, op1=ALU.add))
                ctl.op("vector", lambda E, ec=ec: E.tensor_copy(
                    out=halo[:, ec * 4:(ec + 1) * 4], in_=xi[:, TC:TC + 4]), s_ve)
                ctl.wait("scalar", s_ve)
                ctl.op("scalar", lambda E, ec=ec: E.activation(
                    out=ecc(xc, ec), in_=t1f[:, :], func=AF.Silu), s_ac)
            for ec in range(8):
                bk = banks[ec % 2]
                ctl.wait("tensor", s_ac, bank_free[ec % 2])
                for kc in range(8):
                    ctl.op("tensor", lambda E, bk=bk, ec=ec, kc=kc: E.matmul(
                        bk[:, 0:TC],
                        winp[:, kc * 2048 + 1024 + ec * 128:
                             kc * 2048 + 1024 + (ec + 1) * 128],
                        ecc(hm, kc), start=(kc == 0), stop=(kc == 7)),
                        s_pe if kc == 7 else None)
                ctl.wait("scalar", s_pe)
                ctl.op("scalar", lambda E, bk=bk, ec=ec: E.activation(
                    out=ecc(sz, ec), in_=bk[:, 0:TC], func=AF.Silu), s_ac)
                bank_free[ec % 2] = s_ac.n
            if dump is not None and dump[0] == "xc" and tc == 0:
                ctl.wait("vector", s_ac)
                ctl.op("vector", lambda E: E.tensor_copy(out=dA[:, 0:8 * TC],
                                                         in_=xc[:, :]), s_ve)
                ctl.wait("gpsimd", s_ve)
                ctl.dma("gpsimd", o_dbg, dA[:, 0:8 * TC], s_st)

            ctl.wait("tensor", s_ac)
            ctl.wait("tensor", s_ve)
            for kc in range(8):
                ctl.op("tensor", lambda E, kc=kc: E.matmul(
                    p_st[0:96, 0:TC], xpw[:, kc * 96:(kc + 1) * 96], ecc(xc, kc),
                    start=(kc == 0), stop=(kc == 7)), s_pe if kc == 7 else None)
            ctl.wait("vector", s_pe)
            ctl.op("vector", lambda E: E.tensor_copy(
                out=dbcb[0:96, :], in_=p_st[0:96, 0:TC]))
            ctl.op("vector", lambda E: E.tensor_copy(
                out=bcf[0:32, :], in_=p_st[64:96, 0:TC]), s_ve)
            ctl.wait("gpsimd", s_ve)
            ctl.dma("gpsimd", d_bc[:, tc * TC:(tc + 1) * TC], bcf[0:32, :], s_st)
            ctl.wait("sync", s_st)
            for half, dst in ((0, Bb), (1, Cb)):
                ctl.dma("sync", dst[:, :].rearrange("p (n t) -> p n t", n=NST),
                        d_bc[half * NST:(half + 1) * NST, tc * TC:(tc + 1) * TC]
                        .unsqueeze(0).partition_broadcast(128), s_ld)
            if dump is not None and dump[0] == "dbc" and tc == 0:
                ctl.wait("vector", s_ve)
                ctl.op("vector", lambda E: E.tensor_copy(out=dA[:, 0:TC],
                                                         in_=dbcb[:, :]), s_ve)
                ctl.wait("gpsimd", s_ve)
                ctl.dma("gpsimd", o_dbg, dA[:, 0:TC], s_st)
            for ec in range(8):
                bk = banks[ec % 2]
                ctl.wait("tensor", s_ac, bank_free[ec % 2])
                ctl.wait("tensor", s_ve)
                ctl.op("tensor", lambda E, bk=bk, ec=ec: E.matmul(
                    bk[:, 0:TC], dtw[:, ec * 128:(ec + 1) * 128], dbcb[0:RDT, :],
                    start=True, stop=True), s_pe)
                ctl.wait("scalar", s_pe)
                ctl.wait("scalar", s_ve)
                ctl.op("scalar", lambda E, bk=bk, ec=ec: E.activation(
                    out=t1f[:, :], in_=bk[:, 0:TC], func=AF.Abs,
                    bias=vecp(6)[:, ec:ec + 1]))
                ctl.op("scalar", lambda E: E.activation(
                    out=t1f[:, :], in_=t1f[:, :], func=AF.Exp, scale=-1.0))
                ctl.op("scalar", lambda E: E.activation(
                    out=t1f[:, :], in_=t1f[:, :], func=AF.Ln, bias=cst[:, 1:2]))
                ctl.op("scalar", lambda E, bk=bk, ec=ec: E.activation(
                    out=t2f[:, :], in_=bk[:, 0:TC], func=AF.Relu,
                    bias=vecp(6)[:, ec:ec + 1]), s_ac)
                bank_free[ec % 2] = s_ac.n
                ctl.wait("vector", s_ac)
                if dump is not None and dump[0] == "dtp" and tc == 0 and ec == 3:
                    ctl.op("vector", lambda E, bk=bk: E.tensor_copy(
                        out=dA[:, 0:TC], in_=bk[:, 0:TC]))
                    ctl.op("vector", lambda E: E.tensor_copy(
                        out=dA[:, TC:2 * TC], in_=t1f[:, :]))
                    ctl.op("vector", lambda E: E.tensor_copy(
                        out=dA[:, 2 * TC:3 * TC], in_=t2f[:, :]), s_ve)
                    ctl.wait("gpsimd", s_ve)
                    ctl.dma("gpsimd", o_dbg, dA[:, 0:3 * TC], s_st)
                ctl.op("vector", lambda E, ec=ec: E.tensor_tensor(
                    out=ecc(dl, ec), in0=t1f[:, :], in1=t2f[:, :], op=ALU.add),
                    s_ve)
            if dump is not None and dump[0] == "delta" and tc == 0:
                ctl.wait("vector", s_ve)
                ctl.op("vector", lambda E: E.tensor_copy(out=dA[:, 0:8 * TC],
                                                         in_=dl[:, :]), s_ve)
                ctl.wait("gpsimd", s_ve)
                ctl.dma("gpsimd", o_dbg, dA[:, 0:8 * TC], s_st)
                ctl.wait("scalar", s_st)

            ctl.wait("scalar", s_ve)
            ctl.wait("vector", s_ld)
            ctl.wait("scalar", s_ld)
            scan_done = [0, 0]
            for ec in range(8):
                acc_half = [0, 0]
                for half in range(2):
                    ctl.wait("scalar", s_ve, scan_done[half])
                    for n in range(half * 8, half * 8 + 8):
                        ctl.op("scalar", lambda E, ec=ec, n=n: E.activation(
                            out=dA[:, n * TC:(n + 1) * TC], in_=ecc(dl, ec),
                            func=AF.Exp,
                            scale=acoef[:, ec * NST + n:ec * NST + n + 1]),
                            s_ac if n % 8 == 7 else None)
                    acc_half[half] = s_ac.n
                ctl.wait("vector", s_st)
                ctl.op("vector", lambda E, ec=ec: E.tensor_tensor(
                    out=ymt[:, :], in0=ecc(dl, ec), in1=ecc(xc, ec), op=ALU.mult))
                ctl.op("vector", lambda E: E.tensor_tensor(
                    out=BXs[:, :].rearrange("p (n t) -> p n t", n=NST),
                    in0=ymt[:, :].unsqueeze(1).to_broadcast([128, NST, TC]),
                    in1=Bb[:, :].rearrange("p (n t) -> p n t", n=NST),
                    op=ALU.mult))
                for half in range(2):
                    ctl.wait("vector", s_ac, acc_half[half])
                    for n in range(half * 8, half * 8 + 8):
                        ctl.op("vector", lambda E, ec=ec, n=n: E.tensor_tensor_scan(
                            out=Hs[:, n * TC:(n + 1) * TC],
                            data0=dA[:, n * TC:(n + 1) * TC],
                            data1=BXs[:, n * TC:(n + 1) * TC],
                            initial=hstate[:, ec * NST + n:ec * NST + n + 1],
                            op0=ALU.mult, op1=ALU.add),
                            s_ve if n % 8 == 7 else None)
                    scan_done[half] = s_ve.n
                ctl.op("vector", lambda E, ec=ec: E.tensor_copy(
                    out=hstate[:, ec * NST:(ec + 1) * NST],
                    in_=Hs[:, :].rearrange("p (n t) -> p n t", n=NST)[:, :, TC - 1]))
                ctl.op("vector", lambda E: E.tensor_tensor(
                    out=BXs[:, :], in0=Hs[:, :], in1=Cb[:, :], op=ALU.mult))
                ctl.op("vector", lambda E: E.tensor_reduce(
                    out=ysub[:, :],
                    in_=BXs[:, :].rearrange("p (n t) -> p t n", n=NST),
                    axis=mybir.AxisListType.X, op=ALU.add))
                ctl.op("vector", lambda E, ec=ec: E.scalar_tensor_tensor(
                    out=yq[:, :], in0=ecc(xc, ec), scalar=vecp(7)[:, ec:ec + 1],
                    in1=ysub[:, :], op0=ALU.mult, op1=ALU.add))
                ctl.op("vector", lambda E, ec=ec: E.tensor_tensor(
                    out=ymt[:, :], in0=yq[:, :], in1=ecc(sz, ec), op=ALU.mult), s_ve)
                ctl.wait("gpsimd", s_ve)
                ctl.dma("gpsimd", d_ymix[ec * 128:(ec + 1) * 128,
                                         tc * CH:(tc + 1) * CH], ymt[:, 0:CH], s_st)
                ctl.dma("gpsimd", d_ymix[ec * 128:(ec + 1) * 128,
                                         L // 2 + tc * CH: L // 2 + (tc + 1) * CH],
                        ymt[:, CH:2 * CH], s_st)
            barrier()

        if dump is not None and dump[0] == "ymix":
            for ec in range(8):
                ctl.wait("sync", s_ve)
                ctl.dma("sync", wb[:, 0:512], d_ymix[ec * 128:(ec + 1) * 128, 0:512], s_ld)
                ctl.wait("vector", s_ld)
                ctl.op("vector", lambda E: E.tensor_copy(out=ws[:, 0:512],
                                                         in_=wb[:, 0:512]), s_ve)
                ctl.wait("gpsimd", s_ve)
                ctl.dma("gpsimd", o_dbg[ec * 128:(ec + 1) * 128, :], ws[:, 0:512], s_st)

        if not do_post:
            ctl.wait("sync", s_st)
            _replay(nc, ctl)
            return nc

        f_ = carve(0, F32, 8 * OWN)
        s2 = carve(64 * KB, BF16, 8 * OWN)
        h3 = carve(96 * KB, BF16, 8 * OWN)
        u1 = h3
        Z = 128 * KB
        ymx = carve(Z, BF16, 8 * PC)
        rfull = carve(Z + 8 * KB, F32, 8 * PC)
        x2b = carve(Z + 24 * KB, BF16, 8 * PC)
        pt1 = carve(Z + 32 * KB, F32, PC)
        pt2 = carve(Z + 34 * KB, F32, PC)
        pmb = carve(Z + 36 * KB, BF16, PC)
        psb = carve(Z + 37 * KB, BF16, PC)
        h1 = carve(Z, BF16, 4 * OWN)
        w1s = carve(Z + 16 * KB, BF16, 8 * 128)
        w2s = carve(Z + 18 * KB, BF16, 4 * 128)
        hE = carve(Z, BF16, 16 * (OWN // 2))
        e1s = carve(Z + 38 * KB, BF16, 8 * 128)
        e2s = carve(Z + 40 * KB, BF16, 16 * 128)
        gb_ = carve(Z + 32 * KB, BF16, OWN)
        gful = carve(Z + 8 * KB, BF16, OWN)
        glog = carve(Z + 12 * KB, F32, PC)
        gsc = carve(Z + 14 * KB, F32, 64)
        x3full = carve(Z + 8 * KB, F32, 8 * PC)
        ptf = carve(Z + 32 * KB, F32, PC)
        otm2 = [ws[:, 0:1024], wb[:, :].bitcast(F32)]

        def voff(E):
            if not hasattr(E, "_v256"):
                r = E.alloc_register("rho256")
                E.reg_mov(r, (E.partition_id() % 2) * (L // 2))
                E._v256 = E.snap(r, min_val=0, max_val=L // 2)
            return E._v256

        bank_free = [s_ac.n, s_ac.n]
        vb_free = [s_ve.n, s_ve.n]

        def pdump(srcT, wsrc, cols=PC):
            ctl.wait("vector", s_ac)
            ctl.wait("vector", s_pe)
            for m in range(8):
                ctl.op("vector", lambda E, m=m: E.tensor_copy(
                    out=ecc(rfull, m, PC)[:, 0:cols],
                    in_=ecc(srcT, m, wsrc)[:, 0:cols]), s_ve if m == 7 else None)
            ctl.wait("gpsimd", s_ve)
            ctl.dma("gpsimd", o_dbg, rfull[:, :], s_st)
            ctl.wait("vector", s_st)
            ctl.wait("scalar", s_st)
            ctl.wait("tensor", s_st)

        NPC_RUN = 1 if post_until.startswith("s5a_") else NPC
        for c in range(NPC_RUN):
            ctl.wait("sync", s_ve)
            ctl.wait("sync", s_ac)
            ctl.dma_dyn("sync", lambda E, c=c: (
                ymx[:, :].rearrange("p (e t) -> p e t", e=8),
                d_ymix[:, bass.ds(voff(E) + c * PC, PC)]
                .rearrange("(e p) t -> p e t", p=128)), s_ld, tag="ymx")
            for q in range(4):
                load_transpose(lambda dc: ecc(rfull, dc, PC), q,
                               (lambda E, c=c, q=q:
                                x_own[c * PC + q * 128: c * PC + (q + 1) * 128, :]),
                               q % 2)
            if post_until == "s5a_ld" and c == 0:
                ctl.wait("sync", s_ld)
                ctl.wait("sync", s_ac)
                _replay(nc, ctl)
                return nc
            ctl.wait("vector", s_ac)
            for m in range(8):
                ctl.op("vector", lambda E, m=m: E.tensor_copy(
                    out=ecc(x2b, m, PC), in_=ecc(rfull, m, PC)),
                    s_ve if m == 7 else None)
            colsum(x2b, 0, PC)
            ctl.wait("vector", s_pe)
            ctl.op("vector", lambda E: E.tensor_tensor(
                out=x2b[:, :], in0=x2b[:, :], in1=x2b[:, :], op=ALU.mult), s_ve)
            colsum(x2b, PC, PC)
            stat_rows(PC)
            bcast_row(pmb[:, :], 0, PC)
            bcast_row(psb[:, :], PC, PC)
            if post_until == "s5a_stats" and c == 0:
                ctl.wait("scalar", s_ac)
                _replay(nc, ctl)
                return nc
            opbufs = [wb[:, 0:1024], wb[:, 1024:2048]]
            op_ld = {}
            op_pe = {}

            def op_load(i):
                if i >= 2:
                    ctl.wait("sync", s_pe, op_pe.get(i - 2, 0))
                else:
                    ctl.wait("sync", s_pe)
                ctl.dma("sync", opbufs[i % 2],
                        d_wbf_op[(i % 8) * 128:(i % 8 + 1) * 128, :], s_ld)
                op_ld[i] = s_ld.n

            op_load(0)
            op_load(1)
            for m in range(8):
                bk = banks[m % 2]
                ctl.wait("tensor", s_ve, vb_free[m % 2])
                for kc in range(8):
                    i = m * 8 + kc
                    ctl.wait("tensor", s_ld, op_ld[i])
                    ctl.op("tensor", lambda E, bk=bk, m=m, kc=kc, w=opbufs[i % 2]:
                           E.matmul(
                        bk[:, 0:PC], w[:, m * 128:(m + 1) * 128],
                        ecc(ymx, kc, PC), start=(kc == 0), stop=(kc == 7)),
                        s_pe)
                    op_pe[i] = s_pe.n
                    if i + 2 < 64:
                        op_load(i + 2)
                ctl.wait("vector", s_pe)
                ctl.op("vector", lambda E, m=m: E.tensor_tensor(
                    out=pt1[:, :], in0=ecc(rfull, m, PC), in1=pmb[:, :],
                    op=ALU.subtract))
                ctl.op("vector", lambda E: E.tensor_tensor(
                    out=pt1[:, :], in0=pt1[:, :], in1=psb[:, :], op=ALU.mult))
                ctl.op("vector", lambda E, m=m: E.tensor_scalar(
                    out=pt1[:, :], in0=pt1[:, :], scalar1=vecp(0)[:, m:m + 1],
                    scalar2=vecp(1)[:, m:m + 1], op0=ALU.mult, op1=ALU.add))
                ctl.op("vector", lambda E, m=m, bk=bk: E.tensor_tensor(
                    out=pt2[:, :], in0=bk[:, 0:PC], in1=ecc(rfull, m, PC),
                    op=ALU.add), s_ve)
                vb_free[m % 2] = s_ve.n
                ctl.op("vector", lambda E, m=m: E.tensor_tensor(
                    out=ecc(x2b, m, PC), in0=pt2[:, :], in1=pt1[:, :], op=ALU.add))
                ctl.op("vector", lambda E, m=m, c=c: E.tensor_tensor(
                    out=ecc(s2, m, OWN)[:, c * PC:(c + 1) * PC],
                    in0=ecc(x2b, m, PC), in1=ecc(rfull, m, PC), op=ALU.add), s_ve)
            if post_until == "s5a_op" and c == 0:
                ctl.wait("vector", s_ve)
                _replay(nc, ctl)
                return nc
            ctl.op("vector", lambda E: E.tensor_tensor(
                out=ymx[:, :], in0=x2b[:, :], in1=x2b[:, :], op=ALU.mult), s_ve)
            colsum(x2b, 0, PC)
            colsum(ymx, PC, PC)
            stat_rows(PC)
            bcast_row(pmb[:, :], 0, PC)
            bcast_row(psb[:, :], PC, PC)
            ctl.wait("vector", s_ac)
            for m in range(8):
                ctl.op("vector", lambda E, m=m: E.tensor_tensor(
                    out=pt1[:, :].bitcast(BF16)[:, 0:PC], in0=ecc(x2b, m, PC),
                    in1=pmb[:, :], op=ALU.subtract))
                ctl.op("vector", lambda E: E.tensor_tensor(
                    out=pt1[:, :].bitcast(BF16)[:, 0:PC],
                    in0=pt1[:, :].bitcast(BF16)[:, 0:PC], in1=psb[:, :], op=ALU.mult))
                ctl.op("vector", lambda E, m=m, c=c: E.tensor_scalar(
                    out=ecc(h3, m, OWN)[:, c * PC:(c + 1) * PC],
                    in0=pt1[:, :].bitcast(BF16)[:, 0:PC],
                    scalar1=vecp(2)[:, m:m + 1], scalar2=vecp(3)[:, m:m + 1],
                    op0=ALU.mult, op1=ALU.add), s_ve)
            if dump is not None and c == 0 and dump[0] in ("x2", "h3", "s2"):
                dsrc = {"x2": (x2b, PC), "h3": (h3, OWN), "s2": (s2, OWN)}[dump[0]]
                pdump(dsrc[0], dsrc[1])

        barrier()
        if post_until == "s5a":
            ctl.wait("sync", s_st)
            _replay(nc, ctl)
            return nc
        bank_free = [s_ac.n, s_ac.n]
        vb_free = [s_ve.n, s_ve.n]

        for j in range(8):
            for jm in range(4):
                hidx = j * 512 + jm * 128
                ctl.wait("sync", s_pe)
                ctl.dma("sync", w1s[:, :].rearrange("p (kc h) -> p kc h", h=128),
                        d_wbf_f1.rearrange("(kc p) h -> p kc h", p=128)
                        [:, :, hidx:hidx + 128], s_ld)
                ctl.wait("tensor", s_ld)
                for tt in range(4):
                    bk = banks[tt % 2]
                    ctl.wait("tensor", s_ac, bank_free[tt % 2])
                    for kc in range(8):
                        ctl.op("tensor", lambda E, bk=bk, kc=kc, tt=tt: E.matmul(
                            bk[:, 0:PC], w1s[:, kc * 128:(kc + 1) * 128],
                            ecc(h3, kc, OWN)[:, tt * PC:(tt + 1) * PC],
                            start=(kc == 0), stop=(kc == 7)),
                            s_pe if kc == 7 else None)
                    ctl.wait("scalar", s_pe)
                    ctl.op("scalar", lambda E, bk=bk, j=j, jm=jm, tt=tt: E.activation(
                        out=ecc(h1, jm, OWN)[:, tt * PC:(tt + 1) * PC],
                        in_=bk[:, 0:PC], func=AF.Relu,
                        bias=fb1[:, j * 4 + jm:j * 4 + jm + 1]), s_ac)
                    bank_free[tt % 2] = s_ac.n
            for m in range(8):
                ctl.wait("sync", s_pe)
                ctl.dma("sync", w2s[:, :].rearrange("p (k d) -> p k d", d=128),
                        d_wbf_f2[j * 512:(j + 1) * 512, :]
                        .rearrange("(k p) d -> p k d", p=128)
                        [:, :, m * 128:(m + 1) * 128], s_ld)
                ctl.wait("tensor", s_ld)
                ctl.wait("tensor", s_ac)
                for tt in range(4):
                    bk = banks[tt % 2]
                    ctl.wait("tensor", s_ve, vb_free[tt % 2])
                    for k in range(4):
                        ctl.op("tensor", lambda E, bk=bk, k=k, tt=tt: E.matmul(
                            bk[:, 0:PC], w2s[:, k * 128:(k + 1) * 128],
                            ecc(h1, k, OWN)[:, tt * PC:(tt + 1) * PC],
                            start=(k == 0), stop=(k == 3)), s_pe if k == 3 else None)
                    ctl.wait("vector", s_pe)
                    if j == 0:
                        ctl.op("vector", lambda E, bk=bk, m=m, tt=tt: E.tensor_scalar(
                            out=ecc(f_, m, OWN)[:, tt * PC:(tt + 1) * PC],
                            in0=bk[:, 0:PC], scalar1=1.0, scalar2=None,
                            op0=ALU.mult), s_ve)
                    else:
                        ctl.op("vector", lambda E, bk=bk, m=m, tt=tt:
                               E.scalar_tensor_tensor(
                            out=ecc(f_, m, OWN)[:, tt * PC:(tt + 1) * PC],
                            in0=bk[:, 0:PC], scalar=1.0,
                            in1=ecc(f_, m, OWN)[:, tt * PC:(tt + 1) * PC],
                            op0=ALU.mult, op1=ALU.add), s_ve)
                    vb_free[tt % 2] = s_ve.n
        if dump is not None and dump[0] == "fffn":
            pdump(f_, OWN)
        barrier()
        if post_until == "ffn":
            ctl.wait("sync", s_st)
            _replay(nc, ctl)
            return nc

        for m in range(8):
            ctl.op("vector", lambda E, m=m: E.scalar_tensor_tensor(
                out=ecc(f_, m, OWN), in0=ecc(f_, m, OWN),
                scalar=vecp(8)[:, m:m + 1], in1=ecc(s2, m, OWN),
                op0=ALU.add, op1=ALU.add), s_ve)
        for tt in range(4):
            for m in range(8):
                ctl.op("vector", lambda E, m=m, tt=tt: E.tensor_copy(
                    out=ecc(x2b, m, PC), in_=ecc(f_, m, OWN)[:, tt * PC:(tt + 1) * PC]))
            ctl.op("vector", lambda E: E.tensor_tensor(
                out=ymx[:, :], in0=x2b[:, :], in1=x2b[:, :], op=ALU.mult), s_ve)
            colsum(x2b, 0, PC)
            colsum(ymx, PC, PC)
            stat_rows(PC)
            bcast_row(pmb[:, :], 0, PC)
            bcast_row(psb[:, :], PC, PC)
            ctl.wait("vector", s_ac)
            for m in range(8):
                ctl.op("vector", lambda E, m=m, tt=tt: E.tensor_tensor(
                    out=pt1[:, :].bitcast(BF16)[:, 0:PC], in0=ecc(x2b, m, PC),
                    in1=pmb[:, :], op=ALU.subtract))
                ctl.op("vector", lambda E, m=m, tt=tt: E.tensor_tensor(
                    out=ecc(u1, m, OWN)[:, tt * PC:(tt + 1) * PC],
                    in0=pt1[:, :].bitcast(BF16)[:, 0:PC], in1=psb[:, :],
                    op=ALU.mult), s_ve)
        if dump is not None and dump[0] == "u1":
            pdump(u1, OWN)
        barrier()
        if post_until == "u1":
            ctl.wait("sync", s_st)
            _replay(nc, ctl)
            return nc

        gw128 = carve(Z + 32 * KB, F32, 128)
        gb_free = [s_ac.n, s_ac.n]
        for tt in range(4):
            for sub in range(4):
                bk = banks[sub % 2]
                ctl.wait("tensor", s_ac, gb_free[sub % 2])
                ctl.wait("tensor", s_ve)
                for kc in range(8):
                    ctl.op("tensor", lambda E, bk=bk, kc=kc, tt=tt, sub=sub: E.matmul(
                        bk[:, 0:NEXP],
                        ecc(u1, kc, OWN)[:, tt * PC + sub * 128: tt * PC + (sub + 1) * 128],
                        gwb[:, kc * NEXP:(kc + 1) * NEXP],
                        start=(kc == 0), stop=(kc == 7)), s_pe if kc == 7 else None)
                ctl.wait("scalar", s_pe)
                ctl.op("scalar", lambda E, bk=bk: E.activation(
                    out=gsc[:, 0:8], in_=bk[:, 0:NEXP], func=AF.Identity), s_ac)
                gb_free[sub % 2] = s_ac.n
                ctl.wait("vector", s_ac)
                if dump is not None and dump[0] == "glx" and tt == 0 and sub == 1:
                    ctl.op("vector", lambda E: E.tensor_copy(
                        out=gw128[:, 16:24], in_=gsc[:, 0:8]), s_ve)
                    ctl.wait("gpsimd", s_ve)
                    ctl.dma("gpsimd", o_dbg, gw128[:, 16:24], s_st)
                    ctl.wait("vector", s_st)
                    ctl.wait("scalar", s_st)
                    ctl.wait("tensor", s_st)

                def vc(fn):
                    ctl.op("vector", fn, s_ve)
                    ctl.wait("vector", s_ve)
                vc(lambda E: E.max(out=gsc[:, 8:16], in_=gsc[:, 0:8]))
                if dump is not None and dump[0] == "gm8" and tt == 0 and sub == 1:
                    ctl.op("vector", lambda E: E.tensor_copy(
                        out=gw128[:, 16:24], in_=gsc[:, 0:8]))
                    ctl.op("vector", lambda E: E.tensor_copy(
                        out=gw128[:, 24:32], in_=gsc[:, 8:16]), s_ve)
                    ctl.wait("gpsimd", s_ve)
                    ctl.dma("gpsimd", o_dbg, gw128[:, 16:32], s_st)
                    ctl.wait("vector", s_st)
                    ctl.wait("scalar", s_st)
                    ctl.wait("tensor", s_st)
                vc(lambda E: E.tensor_scalar(
                    out=gsc[:, 16:17], in0=gsc[:, 8:9], scalar1=-1.0, scalar2=None,
                    op0=ALU.mult))
                ctl.wait("scalar", s_ve)
                ctl.op("scalar", lambda E: E.activation(
                    out=gsc[:, 17:25], in_=gsc[:, 0:8], func=AF.Exp,
                    bias=gsc[:, 16:17]), s_ac)
                ctl.wait("vector", s_ac)
                vc(lambda E: E.tensor_scalar(
                    out=gsc[:, 25:33], in0=gsc[:, 0:8], scalar1=gsc[:, 9:10],
                    scalar2=None, op0=ALU.is_ge))
                vc(lambda E: E.tensor_tensor(
                    out=gsc[:, 17:25], in0=gsc[:, 17:25], in1=gsc[:, 25:33],
                    op=ALU.mult))
                vc(lambda E: E.tensor_reduce(
                    out=gsc[:, 33:34], in_=gsc[:, 17:25],
                    axis=mybir.AxisListType.X, op=ALU.add))
                vc(lambda E: E.reciprocal(
                    out=gsc[:, 34:35], in_=gsc[:, 33:34]))
                vc(lambda E: E.tensor_scalar(
                    out=gw128[:, 0:8], in0=gsc[:, 17:25], scalar1=gsc[:, 34:35],
                    scalar2=None, op0=ALU.mult))
                ctl.wait("tensor", s_ve)
                ctl.wait("tensor", s_ac)
                ctl.op("tensor", lambda E: E.transpose(
                    p_tr[:, 0:128], gw128[:, :], idn[:, :]), s_pe)
                ctl.wait("scalar", s_pe)
                ctl.op("scalar", lambda E, tt=tt, sub=sub: E.activation(
                    out=gful[0:NEXP, tt * PC + sub * 128: tt * PC + (sub + 1) * 128],
                    in_=p_tr[0:NEXP, 0:128], func=AF.Identity), s_ac)
                barrier()
                ctl.wait("gpsimd", s_ac)
        ctl.dma("gpsimd", d_g[:, :], gful[0:NEXP, 0:OWN], s_st)
        if dump is not None and dump[0] == "gw":
            ctl.wait("vector", s_ac)
            ctl.op("vector", lambda E: E.tensor_copy(
                out=x2b[:, :].bitcast(F32)[0:NEXP, 0:OWN], in_=gful[0:NEXP, 0:OWN]), s_ve)
            ctl.wait("gpsimd", s_ve)
            ctl.dma("gpsimd", o_dbg, x2b[:, :].bitcast(F32)[0:NEXP, 0:OWN], s_st)
            ctl.wait("vector", s_st)
        barrier()
        if post_until == "gate":
            ctl.wait("sync", s_st)
            _replay(nc, ctl)
            return nc
        bank_free = [s_ac.n, s_ac.n]
        vb_free = [s_ve.n, s_ve.n]

        e1sB = carve(Z + 36 * KB, BF16, 8 * 128)
        e1sC = wb[:, 0:1024]
        e2sB = ws[:, :].bitcast(BF16)
        e1bufs = [e1s, e1sB, e1sC]
        e2bufs = [e2s, e2sB]
        e1_ld = {}
        e1_pe = {}
        e2_ld = {}
        e2_pe = {}

        def e1_load(i):
            q, r = divmod(i, 16)
            ee, hh2 = divmod(q, 2)
            if i >= 3:
                ctl.wait("sync", s_pe, e1_pe.get(i - 3, 0))
            ctl.dma("sync", e1bufs[i % 3][:, :].rearrange("p (kc h) -> p kc h", h=128),
                    d_wbf_e1[ee * D:(ee + 1) * D, :]
                    .rearrange("(kc p) h -> p kc h", p=128)
                    [:, :, r * 128:(r + 1) * 128], s_ld)
            e1_ld[i] = s_ld.n

        def e2_load(i):
            q, m2 = divmod(i, 8)
            ee, hh2 = divmod(q, 2)
            if i >= 2:
                ctl.wait("sync", s_pe, e2_pe.get(i - 2, 0))
            ctl.dma("sync", e2bufs[i % 2][:, :].rearrange("p (k d) -> p k d", d=128),
                    d_wbf_e2[ee * HEXP:(ee + 1) * HEXP, :]
                    .rearrange("(k p) d -> p k d", p=128)
                    [:, :, m2 * 128:(m2 + 1) * 128], s_ld)
            e2_ld[i] = s_ld.n

        for _i in range(3):
            e1_load(_i)
        e2_load(0)
        e2_load(1)
        for e in range(NEXP):
            ctl.wait("sync", s_ve)
            ctl.dma("sync", gb_[:, :],
                    d_g[e:e + 1, :].partition_broadcast(128), s_ld)
            for hh in range(2):
                t0 = hh * (OWN // 2)
                for jm in range(16):
                    i1 = (e * 2 + hh) * 16 + jm
                    w1c = e1bufs[i1 % 3]
                    ctl.wait("tensor", s_ld, e1_ld[i1])
                    for tt in range(2):
                        bk = banks[tt % 2]
                        ctl.wait("tensor", s_ac, bank_free[tt % 2])
                        for kc in range(8):
                            ctl.op("tensor", lambda E, bk=bk, kc=kc, tt=tt, t0=t0,
                                   w=w1c: E.matmul(
                                bk[:, 0:PC], w[:, kc * 128:(kc + 1) * 128],
                                ecc(u1, kc, OWN)[:, t0 + tt * PC: t0 + (tt + 1) * PC],
                                start=(kc == 0), stop=(kc == 7)),
                                s_pe if kc == 7 else None)
                        ctl.wait("scalar", s_pe)
                        ctl.op("scalar", lambda E, bk=bk, e=e, jm=jm, tt=tt:
                               E.activation(
                            out=ecc(hE, jm, OWN // 2)[:, tt * PC:(tt + 1) * PC],
                            in_=bk[:, 0:PC], func=AF.Relu,
                            bias=eb1[:, e * 16 + jm:e * 16 + jm + 1]), s_ac)
                        bank_free[tt % 2] = s_ac.n
                    e1_pe[i1] = s_pe.n
                    if i1 + 3 < 256:
                        e1_load(i1 + 3)
                ctl.wait("vector", s_ac)
                ctl.wait("vector", s_ld)
                for jm in range(16):
                    ctl.op("vector", lambda E, jm=jm, t0=t0: E.tensor_tensor(
                        out=ecc(hE, jm, OWN // 2), in0=ecc(hE, jm, OWN // 2),
                        in1=gb_[:, t0:t0 + OWN // 2], op=ALU.mult),
                        s_ve if jm == 15 else None)
                for m in range(8):
                    i2 = (e * 2 + hh) * 8 + m
                    w2c = e2bufs[i2 % 2]
                    ctl.wait("tensor", s_ld, e2_ld[i2])
                    ctl.wait("tensor", s_ve)
                    for tt in range(2):
                        bk = banks[tt % 2]
                        ctl.wait("tensor", s_ve, vb_free[tt % 2])
                        if e == 0:
                            ctl.op("tensor", lambda E, bk=bk, m=m, tt=tt, t0=t0:
                                   E.matmul(
                                bk[:, 0:PC], b2r[:, m * 128:(m + 1) * 128],
                                gful[0:NEXP, t0 + tt * PC: t0 + (tt + 1) * PC],
                                start=True, stop=False))
                        for k in range(16):
                            ctl.op("tensor", lambda E, bk=bk, k=k, tt=tt, e=e,
                                   w=w2c: E.matmul(
                                bk[:, 0:PC], w[:, k * 128:(k + 1) * 128],
                                ecc(hE, k, OWN // 2)[:, tt * PC:(tt + 1) * PC],
                                start=(k == 0 and e != 0), stop=(k == 15)),
                                s_pe if k == 15 else None)
                        ctl.wait("vector", s_pe)
                        if e == 0:
                            ctl.op("vector", lambda E, bk=bk, m=m, tt=tt, t0=t0:
                                   E.tensor_scalar(
                                out=ecc(f_, m, OWN)[:, t0 + tt * PC: t0 + (tt + 1) * PC],
                                in0=bk[:, 0:PC], scalar1=1.0, scalar2=None,
                                op0=ALU.mult), s_ve)
                        else:
                            ctl.op("vector", lambda E, bk=bk, m=m, tt=tt, t0=t0:
                                   E.scalar_tensor_tensor(
                                out=ecc(f_, m, OWN)[:, t0 + tt * PC: t0 + (tt + 1) * PC],
                                in0=bk[:, 0:PC], scalar=1.0,
                                in1=ecc(f_, m, OWN)[:, t0 + tt * PC: t0 + (tt + 1) * PC],
                                op0=ALU.mult, op1=ALU.add), s_ve)
                        vb_free[tt % 2] = s_ve.n
                    e2_pe[i2] = s_pe.n
                    if i2 + 2 < 128:
                        e2_load(i2 + 2)
        barrier()
        if post_until == "moe":
            ctl.wait("sync", s_st)
            _replay(nc, ctl)
            return nc

        st_done = [s_st.n, s_st.n]
        for tt in range(4):
            ctl.wait("vector", s_pe)
            for m in range(8):
                ctl.op("vector", lambda E, m=m, tt=tt: E.tensor_tensor(
                    out=ecc(x3full, m, PC),
                    in0=ecc(u1, m, OWN)[:, tt * PC:(tt + 1) * PC],
                    in1=ecc(f_, m, OWN)[:, tt * PC:(tt + 1) * PC], op=ALU.add))
            for m in range(8):
                ctl.op("vector", lambda E, m=m: E.tensor_copy(
                    out=ecc(ymx, m, PC), in_=ecc(x3full, m, PC)))
            ctl.op("vector", lambda E: E.tensor_tensor(
                out=x2b[:, :], in0=ymx[:, :], in1=ymx[:, :], op=ALU.mult), s_ve)
            colsum(ymx, 0, PC)
            colsum(x2b, PC, PC)
            stat_rows(PC)
            bcast_row(pmb[:, :], 0, PC)
            bcast_row(psb[:, :], PC, PC)
            ctl.wait("vector", s_ac)
            for m in range(8):
                ctl.op("vector", lambda E, m=m: E.tensor_tensor(
                    out=ptf[:, :], in0=ecc(x3full, m, PC), in1=pmb[:, :],
                    op=ALU.subtract))
                ctl.op("vector", lambda E, m=m: E.tensor_tensor(
                    out=ecc(x3full, m, PC), in0=ptf[:, :], in1=psb[:, :],
                    op=ALU.mult), s_ve)
            if post_until == "s5f_ln" and tt == 0:
                ctl.wait("vector", s_ve)
                _replay(nc, ctl)
                return nc
            for tb in range(4):
                if post_until.startswith("s5f_p") and tt == 0 and tb == 1:
                    nparts = int(post_until[5:])
                    ctl.wait("tensor", s_ve)
                    for g in range(2):
                        for dc4 in range(4):
                            dc = g * 4 + dc4
                            ctl.op("tensor", lambda E, dc=dc, dc4=dc4: E.transpose(
                                p_tr[:, dc4 * 128:(dc4 + 1) * 128],
                                ecc(x3full, dc, PC)[:, 128:256],
                                idn[:, :]), s_pe if dc4 == 3 else None)
                        if nparts >= 2:
                            ctl.wait("scalar", s_pe)
                            for dc4 in range(4):
                                dc = g * 4 + dc4
                                ctl.op("scalar", lambda E, dc=dc, dc4=dc4: E.activation(
                                    out=otm2[1][:, dc * 128:(dc + 1) * 128],
                                    in_=p_tr[:, dc4 * 128:(dc4 + 1) * 128],
                                    func=AF.Identity), s_ac if dc4 == 3 else None)
                            if g == 0:
                                ctl.wait("tensor", s_ac)
                    if nparts >= 3:
                        ctl.wait("gpsimd", s_ac)
                        ctl.dma("gpsimd", o_out[128:256, :], otm2[1][:, :], s_st)
                    ctl.wait("sync", s_st)
                    ctl.wait("sync", s_pe)
                    _replay(nc, ctl)
                    return nc
                p = tb % 2
                ctl.wait("tensor", s_ve)
                ctl.wait("tensor", s_ac)
                ctl.wait("scalar", s_st, st_done[p])
                for g in range(2):
                    for dc4 in range(4):
                        dc = g * 4 + dc4
                        ctl.op("tensor", lambda E, tb=tb, dc=dc, dc4=dc4: E.transpose(
                            p_tr[:, dc4 * 128:(dc4 + 1) * 128],
                            ecc(x3full, dc, PC)[:, tb * 128:(tb + 1) * 128],
                            idn[:, :]), s_pe if dc4 == 3 else None)
                    ctl.wait("scalar", s_pe)
                    for dc4 in range(4):
                        dc = g * 4 + dc4
                        ctl.op("scalar", lambda E, p=p, dc=dc, dc4=dc4: E.activation(
                            out=otm2[p][:, dc * 128:(dc + 1) * 128],
                            in_=p_tr[:, dc4 * 128:(dc4 + 1) * 128],
                            func=AF.Identity), s_ac if dc4 == 3 else None)
                    if g == 0:
                        ctl.wait("tensor", s_ac)
                ctl.wait("gpsimd", s_ac)
                ctl.dma("gpsimd",
                        o_out[tt * PC + tb * 128: tt * PC + (tb + 1) * 128, :],
                        otm2[p][:, :], s_st)
                st_done[p] = s_st.n
                if post_until == f"s5f_tb{tb}" and tt == 0:
                    ctl.wait("sync", s_st)
                    _replay(nc, ctl)
                    return nc
                if post_until == f"s5f_nw{tb}" and tt == 0:
                    _replay(nc, ctl)
                    return nc
                if post_until == "s5f_t1" and tt == 0 and tb == 0:
                    ctl.wait("sync", s_st)
                    _replay(nc, ctl)
                    return nc
                if post_until == f"s5f_tt{tt}" and tb == 3:
                    ctl.wait("sync", s_st)
                    _replay(nc, ctl)
                    return nc

        ctl.wait("sync", s_st)
        _replay(nc, ctl)
        return nc


def _replay(nc, ctl):
    with nc.Block() as block:
        def mk(eng):
            def run(E, q=ctl.q[eng]):
                for fn in q:
                    fn(E)
            return run
        block.sync(mk("sync"))
        block.tensor(mk("tensor"))
        block.vector(mk("vector"))
        block.scalar(mk("scalar"))
        block.gpsimd(mk("gpsimd"))


def _vec_block(v):
    return np.ascontiguousarray(v.reshape(8, 128).T.astype(np.float32))


def _prepare_inputs(inputs):
    f32 = np.float32
    x = np.asarray(inputs["L0_attn_x"], f32)
    pv = np.zeros((128, 96), f32)
    names = ["L0_attn_ln1_g", "L0_attn_ln1_b", "L0_attn_ln2_g", "L0_attn_ln2_b",
             "L0_attn_rms_w", "L0_attn_conv_b", "L0_attn_dt_proj_b",
             "L0_attn_D_param", "L0_attn_ffn_b2"]
    for i, nm in enumerate(names):
        pv[:, 8 * i:8 * (i + 1)] = _vec_block(np.asarray(inputs[nm], f32))
    alog = np.asarray(inputs["L0_attn_A_log"], f32).reshape(8, 128, NST) \
        .transpose(1, 0, 2).reshape(128, 8 * NST)
    cwm = np.asarray(inputs["L0_attn_conv_w"], f32).reshape(8, 128, KCV) \
        .transpose(1, 0, 2).reshape(128, 8 * KCV)
    fb1 = np.asarray(inputs["L0_attn_ffn_b1"], f32).reshape(32, 128).T
    eb1 = np.asarray(inputs["L0_ffn_b1"], f32).reshape(NEXP, 16, 128) \
        .transpose(2, 0, 1).reshape(128, NEXP * 16)
    wsrc = dict(
        w_in=inputs["L0_attn_in_proj_w"], w_xp=inputs["L0_attn_x_proj_w"],
        w_dt=inputs["L0_attn_dt_proj_w"], w_op=inputs["L0_attn_out_proj_w"],
        w_f1=inputs["L0_attn_ffn_w1"], w_f2=inputs["L0_attn_ffn_w2"],
        w_g=inputs["L0_ffn_gate"], w_e1=inputs["L0_ffn_W1"],
        w_e2=inputs["L0_ffn_W2"],
    )
    wpk = np.concatenate([np.asarray(wsrc[nm], f32).ravel()
                          for nm, _ in WPK_LAYOUT])
    ppk = np.zeros((128, PPK_COLS), f32)
    ppk[:, 0:96] = pv
    ppk[:, 96:224] = alog
    ppk[:, 224:256] = cwm
    ppk[:, 256:288] = fb1
    ppk[:, 288:416] = eb1
    ppk[:, 416:544] = np.eye(128, dtype=f32)
    ppk[0:NEXP, 544:544 + D] = np.asarray(inputs["L0_ffn_b2"], f32)
    common = dict(wpk=np.ascontiguousarray(wpk), ppk=np.ascontiguousarray(ppk))
    in_maps = []
    for c in range(8):
        b = c // 2
        rho = c % 2
        m = dict(common)
        m["x"] = np.ascontiguousarray(x[b])
        xo = x[b].reshape(L // CH, CH, D)[rho::2].reshape(OWN, D)
        m["x_own"] = np.ascontiguousarray(xo)
        in_maps.append(m)
    return in_maps


def kernel(**inputs):
    in_maps = _prepare_inputs(inputs)
    nc = build()
    res = run_bass_kernel_spmd(nc, in_maps, core_ids=list(range(8)))
    out = np.empty((B, L, D), np.float32)
    for c in range(8):
        rho = c % 2
        b = c // 2
        oo = np.asarray(res.results[c]["o_out"])
        for jc in range(8):
            g = 2 * jc + rho
            out[b, g * CH:(g + 1) * CH, :] = oo[jc * CH:(jc + 1) * CH, :]
    return out


if __name__ == "__main__":
    build()
